# version 02
# baseline (speedup 1.0000x reference)
import contextlib
import numpy as np
import concourse.bass as bass
import concourse.mybir as mybir
from concourse.bass_utils import run_bass_kernel_spmd
from concourse.bass_core_rust import IndirectOffsetOnAxis

F32 = mybir.dt.float32
I32 = mybir.dt.int32
U32 = mybir.dt.uint32
AF = mybir.ActivationFunctionType
ALU = mybir.AluOpType
AX = mybir.AxisListType

DECAY, LR = 0.99, 0.01
D = 1024
M = 128
NE = 16
H = 4096
C = 128
W = 8
EPC = 2


class Phase:

    def __init__(self, sem, prev=None):
        self.sem, self.prev = sem, prev
        self.cnt = 0
        self.ops = {k: [] for k in ("sync", "tensor", "vector", "scalar", "gpsimd")}

    def do(self, engine, fn, dma=False):
        inc = 16 if dma else 1
        self.ops[engine].append((self.cnt, inc, fn))
        self.cnt += inc

    def emit_block(self, nc, block, tail_done=None):
        for eng_name in ("sync", "tensor", "vector", "scalar", "gpsimd"):
            ops = self.ops[eng_name]
            if not ops and not (tail_done and eng_name == "gpsimd"):
                continue

            def body(eng, ops=ops, eng_name=eng_name):
                for wait, inc, fn in ops:
                    if wait == 0 and self.prev is not None:
                        eng.wait_ge(self.prev.sem, self.prev.cnt)
                    elif wait > 0:
                        eng.wait_ge(self.sem, wait)
                    fn(eng).then_inc(self.sem, inc)
                if tail_done and eng_name == "gpsimd":
                    eng.wait_ge(self.sem, self.cnt)

            getattr(block, eng_name)(body)


def _merge(phs):
    return phs


def _host_consts(S, core):
    cap = S * W // NE
    capT = cap // 128
    SP1 = S + 1
    e0 = EPC * core
    u = np.arange(C)[:, None]
    t = np.arange(C)[None, :]
    maskT = np.where(u < t, LR * (DECAY ** (t - u - 1.0)), 0.0).astype(np.float32)
    qdec = np.broadcast_to(DECAY ** np.arange(C, dtype=np.float64),
                           (128, C)).astype(np.float32)
    kw_col = (LR * DECAY ** (C - 1.0 - np.arange(C)))[:, None].astype(np.float32)
    ident = np.eye(128, dtype=np.float32)
    ltriuT = np.triu(np.ones((128, 128), np.float32), 1)
    iota16 = np.broadcast_to(np.arange(NE, dtype=np.float32), (128, NE)).copy()
    s2d = (np.arange(capT)[None, :] * 128 + np.arange(128)[:, None]).astype(np.float32)

    bp = np.arange(W)[:, None, None, None]
    ep = np.arange(NE)[None, :, None, None]
    bb = np.arange(W + 1)[None, None, :, None]
    jj = np.arange(EPC)[None, None, None, :]
    g18 = ((ep == e0 + jj) & (bp < bb)).astype(np.float32).reshape(128, (W + 1) * EPC)
    dm = (((bp == bb) & (ep < e0 + jj)).astype(np.float32)
          - ((bp < bb) & (ep == e0 + jj)).astype(np.float32))
    dm16 = dm.reshape(128, (W + 1) * EPC)[:, :W * EPC]
    rowb16 = (S * np.arange(W)[:, None] + 0 * np.arange(EPC)[None, :]
              ).astype(np.float32).reshape(1, W * EPC)
    eo = np.arange(NE)[None, None, :]
    mgown = ((np.arange(NE)[None, :, None] == eo)
             & (np.arange(W)[:, None, None] < core)).astype(np.float32).reshape(128, NE)
    return dict(maskT=maskT, qdec=qdec, kw_col=kw_col, ident=ident, ltriuT=ltriuT,
                iota16=iota16, s2d=s2d, g18mat=g18, dm16mat=dm16, rowb16=rowb16,
                mgownmat=mgown)


def _build(S):
    cap = S * W // NE
    capT = cap // 128
    SP1 = S + 1
    NT = S // 128
    NSC = S // 512
    NCH = S // C
    NST = (EPC * cap) // 512
    STPE = cap // 512

    nc = bass.Bass("TRN2", target_bir_lowering=False)

    ap_in = {}
    for name, shape in [
        ("x", [S, D]), ("WkT", [D, M]), ("WvT", [D, M]), ("WqT", [D, M]),
        ("bk", [M]), ("bv", [M]), ("bq", [M]), ("WoT", [M, D]), ("bo", [D]),
        ("Wg", [D, NE]), ("bg", [NE]),
        ("W1p", [EPC, D, H]), ("b1pT", [EPC, 128, H // 128]),
        ("W2p", [EPC, H, D]), ("b2p", [EPC, D]),
        ("maskT", [C, C]), ("qdec", [128, C]), ("kw_col", [C, 1]),
        ("ident", [128, 128]), ("ltriuT", [128, 128]), ("iota16", [128, NE]),
        ("s2d", [128, capT]), ("g18mat", [128, (W + 1) * EPC]),
        ("dm16mat", [128, W * EPC]), ("rowb16", [1, W * EPC]),
        ("mgownmat", [128, NE]),
    ]:
        ap_in[name] = nc.dram_tensor(name, shape, F32, kind="ExternalInput").ap()

    out_d = nc.dram_tensor("out", [S, D], F32, kind="ExternalOutput").ap()
    xn_d = nc.dram_tensor("xn_d", [S, D], F32).ap()
    attn_d = nc.dram_tensor("attn_d", [S, D], F32).ap()
    ag1_in = nc.dram_tensor("ag1_in", [S, D], F32)
    ag1_out = nc.dram_tensor("ag1_out", [W * S, D], F32)
    ag1c_in = nc.dram_tensor("ag1c_in", [1, NE], F32)
    ag1c_out = nc.dram_tensor("ag1c_out", [W, NE], F32)
    ag2_in = nc.dram_tensor("ag2_in", [EPC * cap, D], F32)
    ag2_out = nc.dram_tensor("ag2_out", [W * EPC * cap, D], F32)

    ctx = contextlib.ExitStack()
    with ctx:
        sb = lambda n, s, dt=F32: ctx.enter_context(nc.sbuf_tensor(n, s, dt))
        sem = lambda n: ctx.enter_context(nc.semaphore(n))

        WkT_sb = sb("WkT_sb", [128, 8 * M])
        WvT_sb = sb("WvT_sb", [128, 8 * M])
        WqT_sb = sb("WqT_sb", [128, 8 * M])
        WoT_sb = sb("WoT_sb", [128, D])
        Wg_sb = sb("Wg_sb", [128, 8 * NE])
        bk_r = sb("bk_r", [1, M]); bv_r = sb("bv_r", [1, M]); bq_r = sb("bq_r", [1, M])
        bo_r = sb("bo_r", [1, D]); bg_r = sb("bg_r", [1, NE])
        b1T_sb = sb("b1T_sb", [128, EPC * (H // 128)])
        b2_r = sb("b2_r", [1, EPC * D])
        maskT_sb = sb("maskT_sb", [C, C])
        qdec_sb = sb("qdec_sb", [128, C])
        kw_sb = sb("kw_sb", [C, 1])
        id_sb = sb("id_sb", [128, 128])
        ltr_sb = sb("ltr_sb", [128, 128])
        iota_sb = sb("iota_sb", [128, NE])
        s2d_sb = sb("s2d_sb", [128, capT])
        g18_sb = sb("g18_sb", [128, (W + 1) * EPC])
        dm16_sb = sb("dm16_sb", [128, W * EPC])
        rowb_sb = sb("rowb_sb", [1, W * EPC])
        mgo_sb = sb("mgo_sb", [128, NE])
        ones_r = sb("ones_r", [1, 512])
        ones_c = sb("ones_c", [128, 1])
        idx_st = sb("idx_st", [128, NT])
        gate_st = sb("gate_st", [128, NT])
        rank_st = sb("rank_st", [128, NT])
        runb = sb("runb", [1, NE])
        loff_own = sb("loff_own", [1, NE])
        zer16 = sb("zer16", [1, NE])
        counts_c = sb("counts_c", [128, 1])
        g18row = sb("g18row", [1, (W + 1) * EPC])
        d16row = sb("d16row", [1, W * EPC])
        gown_bc = sb("gown_bc", [128, NE])
        g18bc = sb("g18bc", [128, (W + 1) * EPC])
        d16bc = sb("d16bc", [128, W * EPC])
        src_i = sb("src_i", [128, EPC * capT], I32)

        s_p1 = sem("s_p1"); s_p2 = sem("s_p2")
        s_p3 = sem("s_p3"); s_p5 = sem("s_p5")

        ph1 = Phase(s_p1)
        ph2 = Phase(s_p2, ph1)
        ph3 = Phase(s_p3, ph2)
        ph5 = Phase(s_p5, ph3)

        with contextlib.ExitStack() as c1:
            sb1 = lambda n, s, dt=F32: c1.enter_context(nc.sbuf_tensor(n, s, dt))
            K_sb = sb1("K_sb", [128, S])
            Q_sb = sb1("Q_sb", [128, S])
            Kt_sb = sb1("Kt_sb", [128, S])
            Vt_sb = sb1("Vt_sb", [128, S])

            with contextlib.ExitStack() as c1a:
                sb1a = lambda n, s, dt=F32: c1a.enter_context(nc.sbuf_tensor(n, s, dt))
                ps1 = lambda n, s: c1a.enter_context(nc.psum_tensor(n, s, F32))
                xt = sb1a("xt", [128, D])
                sq = sb1a("sq", [128, D])
                xn_tm = sb1a("xn_tm", [128, D])
                xn_fm = sb1a("xn_fm", [128, 8 * 512])
                lcol = sb1a("lcol", [128, 16])
                maxes = sb1a("maxes", [128, 8])
                idx8 = sb1a("idx8", [128, 8], U32)
                logit_sb = sb1a("logit_sb", [128, NE])
                probs = sb1a("probs", [128, NE])
                oneh = sb1a("oneh", [128, NE])
                posf = sb1a("posf", [128, NE])
                ptr = ps1("ptr", [128, 128])
                pl = ps1("pl", [128, NE])
                pp = ps1("pp", [128, NE])
                pc = ps1("pc", [1, NE])
                pq = ps1("pq", [128, 512])
                pk = ps1("pk", [128, 512])
                pkt = ps1("pkt", [128, 128])
                pvt = ps1("pvt", [128, 128])

                P = ph1
                for wT, dst in [(ap_in["WkT"], WkT_sb), (ap_in["WvT"], WvT_sb),
                                (ap_in["WqT"], WqT_sb)]:
                    P.do("sync", lambda e, wT=wT, dst=dst: e.dma_start(
                        out=dst[:].rearrange("p (c m) -> p c m", c=8),
                        in_=wT.rearrange("(c p) m -> p c m", p=128)), dma=True)
                P.do("sync", lambda e: e.dma_start(out=WoT_sb[:], in_=ap_in["WoT"]), dma=True)
                P.do("sync", lambda e: e.dma_start(
                    out=Wg_sb[:].rearrange("p (c m) -> p c m", c=8),
                    in_=ap_in["Wg"].rearrange("(c p) m -> p c m", p=128)), dma=True)
                for src, dst in [("bk", bk_r), ("bv", bv_r), ("bq", bq_r),
                                 ("bo", bo_r), ("bg", bg_r)]:
                    P.do("sync", lambda e, src=src, dst=dst: e.dma_start(
                        out=dst[:], in_=ap_in[src][None, :]), dma=True)
                for a in range(EPC):
                    P.do("sync", lambda e, a=a: e.dma_start(
                        out=b1T_sb[:, (H // 128) * a:(H // 128) * (a + 1)],
                        in_=ap_in["b1pT"][a]), dma=True)
                    P.do("sync", lambda e, a=a: e.dma_start(
                        out=b2_r[:, D * a:D * (a + 1)],
                        in_=ap_in["b2p"][a][None, :]), dma=True)
                for src, dst in [("maskT", maskT_sb), ("qdec", qdec_sb), ("kw_col", kw_sb),
                                 ("ident", id_sb), ("ltriuT", ltr_sb), ("iota16", iota_sb),
                                 ("s2d", s2d_sb), ("g18mat", g18_sb), ("dm16mat", dm16_sb),
                                 ("rowb16", rowb_sb), ("mgownmat", mgo_sb)]:
                    P.do("sync", lambda e, src=src, dst=dst: e.dma_start(
                        out=dst[:], in_=ap_in[src][:, :]), dma=True)
                P.do("vector", lambda e: e.memset(ones_r[:], 1.0))
                P.do("vector", lambda e: e.memset(ones_c[:], 1.0))
                P.do("vector", lambda e: e.memset(zer16[:], 0.0))
                P.do("vector", lambda e: e.memset(runb[:], 0.0))

                for sc in range(NSC):
                    for it in range(4):
                        ti = 4 * sc + it
                        P.do("sync", lambda e, ti=ti: e.dma_start(
                            out=xt[:], in_=ap_in["x"][128 * ti:128 * (ti + 1), :]), dma=True)
                        P.do("vector", lambda e: e.reduce_sum(lcol[:, 0:1], xt[:], axis=AX.X))
                        P.do("scalar", lambda e: e.activation(sq[:], xt[:], AF.Square,
                                                              accum_out=lcol[:, 2:3]))
                        P.do("vector", lambda e: e.tensor_scalar_mul(lcol[:, 1:2], lcol[:, 0:1], -1.0 / D))
                        P.do("vector", lambda e: e.tensor_scalar_mul(lcol[:, 3:4], lcol[:, 2:3], 1.0 / D))
                        P.do("scalar", lambda e: e.activation(lcol[:, 4:5], lcol[:, 1:2], AF.Square))
                        P.do("vector", lambda e: e.tensor_sub(lcol[:, 5:6], lcol[:, 3:4], lcol[:, 4:5]))
                        P.do("vector", lambda e: e.tensor_scalar_add(lcol[:, 5:6], lcol[:, 5:6], 1e-5))
                        P.do("vector", lambda e: e.reciprocal(lcol[:, 6:7], lcol[:, 5:6]))
                        P.do("scalar", lambda e: e.activation(lcol[:, 7:8], lcol[:, 6:7], AF.Sqrt))
                        P.do("vector", lambda e: e.tensor_scalar(xn_tm[:], xt[:], lcol[:, 1:2],
                                                                 lcol[:, 7:8], ALU.add, ALU.mult))
                        P.do("sync", lambda e, ti=ti: e.dma_start(
                            out=xn_d[128 * ti:128 * (ti + 1), :], in_=xn_tm[:]), dma=True)
                        for dc in range(8):
                            P.do("tensor", lambda e, dc=dc: e.transpose(
                                ptr[:], xn_tm[:, 128 * dc:128 * (dc + 1)], id_sb[:]))
                            P.do("vector", lambda e, dc=dc, it=it: e.tensor_copy(
                                xn_fm[:, 512 * dc + 128 * it:512 * dc + 128 * (it + 1)], ptr[:]))
                        for dc in range(8):
                            P.do("tensor", lambda e, dc=dc, it=it: e.matmul(
                                pl[:], xn_fm[:, 512 * dc + 128 * it:512 * dc + 128 * (it + 1)],
                                Wg_sb[:, NE * dc:NE * (dc + 1)], start=(dc == 0), stop=False))
                        P.do("tensor", lambda e: e.matmul(pl[:], ones_r[:, 0:128], bg_r[:],
                                                          start=False, stop=True))
                        P.do("vector", lambda e: e.tensor_copy(logit_sb[:], pl[:]))
                        P.do("vector", lambda e: e.max(maxes[:], logit_sb[:]))
                        P.do("vector", lambda e: e.tensor_scalar_mul(lcol[:, 8:9], maxes[:, 0:1], -1.0))
                        P.do("scalar", lambda e: e.activation(probs[:], logit_sb[:], AF.Exp,
                                                              bias=lcol[:, 8:9], accum_out=lcol[:, 9:10]))
                        P.do("vector", lambda e, ti=ti: e.reciprocal(gate_st[:, ti:ti + 1], lcol[:, 9:10]))
                        P.do("vector", lambda e: e.max_index(idx8[:], maxes[:], logit_sb[:]))
                        P.do("vector", lambda e, ti=ti: e.tensor_copy(idx_st[:, ti:ti + 1], idx8[:, 0:1]))
                        P.do("vector", lambda e, ti=ti: e.tensor_scalar(
                            oneh[:], iota_sb[:], idx_st[:, ti:ti + 1], None, ALU.is_equal))
                        P.do("tensor", lambda e: e.matmul(pp[:], ltr_sb[:], oneh[:],
                                                          start=True, stop=False))
                        P.do("tensor", lambda e: e.matmul(pp[:], ones_r[:, 0:128], runb[:],
                                                          start=False, stop=True))
                        P.do("vector", lambda e: e.tensor_mul(posf[:], oneh[:], pp[:]))
                        P.do("vector", lambda e, ti=ti: e.reduce_sum(rank_st[:, ti:ti + 1],
                                                                     posf[:], axis=AX.X))
                        P.do("tensor", lambda e: e.matmul(pc[:], ones_c[:], oneh[:],
                                                          start=True, stop=True))
                        P.do("vector", lambda e: e.tensor_add(runb[:], runb[:], pc[:]))
                    t0 = 512 * sc
                    for dst_ps, wT, b_r in [(pq, WqT_sb, bq_r), (pk, WkT_sb, bk_r)]:
                        for dc in range(8):
                            P.do("tensor", lambda e, dc=dc, dst_ps=dst_ps, wT=wT: e.matmul(
                                dst_ps[:], wT[:, M * dc:M * (dc + 1)],
                                xn_fm[:, 512 * dc:512 * (dc + 1)], start=(dc == 0), stop=False))
                        P.do("tensor", lambda e, dst_ps=dst_ps, b_r=b_r: e.matmul(
                            dst_ps[:], b_r[:], ones_r[:, 0:512], start=False, stop=True))
                    P.do("vector", lambda e, t0=t0: e.tensor_copy(Q_sb[:, t0:t0 + 512], pq[:]))
                    P.do("vector", lambda e, t0=t0: e.tensor_copy(K_sb[:, t0:t0 + 512], pk[:]))
                    for it in range(4):
                        tt = 512 * sc + 128 * it
                        for dst_ps, wT, b_r in [(pkt, WkT_sb, bk_r), (pvt, WvT_sb, bv_r)]:
                            for dc in range(8):
                                P.do("tensor", lambda e, dc=dc, it=it, dst_ps=dst_ps, wT=wT: e.matmul(
                                    dst_ps[:], xn_fm[:, 512 * dc + 128 * it:512 * dc + 128 * (it + 1)],
                                    wT[:, M * dc:M * (dc + 1)], start=(dc == 0), stop=False))
                            P.do("tensor", lambda e, dst_ps=dst_ps, b_r=b_r: e.matmul(
                                dst_ps[:], ones_r[:, 0:128], b_r[:], start=False, stop=True))
                        P.do("vector", lambda e, tt=tt: e.tensor_scalar(
                            Kt_sb[:, tt:tt + 128], pkt[:], kw_sb[:], None, ALU.mult))
                        P.do("vector", lambda e, tt=tt: e.tensor_copy(Vt_sb[:, tt:tt + 128], pvt[:]))

                with nc.Block("seg1") as block:
                    ph1.emit_block(nc, block)

            with contextlib.ExitStack() as c2:
                sb2 = lambda n, s, dt=F32: c2.enter_context(nc.sbuf_tensor(n, s, dt))
                ps2 = lambda n, s: c2.enter_context(nc.psum_tensor(n, s, F32))
                T_sb = sb2("T_sb", [128, 128])
                Qd = sb2("Qd", [128, C])
                Am = sb2("Am", [C, C])
                oneh2 = sb2("oneh2", [128, NE])
                posf2 = sb2("posf2", [128, NE])
                dest_f = sb2("dest_f", [128, 1])
                dest_i = sb2("dest_i", [128, 1], I32)
                lobc = sb2("lobc", [128, NE])
                xrow = sb2("xrow", [128, D])
                arow = sb2("arow", [128, D])
                pA = ps2("pA", [C, C])
                pO = ps2("pO", [128, C])
                pT = ps2("pT", [128, 128])
                pao = ps2("pao", [128, 512])
                plo = ps2("plo", [128, NE])

                P = ph2
                P.do("vector", lambda e: e.tensor_tensor_scan(
                    loff_own[:], runb[:], zer16[:], 0.0, ALU.add, ALU.add))
                P.do("vector", lambda e: e.tensor_sub(loff_own[:], loff_own[:], runb[:]))
                P.do("tensor", lambda e: e.matmul(plo[:], ones_r[:, 0:128], loff_own[:],
                                                  start=True, stop=True))
                P.do("vector", lambda e: e.tensor_copy(lobc[:], plo[:]))
                P.do("sync", lambda e: e.dma_start(
                    out=ag1c_in.ap()[:, :], in_=runb[:]), dma=True)
                for ti in range(NT):
                    P.do("sync", lambda e, ti=ti: e.dma_start(
                        out=xrow[:], in_=xn_d[128 * ti:128 * (ti + 1), :]), dma=True)
                    P.do("vector", lambda e, ti=ti: e.tensor_scalar(
                        oneh2[:], iota_sb[:], idx_st[:, ti:ti + 1], None, ALU.is_equal))
                    P.do("vector", lambda e: e.tensor_mul(posf2[:], oneh2[:], lobc[:]))
                    P.do("vector", lambda e: e.reduce_sum(dest_f[:], posf2[:], axis=AX.X))
                    P.do("vector", lambda e, ti=ti: e.tensor_add(
                        dest_f[:], dest_f[:], rank_st[:, ti:ti + 1]))
                    P.do("vector", lambda e: e.tensor_copy(dest_i[:], dest_f[:]))
                    P.do("gpsimd", lambda e: e.indirect_dma_start(
                        ag1_in.ap()[:, :], IndirectOffsetOnAxis(dest_i[:], 0),
                        xrow[:], None), dma=True)
                P.do("vector", lambda e: e.memset(T_sb[:], 0.0))
                for ch in range(NCH):
                    t0 = C * ch
                    P.do("vector", lambda e, t0=t0: e.tensor_mul(
                        Qd[:], Q_sb[:, t0:t0 + C], qdec_sb[:]))
                    P.do("tensor", lambda e, t0=t0: e.matmul(
                        pA[:], K_sb[:, t0:t0 + C], Q_sb[:, t0:t0 + C], start=True, stop=True))
                    P.do("vector", lambda e: e.tensor_mul(Am[:], pA[:], maskT_sb[:]))
                    P.do("tensor", lambda e: e.matmul(pO[:], T_sb[:], Qd[:],
                                                      start=True, stop=False))
                    P.do("tensor", lambda e, t0=t0: e.matmul(
                        pO[:], Vt_sb[:, t0:t0 + C], Am[:], start=False, stop=True))
                    P.do("tensor", lambda e, t0=t0: e.matmul(
                        pT[:], Kt_sb[:, t0:t0 + C], Vt_sb[:, t0:t0 + C], start=True, stop=True))
                    P.do("vector", lambda e: e.scalar_tensor_tensor(
                        T_sb[:], T_sb[:], DECAY ** C, pT[:], ALU.mult, ALU.add))
                    P.do("vector", lambda e, t0=t0: e.tensor_copy(K_sb[:, t0:t0 + C], pO[:]))
                for ti in range(NT):
                    t0 = 128 * ti
                    for dh in range(2):
                        P.do("tensor", lambda e, t0=t0, dh=dh: e.matmul(
                            pao[:], K_sb[:, t0:t0 + 128], WoT_sb[:, 512 * dh:512 * (dh + 1)],
                            start=True, stop=False))
                        P.do("tensor", lambda e, dh=dh: e.matmul(
                            pao[:], ones_r[:, 0:128], bo_r[:, 512 * dh:512 * (dh + 1)],
                            start=False, stop=True))
                        P.do("vector", lambda e, dh=dh: e.tensor_copy(
                            arow[:, 512 * dh:512 * (dh + 1)], pao[:]))
                    P.do("sync", lambda e, t0=t0: e.dma_start(
                        out=attn_d[t0:t0 + 128, :], in_=arow[:]), dma=True)
                P.do("gpsimd", lambda e: e.collective_compute(
                    "AllGather", ALU.bypass, replica_groups=[list(range(W))],
                    ins=[ag1c_in.ap().opt()], outs=[ag1c_out.ap().opt()]))
                P.do("gpsimd", lambda e: e.collective_compute(
                    "AllGather", ALU.bypass, replica_groups=[list(range(W))],
                    ins=[ag1_in.ap().opt()], outs=[ag1_out.ap().opt()]))

                with nc.Block("seg2") as block:
                    ph2.emit_block(nc, block)

        with contextlib.ExitStack() as c3:
            sb3 = lambda n, s, dt=F32: c3.enter_context(nc.sbuf_tensor(n, s, dt))
            ps3 = lambda n, s: c3.enter_context(nc.psum_tensor(n, s, F32))
            xg_tm = sb3("xg_tm", [128, D])
            xg_fm = sb3("xg_fm", [128, 8 * 512])
            h_sb = sb3("h_sb", [128, 32 * 512])
            w1_sb = sb3("w1_sb", [128, 8 * 128])
            w2_sb = sb3("w2_sb", [128, 512])
            y_sb = sb3("y_sb", [128, 512])
            macc = sb3("macc", [128, capT])
            m1 = sb3("m1", [128, capT])
            m2 = sb3("m2", [128, capT])
            mt = sb3("mt", [128, capT])
            pD = ps3("pD", [1, (W + 1) * EPC])
            pBC = ps3("pBC", [128, (W + 1) * EPC])
            ptr3 = ps3("ptr3", [128, 128])
            ph_ps = ps3("ph_ps", [128, 512])
            py = [ps3(f"py{i}", [128, 512]) for i in range(4)]

            P = ph3
            P.do("sync", lambda e: e.dma_start(
                out=counts_c[:, 0:1],
                in_=ag1c_out.ap().rearrange("(x b) e -> x (b e)", x=1).transpose([1, 0])),
                dma=True)
            P.do("tensor", lambda e: e.matmul(pD[:], counts_c[:], g18_sb[:],
                                              start=True, stop=True))
            P.do("vector", lambda e: e.tensor_copy(g18row[:], pD[:]))
            P.do("tensor", lambda e: e.matmul(pBC[:], ones_r[:, 0:128], g18row[:],
                                              start=True, stop=True))
            P.do("vector", lambda e: e.tensor_copy(g18bc[:], pBC[:]))
            P.do("tensor", lambda e: e.matmul(pD[:, 0:W * EPC], counts_c[:], dm16_sb[:],
                                              start=True, stop=False))
            P.do("tensor", lambda e: e.matmul(pD[:, 0:W * EPC], ones_c[0:1, 0:1], rowb_sb[:],
                                              start=False, stop=True))
            P.do("vector", lambda e: e.tensor_copy(d16row[:], pD[:, 0:W * EPC]))
            P.do("tensor", lambda e: e.matmul(pBC[:, 0:W * EPC], ones_r[:, 0:128], d16row[:],
                                              start=True, stop=True))
            P.do("vector", lambda e: e.tensor_copy(d16bc[:], pBC[:, 0:W * EPC]))
            P.do("tensor", lambda e: e.matmul(pD[:, 0:NE], counts_c[:], mgo_sb[:],
                                              start=True, stop=True))
            P.do("vector", lambda e: e.tensor_copy(d16row[:, 0:NE], pD[:, 0:NE]))
            P.do("tensor", lambda e: e.matmul(pBC[:, 0:NE], ones_r[:, 0:128],
                                              d16row[:, 0:NE], start=True, stop=True))
            P.do("vector", lambda e: e.tensor_copy(gown_bc[:], pBC[:, 0:NE]))
            for j in range(EPC):
                P.do("vector", lambda e: e.memset(macc[:], 0.0))
                for b in range(W):
                    P.do("vector", lambda e, b=b, j=j: e.tensor_scalar(
                        m1[:], s2d_sb[:], g18bc[:, EPC * b + j:EPC * b + j + 1],
                        None, ALU.is_ge))
                    P.do("vector", lambda e, b=b, j=j: e.tensor_scalar(
                        m2[:], s2d_sb[:], g18bc[:, EPC * (b + 1) + j:EPC * (b + 1) + j + 1],
                        None, ALU.is_lt))
                    P.do("vector", lambda e: e.tensor_mul(m1[:], m1[:], m2[:]))
                    P.do("vector", lambda e, b=b, j=j: e.tensor_scalar(
                        mt[:], s2d_sb[:], d16bc[:, EPC * b + j:EPC * b + j + 1],
                        None, ALU.add))
                    P.do("vector", lambda e: e.tensor_mul(mt[:], mt[:], m1[:]))
                    P.do("vector", lambda e: e.tensor_add(macc[:], macc[:], mt[:]))
                P.do("vector", lambda e, j=j: e.tensor_copy(
                    src_i[:, capT * j:capT * (j + 1)], macc[:]))
            for st in range(NST):
                j = st // STPE
                stj = st % STPE
                for k in range(4):
                    col = capT * j + 4 * stj + k
                    P.do("gpsimd", lambda e, col=col: e.indirect_dma_start(
                        xg_tm[:], None, ag1_out.ap()[:, :],
                        IndirectOffsetOnAxis(src_i[:, col:col + 1], 0)), dma=True)
                    for dc in range(8):
                        P.do("tensor", lambda e, dc=dc: e.transpose(
                            ptr3[:], xg_tm[:, 128 * dc:128 * (dc + 1)], id_sb[:]))
                        P.do("vector", lambda e, dc=dc, k=k: e.tensor_copy(
                            xg_fm[:, 512 * dc + 128 * k:512 * dc + 128 * (k + 1)], ptr3[:]))
                for hc in range(32):
                    P.do("sync", lambda e, j=j, hc=hc: e.dma_start(
                        out=w1_sb[:].rearrange("p (c h) -> p c h", c=8),
                        in_=ap_in["W1p"][j, :, 128 * hc:128 * (hc + 1)]
                        .rearrange("(c p) h -> p c h", p=128)), dma=True)
                    for dc in range(8):
                        P.do("tensor", lambda e, dc=dc: e.matmul(
                            ph_ps[:], w1_sb[:, 128 * dc:128 * (dc + 1)],
                            xg_fm[:, 512 * dc:512 * (dc + 1)],
                            start=(dc == 0), stop=(dc == 7)))
                    P.do("scalar", lambda e, j=j, hc=hc: e.activation(
                        h_sb[:, 512 * hc:512 * (hc + 1)], ph_ps[:], AF.Relu,
                        bias=b1T_sb[:, 32 * j + hc:32 * j + hc + 1]))
                for dh in range(2):
                    for hc in range(32):
                        P.do("sync", lambda e, j=j, hc=hc, dh=dh: e.dma_start(
                            out=w2_sb[:],
                            in_=ap_in["W2p"][j, 128 * hc:128 * (hc + 1),
                                             512 * dh:512 * (dh + 1)]), dma=True)
                        for tt in range(4):
                            P.do("tensor", lambda e, hc=hc, tt=tt: e.matmul(
                                py[tt][:], h_sb[:, 512 * hc + 128 * tt:512 * hc + 128 * (tt + 1)],
                                w2_sb[:], start=(hc == 0), stop=False))
                    for tt in range(4):
                        P.do("tensor", lambda e, tt=tt, j=j, dh=dh: e.matmul(
                            py[tt][:], ones_r[:, 0:128],
                            b2_r[:, D * j + 512 * dh:D * j + 512 * (dh + 1)],
                            start=False, stop=True))
                        P.do("vector", lambda e, tt=tt: e.tensor_copy(y_sb[:], py[tt][:]))
                        row0 = cap * j + 512 * stj + 128 * tt
                        P.do("sync", lambda e, row0=row0, dh=dh: e.dma_start(
                            out=ag2_in.ap()[row0:row0 + 128, 512 * dh:512 * (dh + 1)],
                            in_=y_sb[:]), dma=True)
            P.do("gpsimd", lambda e: e.collective_compute(
                "AllGather", ALU.bypass, replica_groups=[list(range(W))],
                ins=[ag2_in.ap().opt()], outs=[ag2_out.ap().opt()]))

            with nc.Block("seg3") as block:
                ph3.emit_block(nc, block)

        with contextlib.ExitStack() as c5:
            sb5 = lambda n, s, dt=F32: c5.enter_context(nc.sbuf_tensor(n, s, dt))
            xt5 = sb5("xt5", [128, D])
            at5 = sb5("at5", [128, D])
            yg5 = sb5("yg5", [128, D])
            oneh5 = sb5("oneh5", [128, NE])
            posf5 = sb5("posf5", [128, NE])
            col5 = sb5("col5", [128, 8])
            yri = sb5("yri", [128, 1], I32)
            out5 = sb5("out5", [128, D])

            P = ph5
            for ti in range(NT):
                t0 = 128 * ti
                P.do("sync", lambda e, t0=t0: e.dma_start(
                    out=xt5[:], in_=ap_in["x"][t0:t0 + 128, :]), dma=True)
                P.do("sync", lambda e, t0=t0: e.dma_start(
                    out=at5[:], in_=attn_d[t0:t0 + 128, :]), dma=True)
                P.do("vector", lambda e, ti=ti: e.tensor_scalar(
                    oneh5[:], iota_sb[:], idx_st[:, ti:ti + 1], None, ALU.is_equal))
                P.do("vector", lambda e: e.tensor_mul(posf5[:], oneh5[:], gown_bc[:]))
                P.do("vector", lambda e: e.reduce_sum(col5[:, 0:1], posf5[:], axis=AX.X))
                P.do("vector", lambda e, ti=ti: e.tensor_add(
                    col5[:, 0:1], col5[:, 0:1], rank_st[:, ti:ti + 1]))
                P.do("vector", lambda e: e.tensor_single_scalar(
                    col5[:, 1:2], col5[:, 0:1], float(cap), ALU.is_lt))
                P.do("vector", lambda e, ti=ti: e.tensor_mul(
                    col5[:, 2:3], col5[:, 1:2], gate_st[:, ti:ti + 1]))
                P.do("vector", lambda e, ti=ti: e.tensor_scalar(
                    col5[:, 3:4], idx_st[:, ti:ti + 1], float(cap), None, ALU.mult))
                P.do("vector", lambda e: e.tensor_add(col5[:, 3:4], col5[:, 3:4], col5[:, 0:1]))
                P.do("vector", lambda e: e.tensor_single_scalar(
                    col5[:, 3:4], col5[:, 3:4], float(NE * cap - 1), ALU.min))
                P.do("vector", lambda e: e.tensor_copy(yri[:], col5[:, 3:4]))
                P.do("gpsimd", lambda e: e.indirect_dma_start(
                    yg5[:], None, ag2_out.ap()[:, :],
                    IndirectOffsetOnAxis(yri[:], 0)), dma=True)
                P.do("vector", lambda e: e.scalar_tensor_tensor(
                    out5[:], yg5[:], col5[:, 2:3], at5[:], ALU.mult, ALU.add))
                P.do("vector", lambda e: e.tensor_add(out5[:], out5[:], xt5[:]))
                P.do("sync", lambda e, t0=t0: e.dma_start(
                    out=out_d[t0:t0 + 128, :], in_=out5[:]), dma=True)

            with nc.Block("seg4") as block:
                ph5.emit_block(nc, block, tail_done=True)

    return nc


def _in_maps(S, inputs):
    x = np.asarray(inputs["L0_attn_x"], dtype=np.float32)
    Wk = np.asarray(inputs["L0_attn_Wk"], dtype=np.float32)
    Wv = np.asarray(inputs["L0_attn_Wv"], dtype=np.float32)
    Wq = np.asarray(inputs["L0_attn_Wq"], dtype=np.float32)
    Wo = np.asarray(inputs["L0_attn_Wo"], dtype=np.float32)
    W1 = np.asarray(inputs["L0_ffn_W1"], dtype=np.float32)
    W2 = np.asarray(inputs["L0_ffn_W2"], dtype=np.float32)
    b1 = np.asarray(inputs["L0_ffn_b1"], dtype=np.float32)
    base = {
        "WkT": np.ascontiguousarray(Wk.T), "WvT": np.ascontiguousarray(Wv.T),
        "WqT": np.ascontiguousarray(Wq.T), "WoT": np.ascontiguousarray(Wo.T),
        "bk": np.asarray(inputs["L0_attn_bk"], np.float32),
        "bv": np.asarray(inputs["L0_attn_bv"], np.float32),
        "bq": np.asarray(inputs["L0_attn_bq"], np.float32),
        "bo": np.asarray(inputs["L0_attn_bo"], np.float32),
        "Wg": np.asarray(inputs["L0_ffn_Wg"], np.float32),
        "bg": np.asarray(inputs["L0_ffn_bg"], np.float32),
    }
    maps = []
    for c in range(W):
        m = dict(base)
        m["x"] = np.ascontiguousarray(x[c])
        e0 = EPC * c
        m["W1p"] = np.ascontiguousarray(W1[e0:e0 + EPC])
        m["W2p"] = np.ascontiguousarray(W2[e0:e0 + EPC])
        m["b1pT"] = np.ascontiguousarray(
            b1[e0:e0 + EPC].reshape(EPC, H // 128, 128).transpose(0, 2, 1))
        m["b2p"] = np.ascontiguousarray(
            np.asarray(inputs["L0_ffn_b2"], np.float32)[e0:e0 + EPC])
        m.update(_host_consts(S, c))
        maps.append(m)
    return maps


def _run(S, inputs):
    nc = _build(S)
    maps = _in_maps(S, inputs)
    res = run_bass_kernel_spmd(nc, maps, core_ids=list(range(W)))
    return np.stack([res.results[c]["out"] for c in range(W)], axis=0)


def kernel(**inputs):
    S = np.asarray(inputs["L0_attn_x"]).shape[1]
    return _run(S, inputs)


# version 03
# speedup vs baseline: 1.5863x; 1.5863x over previous
import contextlib
import numpy as np
import concourse.bass as bass
import concourse.mybir as mybir
from concourse.bass_utils import run_bass_kernel_spmd
from concourse.bass_core_rust import IndirectOffsetOnAxis

F32 = mybir.dt.float32
I32 = mybir.dt.int32
U32 = mybir.dt.uint32
AF = mybir.ActivationFunctionType
ALU = mybir.AluOpType
AX = mybir.AxisListType

DECAY, LR = 0.99, 0.01
D = 1024
M = 128
NE = 16
H = 4096
C = 128
W = 8
EPC = 2


class Phase:

    def __init__(self, sem, gd_sem, prev=None):
        self.sem, self.gd_sem, self.prev = sem, gd_sem, prev
        self.cnt = 0
        self.gd_cnt = 0
        self.ops = {k: [] for k in ("sync", "tensor", "vector", "scalar", "gpsimd")}
        self.first = True

    def do(self, engine, fn, dma=False):
        gd = dma and engine == "gpsimd"
        inc = 16 if dma else 1
        self.ops[engine].append((self.cnt, self.gd_cnt, inc, gd, fn))
        if gd:
            self.gd_cnt += inc
        else:
            self.cnt += inc

    def emit_block(self, nc, block, tail_done=None):
        for eng_name in ("sync", "tensor", "vector", "scalar", "gpsimd"):
            ops = self.ops[eng_name]
            if not ops and not (tail_done and eng_name == "gpsimd"):
                continue

            def body(eng, ops=ops, eng_name=eng_name):
                for wait, gdwait, inc, gd, fn in ops:
                    if wait == 0 and gdwait == 0 and self.prev is not None:
                        eng.wait_ge(self.prev.sem, self.prev.cnt)
                        if getattr(self.prev, "gd_cnt", 0) > 0:
                            eng.wait_ge(self.prev.gd_sem, self.prev.gd_cnt)
                    else:
                        if wait > 0:
                            eng.wait_ge(self.sem, wait)
                        if gdwait > 0:
                            eng.wait_ge(self.gd_sem, gdwait)
                    fn(eng).then_inc(self.gd_sem if gd else self.sem, inc)
                if tail_done and eng_name == "gpsimd":
                    eng.wait_ge(self.sem, self.cnt)
                    if self.gd_cnt > 0:
                        eng.wait_ge(self.gd_sem, self.gd_cnt)

            getattr(block, eng_name)(body)


def _merge(phs):
    return phs


def _host_consts(S, core):
    cap = S * W // NE
    capT = cap // 128
    SP1 = S + 1
    e0 = EPC * core
    u = np.arange(C)[:, None]
    t = np.arange(C)[None, :]
    maskT = np.where(u < t, LR * (DECAY ** (t - u - 1.0)), 0.0).astype(np.float32)
    qdec = np.broadcast_to(DECAY ** np.arange(C, dtype=np.float64),
                           (128, C)).astype(np.float32)
    kw_col = (LR * DECAY ** (C - 1.0 - np.arange(C)))[:, None].astype(np.float32)
    ident = np.eye(128, dtype=np.float32)
    ltriuT = np.triu(np.ones((128, 128), np.float32), 1)
    iota16 = np.broadcast_to(np.arange(NE, dtype=np.float32), (128, NE)).copy()
    s2d = (np.arange(capT)[None, :] * 128 + np.arange(128)[:, None]).astype(np.float32)

    bp = np.arange(W)[:, None, None, None]
    ep = np.arange(NE)[None, :, None, None]
    bb = np.arange(W + 1)[None, None, :, None]
    jj = np.arange(EPC)[None, None, None, :]
    g18 = ((ep == e0 + jj) & (bp < bb)).astype(np.float32).reshape(128, (W + 1) * EPC)
    dm = (((bp == bb) & (ep < e0 + jj)).astype(np.float32)
          - ((bp < bb) & (ep == e0 + jj)).astype(np.float32))
    dm16 = dm.reshape(128, (W + 1) * EPC)[:, :W * EPC]
    rowb16 = (S * np.arange(W)[:, None] + 0 * np.arange(EPC)[None, :]
              ).astype(np.float32).reshape(1, W * EPC)
    eo = np.arange(NE)[None, None, :]
    mgown = ((np.arange(NE)[None, :, None] == eo)
             & (np.arange(W)[:, None, None] < core)).astype(np.float32).reshape(128, NE)
    return dict(maskT=maskT, qdec=qdec, kw_col=kw_col, ident=ident, ltriuT=ltriuT,
                iota16=iota16, s2d=s2d, g18mat=g18, dm16mat=dm16, rowb16=rowb16,
                mgownmat=mgown)


def _build(S):
    cap = S * W // NE
    capT = cap // 128
    SP1 = S + 1
    NT = S // 128
    NSC = S // 512
    NCH = S // C
    NST = (EPC * cap) // 512
    STPE = cap // 512

    nc = bass.Bass("TRN2", target_bir_lowering=False)

    ap_in = {}
    for name, shape in [
        ("x", [S, D]), ("WkT", [D, M]), ("WvT", [D, M]), ("WqT", [D, M]),
        ("bk", [M]), ("bv", [M]), ("bq", [M]), ("WoT", [M, D]), ("bo", [D]),
        ("Wg", [D, NE]), ("bg", [NE]),
        ("W1p", [EPC, D, H]), ("b1pT", [EPC, 128, H // 128]),
        ("W2p", [EPC, H, D]), ("b2p", [EPC, D]),
        ("maskT", [C, C]), ("qdec", [128, C]), ("kw_col", [C, 1]),
        ("ident", [128, 128]), ("ltriuT", [128, 128]), ("iota16", [128, NE]),
        ("s2d", [128, capT]), ("g18mat", [128, (W + 1) * EPC]),
        ("dm16mat", [128, W * EPC]), ("rowb16", [1, W * EPC]),
        ("mgownmat", [128, NE]),
    ]:
        ap_in[name] = nc.dram_tensor(name, shape, F32, kind="ExternalInput").ap()

    out_d = nc.dram_tensor("out", [S, D], F32, kind="ExternalOutput").ap()
    xn_d = nc.dram_tensor("xn_d", [S, D], F32).ap()
    attn_d = nc.dram_tensor("attn_d", [S, D], F32).ap()
    ag1_in = nc.dram_tensor("ag1_in", [S, D], F32)
    ag1_out = nc.dram_tensor("ag1_out", [W * S, D], F32)
    ag1c_in = nc.dram_tensor("ag1c_in", [1, NE], F32)
    ag1c_out = nc.dram_tensor("ag1c_out", [W, NE], F32)
    ag2_in = nc.dram_tensor("ag2_in", [EPC * cap, D], F32)
    ag2_out = nc.dram_tensor("ag2_out", [W * EPC * cap, D], F32)

    ctx = contextlib.ExitStack()
    with ctx:
        sb = lambda n, s, dt=F32: ctx.enter_context(nc.sbuf_tensor(n, s, dt))
        sem = lambda n: ctx.enter_context(nc.semaphore(n))

        WkT_sb = sb("WkT_sb", [128, 8 * M])
        WvT_sb = sb("WvT_sb", [128, 8 * M])
        WqT_sb = sb("WqT_sb", [128, 8 * M])
        WoT_sb = sb("WoT_sb", [128, D])
        Wg_sb = sb("Wg_sb", [128, 8 * NE])
        bk_r = sb("bk_r", [1, M]); bv_r = sb("bv_r", [1, M]); bq_r = sb("bq_r", [1, M])
        bo_r = sb("bo_r", [1, D]); bg_r = sb("bg_r", [1, NE])
        b1T_sb = sb("b1T_sb", [128, EPC * (H // 128)])
        b2_r = sb("b2_r", [1, EPC * D])
        maskT_sb = sb("maskT_sb", [C, C])
        qdec_sb = sb("qdec_sb", [128, C])
        kw_sb = sb("kw_sb", [C, 1])
        id_sb = sb("id_sb", [128, 128])
        ltr_sb = sb("ltr_sb", [128, 128])
        iota_sb = sb("iota_sb", [128, NE])
        s2d_sb = sb("s2d_sb", [128, capT])
        g18_sb = sb("g18_sb", [128, (W + 1) * EPC])
        dm16_sb = sb("dm16_sb", [128, W * EPC])
        rowb_sb = sb("rowb_sb", [1, W * EPC])
        mgo_sb = sb("mgo_sb", [128, NE])
        ones_r = sb("ones_r", [1, 512])
        ones_c = sb("ones_c", [128, 1])
        idx_st = sb("idx_st", [128, NT])
        gate_st = sb("gate_st", [128, NT])
        rank_st = sb("rank_st", [128, NT])
        runb = sb("runb", [1, NE])
        loff_own = sb("loff_own", [1, NE])
        zer16 = sb("zer16", [1, NE])
        counts_c = sb("counts_c", [128, 1])
        g18row = sb("g18row", [1, (W + 1) * EPC])
        d16row = sb("d16row", [1, W * EPC])
        gown_bc = sb("gown_bc", [128, NE])
        g18bc = sb("g18bc", [128, (W + 1) * EPC])
        d16bc = sb("d16bc", [128, W * EPC])
        src_i = sb("src_i", [128, EPC * capT], I32)

        s_p1 = sem("s_p1"); s_p2 = sem("s_p2")
        s_p3 = sem("s_p3"); s_p5 = sem("s_p5")
        s_p1g = sem("s_p1g"); s_p2g = sem("s_p2g")
        s_p3g = sem("s_p3g"); s_p5g = sem("s_p5g")
        gsem = [sem("gs0"), sem("gs1")]
        w1s = [sem("w1s0"), sem("w1s1")]
        w2s = [sem("w2s0"), sem("w2s1")]
        osem = [sem("os0"), sem("os1")]
        mm3 = sem("mm3"); rl3 = sem("rl3"); ev3 = sem("ev3")
        agd = sem("agd")

        ph1 = Phase(s_p1, s_p1g)
        ph2 = Phase(s_p2, s_p2g, ph1)
        ph3 = Phase(s_p3, s_p3g, ph2)
        import types
        ph4d = types.SimpleNamespace(sem=agd, cnt=1, gd_cnt=0, gd_sem=None)
        ph5 = Phase(s_p5, s_p5g, ph4d)

        with contextlib.ExitStack() as c1:
            sb1 = lambda n, s, dt=F32: c1.enter_context(nc.sbuf_tensor(n, s, dt))
            K_sb = sb1("K_sb", [128, S])
            Q_sb = sb1("Q_sb", [128, S])
            Kt_sb = sb1("Kt_sb", [128, S])
            Vt_sb = sb1("Vt_sb", [128, S])

            with contextlib.ExitStack() as c1a:
                sb1a = lambda n, s, dt=F32: c1a.enter_context(nc.sbuf_tensor(n, s, dt))
                ps1 = lambda n, s: c1a.enter_context(nc.psum_tensor(n, s, F32))
                xt = sb1a("xt", [128, D])
                sq = sb1a("sq", [128, D])
                xn_tm = sb1a("xn_tm", [128, D])
                xn_fm = sb1a("xn_fm", [128, 8 * 512])
                lcol = sb1a("lcol", [128, 16])
                maxes = sb1a("maxes", [128, 8])
                idx8 = sb1a("idx8", [128, 8], U32)
                logit_sb = sb1a("logit_sb", [128, NE])
                probs = sb1a("probs", [128, NE])
                oneh = sb1a("oneh", [128, NE])
                posf = sb1a("posf", [128, NE])
                ptr = ps1("ptr", [128, 128])
                pl = ps1("pl", [128, NE])
                pp = ps1("pp", [128, NE])
                pc = ps1("pc", [1, NE])
                pq = ps1("pq", [128, 512])
                pk = ps1("pk", [128, 512])
                pkt = ps1("pkt", [128, 128])
                pvt = ps1("pvt", [128, 128])

                P = ph1
                for wT, dst in [(ap_in["WkT"], WkT_sb), (ap_in["WvT"], WvT_sb),
                                (ap_in["WqT"], WqT_sb)]:
                    P.do("sync", lambda e, wT=wT, dst=dst: e.dma_start(
                        out=dst[:].rearrange("p (c m) -> p c m", c=8),
                        in_=wT.rearrange("(c p) m -> p c m", p=128)), dma=True)
                P.do("sync", lambda e: e.dma_start(out=WoT_sb[:], in_=ap_in["WoT"]), dma=True)
                P.do("sync", lambda e: e.dma_start(
                    out=Wg_sb[:].rearrange("p (c m) -> p c m", c=8),
                    in_=ap_in["Wg"].rearrange("(c p) m -> p c m", p=128)), dma=True)
                for src, dst in [("bk", bk_r), ("bv", bv_r), ("bq", bq_r),
                                 ("bo", bo_r), ("bg", bg_r)]:
                    P.do("sync", lambda e, src=src, dst=dst: e.dma_start(
                        out=dst[:], in_=ap_in[src][None, :]), dma=True)
                for a in range(EPC):
                    P.do("sync", lambda e, a=a: e.dma_start(
                        out=b1T_sb[:, (H // 128) * a:(H // 128) * (a + 1)],
                        in_=ap_in["b1pT"][a]), dma=True)
                    P.do("sync", lambda e, a=a: e.dma_start(
                        out=b2_r[:, D * a:D * (a + 1)],
                        in_=ap_in["b2p"][a][None, :]), dma=True)
                for src, dst in [("maskT", maskT_sb), ("qdec", qdec_sb), ("kw_col", kw_sb),
                                 ("ident", id_sb), ("ltriuT", ltr_sb), ("iota16", iota_sb),
                                 ("s2d", s2d_sb), ("g18mat", g18_sb), ("dm16mat", dm16_sb),
                                 ("rowb16", rowb_sb), ("mgownmat", mgo_sb)]:
                    P.do("sync", lambda e, src=src, dst=dst: e.dma_start(
                        out=dst[:], in_=ap_in[src][:, :]), dma=True)
                P.do("vector", lambda e: e.memset(ones_r[:], 1.0))
                P.do("vector", lambda e: e.memset(ones_c[:], 1.0))
                P.do("vector", lambda e: e.memset(zer16[:], 0.0))
                P.do("vector", lambda e: e.memset(runb[:], 0.0))

                for sc in range(NSC):
                    for it in range(4):
                        ti = 4 * sc + it
                        P.do("sync", lambda e, ti=ti: e.dma_start(
                            out=xt[:], in_=ap_in["x"][128 * ti:128 * (ti + 1), :]), dma=True)
                        P.do("vector", lambda e: e.reduce_sum(lcol[:, 0:1], xt[:], axis=AX.X))
                        P.do("scalar", lambda e: e.activation(sq[:], xt[:], AF.Square,
                                                              accum_out=lcol[:, 2:3]))
                        P.do("vector", lambda e: e.tensor_scalar_mul(lcol[:, 1:2], lcol[:, 0:1], -1.0 / D))
                        P.do("vector", lambda e: e.tensor_scalar_mul(lcol[:, 3:4], lcol[:, 2:3], 1.0 / D))
                        P.do("scalar", lambda e: e.activation(lcol[:, 4:5], lcol[:, 1:2], AF.Square))
                        P.do("vector", lambda e: e.tensor_sub(lcol[:, 5:6], lcol[:, 3:4], lcol[:, 4:5]))
                        P.do("vector", lambda e: e.tensor_scalar_add(lcol[:, 5:6], lcol[:, 5:6], 1e-5))
                        P.do("vector", lambda e: e.reciprocal(lcol[:, 6:7], lcol[:, 5:6]))
                        P.do("scalar", lambda e: e.activation(lcol[:, 7:8], lcol[:, 6:7], AF.Sqrt))
                        P.do("vector", lambda e: e.tensor_scalar(xn_tm[:], xt[:], lcol[:, 1:2],
                                                                 lcol[:, 7:8], ALU.add, ALU.mult))
                        P.do("sync", lambda e, ti=ti: e.dma_start(
                            out=xn_d[128 * ti:128 * (ti + 1), :], in_=xn_tm[:]), dma=True)
                        for dc in range(8):
                            P.do("tensor", lambda e, dc=dc: e.transpose(
                                ptr[:], xn_tm[:, 128 * dc:128 * (dc + 1)], id_sb[:]))
                            P.do("vector", lambda e, dc=dc, it=it: e.tensor_copy(
                                xn_fm[:, 512 * dc + 128 * it:512 * dc + 128 * (it + 1)], ptr[:]))
                        for dc in range(8):
                            P.do("tensor", lambda e, dc=dc, it=it: e.matmul(
                                pl[:], xn_fm[:, 512 * dc + 128 * it:512 * dc + 128 * (it + 1)],
                                Wg_sb[:, NE * dc:NE * (dc + 1)], start=(dc == 0), stop=False))
                        P.do("tensor", lambda e: e.matmul(pl[:], ones_r[:, 0:128], bg_r[:],
                                                          start=False, stop=True))
                        P.do("vector", lambda e: e.tensor_copy(logit_sb[:], pl[:]))
                        P.do("vector", lambda e: e.max(maxes[:], logit_sb[:]))
                        P.do("vector", lambda e: e.tensor_scalar_mul(lcol[:, 8:9], maxes[:, 0:1], -1.0))
                        P.do("scalar", lambda e: e.activation(probs[:], logit_sb[:], AF.Exp,
                                                              bias=lcol[:, 8:9], accum_out=lcol[:, 9:10]))
                        P.do("vector", lambda e, ti=ti: e.reciprocal(gate_st[:, ti:ti + 1], lcol[:, 9:10]))
                        P.do("vector", lambda e: e.max_index(idx8[:], maxes[:], logit_sb[:]))
                        P.do("vector", lambda e, ti=ti: e.tensor_copy(idx_st[:, ti:ti + 1], idx8[:, 0:1]))
                        P.do("vector", lambda e, ti=ti: e.tensor_scalar(
                            oneh[:], iota_sb[:], idx_st[:, ti:ti + 1], None, ALU.is_equal))
                        P.do("tensor", lambda e: e.matmul(pp[:], ltr_sb[:], oneh[:],
                                                          start=True, stop=False))
                        P.do("tensor", lambda e: e.matmul(pp[:], ones_r[:, 0:128], runb[:],
                                                          start=False, stop=True))
                        P.do("vector", lambda e: e.tensor_mul(posf[:], oneh[:], pp[:]))
                        P.do("vector", lambda e, ti=ti: e.reduce_sum(rank_st[:, ti:ti + 1],
                                                                     posf[:], axis=AX.X))
                        P.do("tensor", lambda e: e.matmul(pc[:], ones_c[:], oneh[:],
                                                          start=True, stop=True))
                        P.do("vector", lambda e: e.tensor_add(runb[:], runb[:], pc[:]))
                    t0 = 512 * sc
                    for dst_ps, wT, b_r in [(pq, WqT_sb, bq_r), (pk, WkT_sb, bk_r)]:
                        for dc in range(8):
                            P.do("tensor", lambda e, dc=dc, dst_ps=dst_ps, wT=wT: e.matmul(
                                dst_ps[:], wT[:, M * dc:M * (dc + 1)],
                                xn_fm[:, 512 * dc:512 * (dc + 1)], start=(dc == 0), stop=False))
                        P.do("tensor", lambda e, dst_ps=dst_ps, b_r=b_r: e.matmul(
                            dst_ps[:], b_r[:], ones_r[:, 0:512], start=False, stop=True))
                    P.do("vector", lambda e, t0=t0: e.tensor_copy(Q_sb[:, t0:t0 + 512], pq[:]))
                    P.do("vector", lambda e, t0=t0: e.tensor_copy(K_sb[:, t0:t0 + 512], pk[:]))
                    for it in range(4):
                        tt = 512 * sc + 128 * it
                        for dst_ps, wT, b_r in [(pkt, WkT_sb, bk_r), (pvt, WvT_sb, bv_r)]:
                            for dc in range(8):
                                P.do("tensor", lambda e, dc=dc, it=it, dst_ps=dst_ps, wT=wT: e.matmul(
                                    dst_ps[:], xn_fm[:, 512 * dc + 128 * it:512 * dc + 128 * (it + 1)],
                                    wT[:, M * dc:M * (dc + 1)], start=(dc == 0), stop=False))
                            P.do("tensor", lambda e, dst_ps=dst_ps, b_r=b_r: e.matmul(
                                dst_ps[:], ones_r[:, 0:128], b_r[:], start=False, stop=True))
                        P.do("vector", lambda e, tt=tt: e.tensor_scalar(
                            Kt_sb[:, tt:tt + 128], pkt[:], kw_sb[:], None, ALU.mult))
                        P.do("vector", lambda e, tt=tt: e.tensor_copy(Vt_sb[:, tt:tt + 128], pvt[:]))

                with nc.Block("seg1") as block:
                    ph1.emit_block(nc, block)

            with contextlib.ExitStack() as c2:
                sb2 = lambda n, s, dt=F32: c2.enter_context(nc.sbuf_tensor(n, s, dt))
                ps2 = lambda n, s: c2.enter_context(nc.psum_tensor(n, s, F32))
                T_sb = sb2("T_sb", [128, 128])
                Qd = sb2("Qd", [128, C])
                Am = sb2("Am", [C, C])
                oneh2 = sb2("oneh2", [128, NE])
                posf2 = sb2("posf2", [128, NE])
                dest_f = sb2("dest_f", [128, 1])
                dest_i = sb2("dest_i", [128, 1], I32)
                lobc = sb2("lobc", [128, NE])
                xrow = sb2("xrow", [128, D])
                arow = sb2("arow", [128, D])
                pA = ps2("pA", [C, C])
                pO = ps2("pO", [128, C])
                pT = ps2("pT", [128, 128])
                pao = ps2("pao", [128, 512])
                plo = ps2("plo", [128, NE])

                P = ph2
                P.do("vector", lambda e: e.tensor_tensor_scan(
                    loff_own[:], runb[:], zer16[:], 0.0, ALU.add, ALU.add))
                P.do("vector", lambda e: e.tensor_sub(loff_own[:], loff_own[:], runb[:]))
                P.do("tensor", lambda e: e.matmul(plo[:], ones_r[:, 0:128], loff_own[:],
                                                  start=True, stop=True))
                P.do("vector", lambda e: e.tensor_copy(lobc[:], plo[:]))
                P.do("sync", lambda e: e.dma_start(
                    out=ag1c_in.ap()[:, :], in_=runb[:]), dma=True)
                for ti in range(NT):
                    P.do("sync", lambda e, ti=ti: e.dma_start(
                        out=xrow[:], in_=xn_d[128 * ti:128 * (ti + 1), :]), dma=True)
                    P.do("vector", lambda e, ti=ti: e.tensor_scalar(
                        oneh2[:], iota_sb[:], idx_st[:, ti:ti + 1], None, ALU.is_equal))
                    P.do("vector", lambda e: e.tensor_mul(posf2[:], oneh2[:], lobc[:]))
                    P.do("vector", lambda e: e.reduce_sum(dest_f[:], posf2[:], axis=AX.X))
                    P.do("vector", lambda e, ti=ti: e.tensor_add(
                        dest_f[:], dest_f[:], rank_st[:, ti:ti + 1]))
                    P.do("vector", lambda e: e.tensor_copy(dest_i[:], dest_f[:]))
                    P.do("gpsimd", lambda e: e.indirect_dma_start(
                        ag1_in.ap()[:, :], IndirectOffsetOnAxis(dest_i[:], 0),
                        xrow[:], None), dma=True)
                P.do("vector", lambda e: e.memset(T_sb[:], 0.0))
                for ch in range(NCH):
                    t0 = C * ch
                    P.do("vector", lambda e, t0=t0: e.tensor_mul(
                        Qd[:], Q_sb[:, t0:t0 + C], qdec_sb[:]))
                    P.do("tensor", lambda e, t0=t0: e.matmul(
                        pA[:], K_sb[:, t0:t0 + C], Q_sb[:, t0:t0 + C], start=True, stop=True))
                    P.do("vector", lambda e: e.tensor_mul(Am[:], pA[:], maskT_sb[:]))
                    P.do("tensor", lambda e: e.matmul(pO[:], T_sb[:], Qd[:],
                                                      start=True, stop=False))
                    P.do("tensor", lambda e, t0=t0: e.matmul(
                        pO[:], Vt_sb[:, t0:t0 + C], Am[:], start=False, stop=True))
                    P.do("tensor", lambda e, t0=t0: e.matmul(
                        pT[:], Kt_sb[:, t0:t0 + C], Vt_sb[:, t0:t0 + C], start=True, stop=True))
                    P.do("vector", lambda e: e.scalar_tensor_tensor(
                        T_sb[:], T_sb[:], DECAY ** C, pT[:], ALU.mult, ALU.add))
                    P.do("vector", lambda e, t0=t0: e.tensor_copy(K_sb[:, t0:t0 + C], pO[:]))
                for ti in range(NT):
                    t0 = 128 * ti
                    for dh in range(2):
                        P.do("tensor", lambda e, t0=t0, dh=dh: e.matmul(
                            pao[:], K_sb[:, t0:t0 + 128], WoT_sb[:, 512 * dh:512 * (dh + 1)],
                            start=True, stop=False))
                        P.do("tensor", lambda e, dh=dh: e.matmul(
                            pao[:], ones_r[:, 0:128], bo_r[:, 512 * dh:512 * (dh + 1)],
                            start=False, stop=True))
                        P.do("vector", lambda e, dh=dh: e.tensor_copy(
                            arow[:, 512 * dh:512 * (dh + 1)], pao[:]))
                    P.do("sync", lambda e, t0=t0: e.dma_start(
                        out=attn_d[t0:t0 + 128, :], in_=arow[:]), dma=True)
                P.do("gpsimd", lambda e: e.collective_compute(
                    "AllGather", ALU.bypass, replica_groups=[list(range(W))],
                    ins=[ag1c_in.ap().opt()], outs=[ag1c_out.ap().opt()]))
                P.do("gpsimd", lambda e: e.collective_compute(
                    "AllGather", ALU.bypass, replica_groups=[list(range(W))],
                    ins=[ag1_in.ap().opt()], outs=[ag1_out.ap().opt()]))

                with nc.Block("seg2") as block:
                    ph2.emit_block(nc, block)

        with contextlib.ExitStack() as c3:
            sb3 = lambda n, s, dt=F32: c3.enter_context(nc.sbuf_tensor(n, s, dt))
            ps3 = lambda n, s: c3.enter_context(nc.psum_tensor(n, s, F32))
            xg_tm = [sb3("xg_tmA", [128, D]), sb3("xg_tmB", [128, D])]
            xg_fm = sb3("xg_fm", [128, 8 * 512])
            h_sb = sb3("h_sb", [128, 32 * 512])
            w1_sb = [sb3("w1_sbA", [128, 8 * 128]), sb3("w1_sbB", [128, 8 * 128])]
            w2_sb = [sb3("w2_sbA", [128, 512]), sb3("w2_sbB", [128, 512])]
            y_sb = [sb3("y_sbA", [128, 512]), sb3("y_sbB", [128, 512])]
            macc = sb3("macc", [128, capT])
            m1 = sb3("m1", [128, capT])
            m2 = sb3("m2", [128, capT])
            mt = sb3("mt", [128, capT])
            ptr3 = ps3("ptr3", [128, 128])
            ptr3b = ps3("ptr3b", [128, 128])
            ph_ps = [ps3("ph_psA", [128, 512]), ps3("ph_psB", [128, 512])]
            py = [ps3(f"py{i}", [128, 512]) for i in range(4)]

            P = ph3
            P.do("sync", lambda e: e.dma_start(
                out=counts_c[:, 0:1],
                in_=ag1c_out.ap().rearrange("(x b) e -> x (b e)", x=1).transpose([1, 0])),
                dma=True)
            P.do("tensor", lambda e: e.matmul(py[0][0:1, 0:(W + 1) * EPC], counts_c[:], g18_sb[:],
                                              start=True, stop=True))
            P.do("vector", lambda e: e.tensor_copy(g18row[:], py[0][0:1, 0:(W + 1) * EPC]))
            P.do("tensor", lambda e: e.matmul(py[1][:, 0:(W + 1) * EPC], ones_r[:, 0:128], g18row[:],
                                              start=True, stop=True))
            P.do("vector", lambda e: e.tensor_copy(g18bc[:], py[1][:, 0:(W + 1) * EPC]))
            P.do("tensor", lambda e: e.matmul(py[0][0:1, 0:W * EPC], counts_c[:], dm16_sb[:],
                                              start=True, stop=False))
            P.do("tensor", lambda e: e.matmul(py[0][0:1, 0:W * EPC], ones_c[0:1, 0:1], rowb_sb[:],
                                              start=False, stop=True))
            P.do("vector", lambda e: e.tensor_copy(d16row[:], py[0][0:1, 0:W * EPC]))
            P.do("tensor", lambda e: e.matmul(py[1][:, 0:W * EPC], ones_r[:, 0:128], d16row[:],
                                              start=True, stop=True))
            P.do("vector", lambda e: e.tensor_copy(d16bc[:], py[1][:, 0:W * EPC]))
            P.do("tensor", lambda e: e.matmul(py[0][0:1, 0:NE], counts_c[:], mgo_sb[:],
                                              start=True, stop=True))
            P.do("vector", lambda e: e.tensor_copy(d16row[:, 0:NE], py[0][0:1, 0:NE]))
            P.do("tensor", lambda e: e.matmul(py[1][:, 0:NE], ones_r[:, 0:128],
                                              d16row[:, 0:NE], start=True, stop=True))
            P.do("vector", lambda e: e.tensor_copy(gown_bc[:], py[1][:, 0:NE]))
            for j in range(EPC):
                P.do("vector", lambda e: e.memset(macc[:], 0.0))
                for b in range(W):
                    P.do("vector", lambda e, b=b, j=j: e.tensor_scalar(
                        m1[:], s2d_sb[:], g18bc[:, EPC * b + j:EPC * b + j + 1],
                        None, ALU.is_ge))
                    P.do("vector", lambda e, b=b, j=j: e.tensor_scalar(
                        m2[:], s2d_sb[:], g18bc[:, EPC * (b + 1) + j:EPC * (b + 1) + j + 1],
                        None, ALU.is_lt))
                    P.do("vector", lambda e: e.tensor_mul(m1[:], m1[:], m2[:]))
                    P.do("vector", lambda e, b=b, j=j: e.tensor_scalar(
                        mt[:], s2d_sb[:], d16bc[:, EPC * b + j:EPC * b + j + 1],
                        None, ALU.add))
                    P.do("vector", lambda e: e.tensor_mul(mt[:], mt[:], m1[:]))
                    P.do("vector", lambda e: e.tensor_add(macc[:], macc[:], mt[:]))
                P.do("vector", lambda e, j=j: e.tensor_copy(
                    src_i[:, capT * j:capT * (j + 1)], macc[:]))
            NG = NST * 4
            NH1 = NST * 32
            NW2 = NST * 64
            NY = NST * 8
            tr_ev = {}
            w1grp_ev = {}

            w2grp_ev = {}
            bias_ev = {}
            evac_ev = {}
            relu_ev = {}
            xg_done = {}
            yev = {}
            py_last = [0, 0, 0, 0]

            pe_ops = []
            sc_ops = []
            ve_ops = []
            sy_ops = []
            gp_ops = []
            mm_c = [0]; rl_c = [0]; ev_c = [0]

            for st in range(NST):
                j = st // STPE
                stj = st % STPE
                for k in range(4):
                    g = st * 4 + k
                    p = g % 2
                    col = capT * j + 4 * stj + k
                    waits = []
                    if g >= 2:
                        waits.append((gsem[p], 16 * (g // 2)))
                        waits.append((mm3, tr_ev[(g - 2) * 8 + 7]))
                    gp_ops.append((waits, (lambda e, col=col, p=p: e.indirect_dma_start(
                        xg_tm[p][:], None, ag1_out.ap()[:, :],
                        IndirectOffsetOnAxis(src_i[:, col:col + 1], 0))),
                        (gsem[p], 16)))
                    for dc in range(8):
                        n = g * 8 + dc
                        q = n % 2
                        PT = ptr3 if q == 0 else ptr3b
                        waits = []
                        if dc == 0:
                            waits.append((gsem[p], 16 * (g // 2 + 1)))
                        if n >= 2:
                            waits.append((rl3, evac_ev[n - 2]))
                        mm_c[0] += 1
                        tr_ev[n] = mm_c[0]
                        pe_ops.append((waits, (lambda e, dc=dc, p=p, PT=PT: e.transpose(
                            PT[:], xg_tm[p][:, 128 * dc:128 * (dc + 1)], id_sb[:])),
                            (mm3, 1)))
                        rl_c[0] += 1
                        evac_ev[n] = rl_c[0]
                        sc_ops.append(([(mm3, tr_ev[n])],
                                       (lambda e, dc=dc, k=k, PT=PT: e.copy(
                                           xg_fm[:, 512 * dc + 128 * k:512 * dc + 128 * (k + 1)],
                                           PT[:])), (rl3, 1)))
                xg_done[st] = rl_c[0]
                for hc in range(32):
                    l = st * 32 + hc
                    p = l % 2
                    waits = [(w1s[p], 16 * (l // 2 + 1))]
                    if hc == 0:
                        waits.append((rl3, xg_done[st]))
                    if l >= 2:
                        waits.append((rl3, relu_ev[l - 2]))
                    for dc in range(8):
                        mmw = waits if dc == 0 else []
                        inc = None
                        if dc == 7:
                            mm_c[0] += 1
                            w1grp_ev[l] = mm_c[0]
                            inc = (mm3, 1)
                        pe_ops.append((mmw, (lambda e, dc=dc, p=p: e.matmul(
                            ph_ps[p][:], w1_sb[p][:, 128 * dc:128 * (dc + 1)],
                            xg_fm[:, 512 * dc:512 * (dc + 1)],
                            start=(dc == 0), stop=(dc == 7))), inc))
                    lw = [(w1s[p], 16 * (l // 2))] if l >= 2 else []
                    if l >= 2:
                        lw.append((mm3, w1grp_ev[l - 2]))
                    sy_ops.append((lw, (lambda e, j=j, hc=hc, p=p: e.dma_start(
                        out=w1_sb[p][:].rearrange("p (c h) -> p c h", c=8),
                        in_=ap_in["W1p"][j, :, 128 * hc:128 * (hc + 1)]
                        .rearrange("(c p) h -> p c h", p=128))), (w1s[p], 16)))
                    rl_c[0] += 1
                    relu_ev[l] = rl_c[0]
                    sc_ops.append(([(mm3, w1grp_ev[l])],
                                   (lambda e, j=j, hc=hc, p=p: e.activation(
                                       h_sb[:, 512 * hc:512 * (hc + 1)], ph_ps[p][:],
                                       AF.Relu,
                                       bias=b1T_sb[:, 32 * j + hc:32 * j + hc + 1])),
                                   (rl3, 1)))
                for dh in range(2):
                    for hc in range(32):
                        mw2 = (st * 2 + dh) * 32 + hc
                        q = mw2 % 2
                        waits = [(w2s[q], 16 * (mw2 // 2 + 1))]
                        if dh == 0:
                            waits.append((rl3, relu_ev[st * 32 + hc]))
                        if hc == 0:
                            pl_ = max(py_last)
                            if pl_ > 0:
                                waits.append((ev3, pl_))
                        for tt in range(4):
                            mmw = waits if tt == 0 else []
                            inc = None
                            if tt == 3:
                                mm_c[0] += 1
                                w2grp_ev[mw2] = mm_c[0]
                                inc = (mm3, 1)
                            pe_ops.append((mmw, (lambda e, hc=hc, tt=tt, q=q: e.matmul(
                                py[tt][:],
                                h_sb[:, 512 * hc + 128 * tt:512 * hc + 128 * (tt + 1)],
                                w2_sb[q][:], start=(hc == 0), stop=False)), inc))
                        lw = [(w2s[q], 16 * (mw2 // 2))] if mw2 >= 2 else []
                        if mw2 >= 2:
                            lw.append((mm3, w2grp_ev[mw2 - 2]))
                        sy_ops.append((lw, (lambda e, j=j, hc=hc, dh=dh, q=q: e.dma_start(
                            out=w2_sb[q][:],
                            in_=ap_in["W2p"][j, 128 * hc:128 * (hc + 1),
                                             512 * dh:512 * (dh + 1)])), (w2s[q], 16)))
                    for tt in range(4):
                        m = (st * 2 + dh) * 4 + tt
                        q = m % 2
                        mm_c[0] += 1
                        bias_ev[m] = mm_c[0]
                        pe_ops.append(([], (lambda e, tt=tt, j=j, dh=dh: e.matmul(
                            py[tt][:], ones_r[:, 0:128],
                            b2_r[:, D * j + 512 * dh:D * j + 512 * (dh + 1)],
                            start=False, stop=True)), (mm3, 1)))
                        ew = [(mm3, bias_ev[m])]
                        if m >= 2:
                            ew.append((osem[q], 16 * (m // 2)))
                        ev_c[0] += 1
                        yev[m] = ev_c[0]
                        py_last[tt] = ev_c[0]
                        ve_ops.append((ew, (lambda e, tt=tt, q=q: e.tensor_copy(
                            y_sb[q][:], py[tt][:])), (ev3, 1)))
                        row0 = cap * j + 512 * stj + 128 * tt
                        sw = [(ev3, yev[m])]
                        if m >= 2:
                            sw.append((osem[q], 16 * (m // 2)))
                        gp_ops.append((sw, (lambda e, row0=row0, dh=dh, q=q: e.dma_start(
                            out=ag2_in.ap()[row0:row0 + 128, 512 * dh:512 * (dh + 1)],
                            in_=y_sb[q][:])), (osem[q], 16)))
            no2 = [NY // 2, NY - NY // 2]
            gp_ops.append(([(osem[0], 16 * no2[0]), (osem[1], 16 * no2[1])],
                           (lambda e: e.collective_compute(
                               "AllGather", ALU.bypass,
                               replica_groups=[list(range(W))],
                               ins=[ag2_in.ap().opt()], outs=[ag2_out.ap().opt()])),
                           (agd, 1)))

            with nc.Block("seg3") as block:
                ph3.emit_block(nc, block)
                start_w = (s_p3, ph3.cnt)

                def chain_body(ops, start=start_w):
                    def body(eng):
                        eng.wait_ge(start[0], start[1])
                        for waits, fn, inc in ops:
                            for sm, v in waits:
                                eng.wait_ge(sm, v)
                            inst = fn(eng)
                            if inc is not None:
                                inst.then_inc(inc[0], inc[1])
                    return body

                block.tensor(chain_body(pe_ops))
                block.scalar(chain_body(sc_ops))
                block.vector(chain_body(ve_ops))
                block.sync(chain_body(sy_ops))
                block.gpsimd(chain_body(gp_ops))


        with contextlib.ExitStack() as c5:
            sb5 = lambda n, s, dt=F32: c5.enter_context(nc.sbuf_tensor(n, s, dt))
            xt5 = sb5("xt5", [128, D])
            at5 = sb5("at5", [128, D])
            yg5 = sb5("yg5", [128, D])
            oneh5 = sb5("oneh5", [128, NE])
            posf5 = sb5("posf5", [128, NE])
            col5 = sb5("col5", [128, 8])
            yri = sb5("yri", [128, 1], I32)
            out5 = sb5("out5", [128, D])

            P = ph5
            for ti in range(NT):
                t0 = 128 * ti
                P.do("sync", lambda e, t0=t0: e.dma_start(
                    out=xt5[:], in_=ap_in["x"][t0:t0 + 128, :]), dma=True)
                P.do("sync", lambda e, t0=t0: e.dma_start(
                    out=at5[:], in_=attn_d[t0:t0 + 128, :]), dma=True)
                P.do("vector", lambda e, ti=ti: e.tensor_scalar(
                    oneh5[:], iota_sb[:], idx_st[:, ti:ti + 1], None, ALU.is_equal))
                P.do("vector", lambda e: e.tensor_mul(posf5[:], oneh5[:], gown_bc[:]))
                P.do("vector", lambda e: e.reduce_sum(col5[:, 0:1], posf5[:], axis=AX.X))
                P.do("vector", lambda e, ti=ti: e.tensor_add(
                    col5[:, 0:1], col5[:, 0:1], rank_st[:, ti:ti + 1]))
                P.do("vector", lambda e: e.tensor_single_scalar(
                    col5[:, 1:2], col5[:, 0:1], float(cap), ALU.is_lt))
                P.do("vector", lambda e, ti=ti: e.tensor_mul(
                    col5[:, 2:3], col5[:, 1:2], gate_st[:, ti:ti + 1]))
                P.do("vector", lambda e, ti=ti: e.tensor_scalar(
                    col5[:, 3:4], idx_st[:, ti:ti + 1], float(cap), None, ALU.mult))
                P.do("vector", lambda e: e.tensor_add(col5[:, 3:4], col5[:, 3:4], col5[:, 0:1]))
                P.do("vector", lambda e: e.tensor_single_scalar(
                    col5[:, 3:4], col5[:, 3:4], float(NE * cap - 1), ALU.min))
                P.do("vector", lambda e: e.tensor_copy(yri[:], col5[:, 3:4]))
                P.do("gpsimd", lambda e: e.indirect_dma_start(
                    yg5[:], None, ag2_out.ap()[:, :],
                    IndirectOffsetOnAxis(yri[:], 0)), dma=True)
                P.do("vector", lambda e: e.scalar_tensor_tensor(
                    out5[:], yg5[:], col5[:, 2:3], at5[:], ALU.mult, ALU.add))
                P.do("vector", lambda e: e.tensor_add(out5[:], out5[:], xt5[:]))
                P.do("sync", lambda e, t0=t0: e.dma_start(
                    out=out_d[t0:t0 + 128, :], in_=out5[:]), dma=True)

            with nc.Block("seg4") as block:
                ph5.emit_block(nc, block, tail_done=True)

    return nc


def _in_maps(S, inputs):
    x = np.asarray(inputs["L0_attn_x"], dtype=np.float32)
    Wk = np.asarray(inputs["L0_attn_Wk"], dtype=np.float32)
    Wv = np.asarray(inputs["L0_attn_Wv"], dtype=np.float32)
    Wq = np.asarray(inputs["L0_attn_Wq"], dtype=np.float32)
    Wo = np.asarray(inputs["L0_attn_Wo"], dtype=np.float32)
    W1 = np.asarray(inputs["L0_ffn_W1"], dtype=np.float32)
    W2 = np.asarray(inputs["L0_ffn_W2"], dtype=np.float32)
    b1 = np.asarray(inputs["L0_ffn_b1"], dtype=np.float32)
    base = {
        "WkT": np.ascontiguousarray(Wk.T), "WvT": np.ascontiguousarray(Wv.T),
        "WqT": np.ascontiguousarray(Wq.T), "WoT": np.ascontiguousarray(Wo.T),
        "bk": np.asarray(inputs["L0_attn_bk"], np.float32),
        "bv": np.asarray(inputs["L0_attn_bv"], np.float32),
        "bq": np.asarray(inputs["L0_attn_bq"], np.float32),
        "bo": np.asarray(inputs["L0_attn_bo"], np.float32),
        "Wg": np.asarray(inputs["L0_ffn_Wg"], np.float32),
        "bg": np.asarray(inputs["L0_ffn_bg"], np.float32),
    }
    maps = []
    for c in range(W):
        m = dict(base)
        m["x"] = np.ascontiguousarray(x[c])
        e0 = EPC * c
        m["W1p"] = np.ascontiguousarray(W1[e0:e0 + EPC])
        m["W2p"] = np.ascontiguousarray(W2[e0:e0 + EPC])
        m["b1pT"] = np.ascontiguousarray(
            b1[e0:e0 + EPC].reshape(EPC, H // 128, 128).transpose(0, 2, 1))
        m["b2p"] = np.ascontiguousarray(
            np.asarray(inputs["L0_ffn_b2"], np.float32)[e0:e0 + EPC])
        m.update(_host_consts(S, c))
        maps.append(m)
    return maps


def _run(S, inputs):
    nc = _build(S)
    maps = _in_maps(S, inputs)
    res = run_bass_kernel_spmd(nc, maps, core_ids=list(range(W)))
    return np.stack([res.results[c]["out"] for c in range(W)], axis=0)


def kernel(**inputs):
    S = np.asarray(inputs["L0_attn_x"]).shape[1]
    return _run(S, inputs)


# version 04
# speedup vs baseline: 1.6339x; 1.0300x over previous
import contextlib
import numpy as np
import concourse.bass as bass
import concourse.mybir as mybir
from concourse.bass_utils import run_bass_kernel_spmd
from concourse.bass_core_rust import IndirectOffsetOnAxis

F32 = mybir.dt.float32
I32 = mybir.dt.int32
U32 = mybir.dt.uint32
AF = mybir.ActivationFunctionType
ALU = mybir.AluOpType
AX = mybir.AxisListType

DECAY, LR = 0.99, 0.01
D = 1024
M = 128
NE = 16
H = 4096
C = 128
W = 8
EPC = 2


class Phase:

    def __init__(self, sem, gd_sem, prev=None):
        self.sem, self.gd_sem, self.prev = sem, gd_sem, prev
        self.cnt = 0
        self.gd_cnt = 0
        self.ops = {k: [] for k in ("sync", "tensor", "vector", "scalar", "gpsimd")}
        self.first = True

    def do(self, engine, fn, dma=False):
        gd = dma and engine == "gpsimd"
        inc = 16 if dma else 1
        self.ops[engine].append((self.cnt, self.gd_cnt, inc, gd, fn))
        if gd:
            self.gd_cnt += inc
        else:
            self.cnt += inc

    def emit_block(self, nc, block, tail_done=None):
        for eng_name in ("sync", "tensor", "vector", "scalar", "gpsimd"):
            ops = self.ops[eng_name]
            if not ops and not (tail_done and eng_name == "gpsimd"):
                continue

            def body(eng, ops=ops, eng_name=eng_name):
                for wait, gdwait, inc, gd, fn in ops:
                    if wait == 0 and gdwait == 0 and self.prev is not None:
                        eng.wait_ge(self.prev.sem, self.prev.cnt)
                        if getattr(self.prev, "gd_cnt", 0) > 0:
                            eng.wait_ge(self.prev.gd_sem, self.prev.gd_cnt)
                    else:
                        if wait > 0:
                            eng.wait_ge(self.sem, wait)
                        if gdwait > 0:
                            eng.wait_ge(self.gd_sem, gdwait)
                    fn(eng).then_inc(self.gd_sem if gd else self.sem, inc)
                if tail_done and eng_name == "gpsimd":
                    eng.wait_ge(self.sem, self.cnt)
                    if self.gd_cnt > 0:
                        eng.wait_ge(self.gd_sem, self.gd_cnt)

            getattr(block, eng_name)(body)


def _merge(phs):
    return phs


def _host_consts(S, core):
    cap = S * W // NE
    capT = cap // 128
    SP1 = S + 1
    e0 = EPC * core
    u = np.arange(C)[:, None]
    t = np.arange(C)[None, :]
    maskT = np.where(u < t, LR * (DECAY ** (t - u - 1.0)), 0.0).astype(np.float32)
    qdec = np.broadcast_to(DECAY ** np.arange(C, dtype=np.float64),
                           (128, C)).astype(np.float32)
    kw_col = (LR * DECAY ** (C - 1.0 - np.arange(C)))[:, None].astype(np.float32)
    ident = np.eye(128, dtype=np.float32)
    ltriuT = np.triu(np.ones((128, 128), np.float32), 1)
    iota16 = np.broadcast_to(np.arange(NE, dtype=np.float32), (128, NE)).copy()
    s2d = (np.arange(capT)[None, :] * 128 + np.arange(128)[:, None]).astype(np.float32)

    bp = np.arange(W)[:, None, None, None]
    ep = np.arange(NE)[None, :, None, None]
    bb = np.arange(W + 1)[None, None, :, None]
    jj = np.arange(EPC)[None, None, None, :]
    g18 = ((ep == e0 + jj) & (bp < bb)).astype(np.float32).reshape(128, (W + 1) * EPC)
    dm = (((bp == bb) & (ep < e0 + jj)).astype(np.float32)
          - ((bp < bb) & (ep == e0 + jj)).astype(np.float32))
    dm16 = dm.reshape(128, (W + 1) * EPC)[:, :W * EPC]
    rowb16 = (S * np.arange(W)[:, None] + 0 * np.arange(EPC)[None, :]
              ).astype(np.float32).reshape(1, W * EPC)
    eo = np.arange(NE)[None, None, :]
    mgown = ((np.arange(NE)[None, :, None] == eo)
             & (np.arange(W)[:, None, None] < core)).astype(np.float32).reshape(128, NE)
    return dict(maskT=maskT, qdec=qdec, kw_col=kw_col, ident=ident, ltriuT=ltriuT,
                iota16=iota16, s2d=s2d, g18mat=g18, dm16mat=dm16, rowb16=rowb16,
                mgownmat=mgown)


def _build(S):
    cap = S * W // NE
    capT = cap // 128
    SP1 = S + 1
    NT = S // 128
    NSC = S // 512
    NCH = S // C
    NST = (EPC * cap) // 512
    STPE = cap // 512

    nc = bass.Bass("TRN2", target_bir_lowering=False)

    ap_in = {}
    for name, shape in [
        ("x", [S, D]), ("WkT", [D, M]), ("WvT", [D, M]), ("WqT", [D, M]),
        ("bk", [M]), ("bv", [M]), ("bq", [M]), ("WoT", [M, D]), ("bo", [D]),
        ("Wg", [D, NE]), ("bg", [NE]),
        ("W1p", [EPC, D, H]), ("b1pT", [EPC, 128, H // 128]),
        ("W2p", [EPC, H, D]), ("b2p", [EPC, D]),
        ("maskT", [C, C]), ("qdec", [128, C]), ("kw_col", [C, 1]),
        ("ident", [128, 128]), ("ltriuT", [128, 128]), ("iota16", [128, NE]),
        ("s2d", [128, capT]), ("g18mat", [128, (W + 1) * EPC]),
        ("dm16mat", [128, W * EPC]), ("rowb16", [1, W * EPC]),
        ("mgownmat", [128, NE]),
    ]:
        ap_in[name] = nc.dram_tensor(name, shape, F32, kind="ExternalInput").ap()

    out_d = nc.dram_tensor("out", [S, D], F32, kind="ExternalOutput").ap()
    xn_d = nc.dram_tensor("xn_d", [S, D], F32).ap()
    attn_d = nc.dram_tensor("attn_d", [S, D], F32).ap()
    ag1_in = nc.dram_tensor("ag1_in", [S, D], F32)
    ag1_out = nc.dram_tensor("ag1_out", [W * S, D], F32)
    ag1c_in = nc.dram_tensor("ag1c_in", [1, NE], F32)
    ag1c_out = nc.dram_tensor("ag1c_out", [W, NE], F32)
    ag2_in = nc.dram_tensor("ag2_in", [EPC * cap, D], F32)
    ag2_out = nc.dram_tensor("ag2_out", [W * EPC * cap, D], F32)

    ctx = contextlib.ExitStack()
    with ctx:
        sb = lambda n, s, dt=F32: ctx.enter_context(nc.sbuf_tensor(n, s, dt))
        sem = lambda n: ctx.enter_context(nc.semaphore(n))

        WkT_sb = sb("WkT_sb", [128, 8 * M])
        WvT_sb = sb("WvT_sb", [128, 8 * M])
        WqT_sb = sb("WqT_sb", [128, 8 * M])
        WoT_sb = sb("WoT_sb", [128, D])
        Wg_sb = sb("Wg_sb", [128, 8 * NE])
        bk_r = sb("bk_r", [1, M]); bv_r = sb("bv_r", [1, M]); bq_r = sb("bq_r", [1, M])
        bo_r = sb("bo_r", [1, D]); bg_r = sb("bg_r", [1, NE])
        b1T_sb = sb("b1T_sb", [128, EPC * (H // 128)])
        b2_r = sb("b2_r", [1, EPC * D])
        maskT_sb = sb("maskT_sb", [C, C])
        qdec_sb = sb("qdec_sb", [128, C])
        kw_sb = sb("kw_sb", [C, 1])
        id_sb = sb("id_sb", [128, 128])
        ltr_sb = sb("ltr_sb", [128, 128])
        iota_sb = sb("iota_sb", [128, NE])
        s2d_sb = sb("s2d_sb", [128, capT])
        g18_sb = sb("g18_sb", [128, (W + 1) * EPC])
        dm16_sb = sb("dm16_sb", [128, W * EPC])
        rowb_sb = sb("rowb_sb", [1, W * EPC])
        mgo_sb = sb("mgo_sb", [128, NE])
        ones_r = sb("ones_r", [1, 512])
        ones_c = sb("ones_c", [128, 1])
        idx_st = sb("idx_st", [128, NT])
        gate_st = sb("gate_st", [128, NT])
        rank_st = sb("rank_st", [128, NT])
        runb = sb("runb", [1, NE])
        loff_own = sb("loff_own", [1, NE])
        zer16 = sb("zer16", [1, NE])
        counts_c = sb("counts_c", [128, 1])
        g18row = sb("g18row", [1, (W + 1) * EPC])
        d16row = sb("d16row", [1, W * EPC])
        gown_bc = sb("gown_bc", [128, NE])
        g18bc = sb("g18bc", [128, (W + 1) * EPC])
        d16bc = sb("d16bc", [128, W * EPC])
        src_i = sb("src_i", [128, EPC * capT], I32)

        s_p1 = sem("s_p1"); s_p2 = sem("s_p2")
        s_p3 = sem("s_p3"); s_p5 = sem("s_p5")
        s_p1g = sem("s_p1g"); s_p2g = sem("s_p2g")
        s_p3g = sem("s_p3g"); s_p5g = sem("s_p5g")
        gsem = [sem("gs0"), sem("gs1")]
        w1s = [sem("w1s0"), sem("w1s1")]
        w2s = [sem("w2s0"), sem("w2s1")]
        osem = [sem("os0"), sem("os1")]
        mm3 = sem("mm3"); rl3 = sem("rl3"); ev3 = sem("ev3")
        agd = sem("agd")
        v5 = sem("v5")
        l5sem = [sem("l5a"), sem("l5b")]
        g5sem = [sem("g5a"), sem("g5b")]
        o5sem = [sem("o5a"), sem("o5b")]

        ph1 = Phase(s_p1, s_p1g)
        ph2 = Phase(s_p2, s_p2g, ph1)
        ph3 = Phase(s_p3, s_p3g, ph2)
        import types
        ph4d = types.SimpleNamespace(sem=agd, cnt=1, gd_cnt=0, gd_sem=None)
        ph5 = Phase(s_p5, s_p5g, ph4d)

        with contextlib.ExitStack() as c1:
            sb1 = lambda n, s, dt=F32: c1.enter_context(nc.sbuf_tensor(n, s, dt))
            K_sb = sb1("K_sb", [128, S])
            Q_sb = sb1("Q_sb", [128, S])
            Kt_sb = sb1("Kt_sb", [128, S])
            Vt_sb = sb1("Vt_sb", [128, S])

            with contextlib.ExitStack() as c1a:
                sb1a = lambda n, s, dt=F32: c1a.enter_context(nc.sbuf_tensor(n, s, dt))
                ps1 = lambda n, s: c1a.enter_context(nc.psum_tensor(n, s, F32))
                xt = sb1a("xt", [128, D])
                sq = sb1a("sq", [128, D])
                xn_tm = sb1a("xn_tm", [128, D])
                xn_fm = sb1a("xn_fm", [128, 8 * 512])
                lcol = sb1a("lcol", [128, 16])
                maxes = sb1a("maxes", [128, 8])
                idx8 = sb1a("idx8", [128, 8], U32)
                logit_sb = sb1a("logit_sb", [128, NE])
                probs = sb1a("probs", [128, NE])
                oneh = sb1a("oneh", [128, NE])
                posf = sb1a("posf", [128, NE])
                ptr = ps1("ptr", [128, 128])
                pl = ps1("pl", [128, NE])
                pp = ps1("pp", [128, NE])
                pc = ps1("pc", [1, NE])
                pq = ps1("pq", [128, 512])
                pk = ps1("pk", [128, 512])
                pkt = ps1("pkt", [128, 128])
                pvt = ps1("pvt", [128, 128])

                P = ph1
                for wT, dst in [(ap_in["WkT"], WkT_sb), (ap_in["WvT"], WvT_sb),
                                (ap_in["WqT"], WqT_sb)]:
                    P.do("sync", lambda e, wT=wT, dst=dst: e.dma_start(
                        out=dst[:].rearrange("p (c m) -> p c m", c=8),
                        in_=wT.rearrange("(c p) m -> p c m", p=128)), dma=True)
                P.do("sync", lambda e: e.dma_start(out=WoT_sb[:], in_=ap_in["WoT"]), dma=True)
                P.do("sync", lambda e: e.dma_start(
                    out=Wg_sb[:].rearrange("p (c m) -> p c m", c=8),
                    in_=ap_in["Wg"].rearrange("(c p) m -> p c m", p=128)), dma=True)
                for src, dst in [("bk", bk_r), ("bv", bv_r), ("bq", bq_r),
                                 ("bo", bo_r), ("bg", bg_r)]:
                    P.do("sync", lambda e, src=src, dst=dst: e.dma_start(
                        out=dst[:], in_=ap_in[src][None, :]), dma=True)
                for a in range(EPC):
                    P.do("sync", lambda e, a=a: e.dma_start(
                        out=b1T_sb[:, (H // 128) * a:(H // 128) * (a + 1)],
                        in_=ap_in["b1pT"][a]), dma=True)
                    P.do("sync", lambda e, a=a: e.dma_start(
                        out=b2_r[:, D * a:D * (a + 1)],
                        in_=ap_in["b2p"][a][None, :]), dma=True)
                for src, dst in [("maskT", maskT_sb), ("qdec", qdec_sb), ("kw_col", kw_sb),
                                 ("ident", id_sb), ("ltriuT", ltr_sb), ("iota16", iota_sb),
                                 ("s2d", s2d_sb), ("g18mat", g18_sb), ("dm16mat", dm16_sb),
                                 ("rowb16", rowb_sb), ("mgownmat", mgo_sb)]:
                    P.do("sync", lambda e, src=src, dst=dst: e.dma_start(
                        out=dst[:], in_=ap_in[src][:, :]), dma=True)
                P.do("vector", lambda e: e.memset(ones_r[:], 1.0))
                P.do("vector", lambda e: e.memset(ones_c[:], 1.0))
                P.do("vector", lambda e: e.memset(zer16[:], 0.0))
                P.do("vector", lambda e: e.memset(runb[:], 0.0))

                for sc in range(NSC):
                    for it in range(4):
                        ti = 4 * sc + it
                        P.do("sync", lambda e, ti=ti: e.dma_start(
                            out=xt[:], in_=ap_in["x"][128 * ti:128 * (ti + 1), :]), dma=True)
                        P.do("vector", lambda e: e.reduce_sum(lcol[:, 0:1], xt[:], axis=AX.X))
                        P.do("scalar", lambda e: e.activation(sq[:], xt[:], AF.Square,
                                                              accum_out=lcol[:, 2:3]))
                        P.do("vector", lambda e: e.tensor_scalar_mul(lcol[:, 1:2], lcol[:, 0:1], -1.0 / D))
                        P.do("vector", lambda e: e.tensor_scalar_mul(lcol[:, 3:4], lcol[:, 2:3], 1.0 / D))
                        P.do("scalar", lambda e: e.activation(lcol[:, 4:5], lcol[:, 1:2], AF.Square))
                        P.do("vector", lambda e: e.tensor_sub(lcol[:, 5:6], lcol[:, 3:4], lcol[:, 4:5]))
                        P.do("vector", lambda e: e.tensor_scalar_add(lcol[:, 5:6], lcol[:, 5:6], 1e-5))
                        P.do("vector", lambda e: e.reciprocal(lcol[:, 6:7], lcol[:, 5:6]))
                        P.do("scalar", lambda e: e.activation(lcol[:, 7:8], lcol[:, 6:7], AF.Sqrt))
                        P.do("vector", lambda e: e.tensor_scalar(xn_tm[:], xt[:], lcol[:, 1:2],
                                                                 lcol[:, 7:8], ALU.add, ALU.mult))
                        P.do("sync", lambda e, ti=ti: e.dma_start(
                            out=xn_d[128 * ti:128 * (ti + 1), :], in_=xn_tm[:]), dma=True)
                        for dc in range(8):
                            P.do("tensor", lambda e, dc=dc: e.transpose(
                                ptr[:], xn_tm[:, 128 * dc:128 * (dc + 1)], id_sb[:]))
                            P.do("vector", lambda e, dc=dc, it=it: e.tensor_copy(
                                xn_fm[:, 512 * dc + 128 * it:512 * dc + 128 * (it + 1)], ptr[:]))
                        for dc in range(8):
                            P.do("tensor", lambda e, dc=dc, it=it: e.matmul(
                                pl[:], xn_fm[:, 512 * dc + 128 * it:512 * dc + 128 * (it + 1)],
                                Wg_sb[:, NE * dc:NE * (dc + 1)], start=(dc == 0), stop=False))
                        P.do("tensor", lambda e: e.matmul(pl[:], ones_r[:, 0:128], bg_r[:],
                                                          start=False, stop=True))
                        P.do("vector", lambda e: e.tensor_copy(logit_sb[:], pl[:]))
                        P.do("vector", lambda e: e.max(maxes[:], logit_sb[:]))
                        P.do("vector", lambda e: e.tensor_scalar_mul(lcol[:, 8:9], maxes[:, 0:1], -1.0))
                        P.do("scalar", lambda e: e.activation(probs[:], logit_sb[:], AF.Exp,
                                                              bias=lcol[:, 8:9], accum_out=lcol[:, 9:10]))
                        P.do("vector", lambda e, ti=ti: e.reciprocal(gate_st[:, ti:ti + 1], lcol[:, 9:10]))
                        P.do("vector", lambda e: e.max_index(idx8[:], maxes[:], logit_sb[:]))
                        P.do("vector", lambda e, ti=ti: e.tensor_copy(idx_st[:, ti:ti + 1], idx8[:, 0:1]))
                        P.do("vector", lambda e, ti=ti: e.tensor_scalar(
                            oneh[:], iota_sb[:], idx_st[:, ti:ti + 1], None, ALU.is_equal))
                        P.do("tensor", lambda e: e.matmul(pp[:], ltr_sb[:], oneh[:],
                                                          start=True, stop=False))
                        P.do("tensor", lambda e: e.matmul(pp[:], ones_r[:, 0:128], runb[:],
                                                          start=False, stop=True))
                        P.do("vector", lambda e: e.tensor_mul(posf[:], oneh[:], pp[:]))
                        P.do("vector", lambda e, ti=ti: e.reduce_sum(rank_st[:, ti:ti + 1],
                                                                     posf[:], axis=AX.X))
                        P.do("tensor", lambda e: e.matmul(pc[:], ones_c[:], oneh[:],
                                                          start=True, stop=True))
                        P.do("vector", lambda e: e.tensor_add(runb[:], runb[:], pc[:]))
                    t0 = 512 * sc
                    for dst_ps, wT, b_r in [(pq, WqT_sb, bq_r), (pk, WkT_sb, bk_r)]:
                        for dc in range(8):
                            P.do("tensor", lambda e, dc=dc, dst_ps=dst_ps, wT=wT: e.matmul(
                                dst_ps[:], wT[:, M * dc:M * (dc + 1)],
                                xn_fm[:, 512 * dc:512 * (dc + 1)], start=(dc == 0), stop=False))
                        P.do("tensor", lambda e, dst_ps=dst_ps, b_r=b_r: e.matmul(
                            dst_ps[:], b_r[:], ones_r[:, 0:512], start=False, stop=True))
                    P.do("vector", lambda e, t0=t0: e.tensor_copy(Q_sb[:, t0:t0 + 512], pq[:]))
                    P.do("vector", lambda e, t0=t0: e.tensor_copy(K_sb[:, t0:t0 + 512], pk[:]))
                    for it in range(4):
                        tt = 512 * sc + 128 * it
                        for dst_ps, wT, b_r in [(pkt, WkT_sb, bk_r), (pvt, WvT_sb, bv_r)]:
                            for dc in range(8):
                                P.do("tensor", lambda e, dc=dc, it=it, dst_ps=dst_ps, wT=wT: e.matmul(
                                    dst_ps[:], xn_fm[:, 512 * dc + 128 * it:512 * dc + 128 * (it + 1)],
                                    wT[:, M * dc:M * (dc + 1)], start=(dc == 0), stop=False))
                            P.do("tensor", lambda e, dst_ps=dst_ps, b_r=b_r: e.matmul(
                                dst_ps[:], ones_r[:, 0:128], b_r[:], start=False, stop=True))
                        P.do("vector", lambda e, tt=tt: e.tensor_scalar(
                            Kt_sb[:, tt:tt + 128], pkt[:], kw_sb[:], None, ALU.mult))
                        P.do("vector", lambda e, tt=tt: e.tensor_copy(Vt_sb[:, tt:tt + 128], pvt[:]))

                with nc.Block("seg1") as block:
                    ph1.emit_block(nc, block)

            with contextlib.ExitStack() as c2:
                sb2 = lambda n, s, dt=F32: c2.enter_context(nc.sbuf_tensor(n, s, dt))
                ps2 = lambda n, s: c2.enter_context(nc.psum_tensor(n, s, F32))
                T_sb = sb2("T_sb", [128, 128])
                Qd = sb2("Qd", [128, C])
                Am = sb2("Am", [C, C])
                oneh2 = sb2("oneh2", [128, NE])
                posf2 = sb2("posf2", [128, NE])
                dest_f = sb2("dest_f", [128, 1])
                dest_i = sb2("dest_i", [128, 1], I32)
                lobc = sb2("lobc", [128, NE])
                xrow = sb2("xrow", [128, D])
                arow = sb2("arow", [128, D])
                pA = ps2("pA", [C, C])
                pO = ps2("pO", [128, C])
                pT = ps2("pT", [128, 128])
                pao = ps2("pao", [128, 512])
                plo = ps2("plo", [128, NE])

                P = ph2
                P.do("vector", lambda e: e.tensor_tensor_scan(
                    loff_own[:], runb[:], zer16[:], 0.0, ALU.add, ALU.add))
                P.do("vector", lambda e: e.tensor_sub(loff_own[:], loff_own[:], runb[:]))
                P.do("tensor", lambda e: e.matmul(plo[:], ones_r[:, 0:128], loff_own[:],
                                                  start=True, stop=True))
                P.do("vector", lambda e: e.tensor_copy(lobc[:], plo[:]))
                P.do("sync", lambda e: e.dma_start(
                    out=ag1c_in.ap()[:, :], in_=runb[:]), dma=True)
                for ti in range(NT):
                    P.do("sync", lambda e, ti=ti: e.dma_start(
                        out=xrow[:], in_=xn_d[128 * ti:128 * (ti + 1), :]), dma=True)
                    P.do("vector", lambda e, ti=ti: e.tensor_scalar(
                        oneh2[:], iota_sb[:], idx_st[:, ti:ti + 1], None, ALU.is_equal))
                    P.do("vector", lambda e: e.tensor_mul(posf2[:], oneh2[:], lobc[:]))
                    P.do("vector", lambda e: e.reduce_sum(dest_f[:], posf2[:], axis=AX.X))
                    P.do("vector", lambda e, ti=ti: e.tensor_add(
                        dest_f[:], dest_f[:], rank_st[:, ti:ti + 1]))
                    P.do("vector", lambda e: e.tensor_copy(dest_i[:], dest_f[:]))
                    P.do("gpsimd", lambda e: e.indirect_dma_start(
                        ag1_in.ap()[:, :], IndirectOffsetOnAxis(dest_i[:], 0),
                        xrow[:], None), dma=True)
                P.do("vector", lambda e: e.memset(T_sb[:], 0.0))
                for ch in range(NCH):
                    t0 = C * ch
                    P.do("vector", lambda e, t0=t0: e.tensor_mul(
                        Qd[:], Q_sb[:, t0:t0 + C], qdec_sb[:]))
                    P.do("tensor", lambda e, t0=t0: e.matmul(
                        pA[:], K_sb[:, t0:t0 + C], Q_sb[:, t0:t0 + C], start=True, stop=True))
                    P.do("vector", lambda e: e.tensor_mul(Am[:], pA[:], maskT_sb[:]))
                    P.do("tensor", lambda e: e.matmul(pO[:], T_sb[:], Qd[:],
                                                      start=True, stop=False))
                    P.do("tensor", lambda e, t0=t0: e.matmul(
                        pO[:], Vt_sb[:, t0:t0 + C], Am[:], start=False, stop=True))
                    P.do("tensor", lambda e, t0=t0: e.matmul(
                        pT[:], Kt_sb[:, t0:t0 + C], Vt_sb[:, t0:t0 + C], start=True, stop=True))
                    P.do("vector", lambda e: e.scalar_tensor_tensor(
                        T_sb[:], T_sb[:], DECAY ** C, pT[:], ALU.mult, ALU.add))
                    P.do("vector", lambda e, t0=t0: e.tensor_copy(K_sb[:, t0:t0 + C], pO[:]))
                for ti in range(NT):
                    t0 = 128 * ti
                    for dh in range(2):
                        P.do("tensor", lambda e, t0=t0, dh=dh: e.matmul(
                            pao[:], K_sb[:, t0:t0 + 128], WoT_sb[:, 512 * dh:512 * (dh + 1)],
                            start=True, stop=False))
                        P.do("tensor", lambda e, dh=dh: e.matmul(
                            pao[:], ones_r[:, 0:128], bo_r[:, 512 * dh:512 * (dh + 1)],
                            start=False, stop=True))
                        P.do("vector", lambda e, dh=dh: e.tensor_copy(
                            arow[:, 512 * dh:512 * (dh + 1)], pao[:]))
                    P.do("sync", lambda e, t0=t0: e.dma_start(
                        out=attn_d[t0:t0 + 128, :], in_=arow[:]), dma=True)

                P.do("gpsimd", lambda e: e.collective_compute(
                    "AllGather", ALU.bypass, replica_groups=[list(range(W))],
                    ins=[ag1c_in.ap().opt()], outs=[ag1c_out.ap().opt()]))
                P.do("gpsimd", lambda e: e.collective_compute(
                    "AllGather", ALU.bypass, replica_groups=[list(range(W))],
                    ins=[ag1_in.ap().opt()], outs=[ag1_out.ap().opt()]))
                with nc.Block("seg2") as block:
                    ph2.emit_block(nc, block)

        with contextlib.ExitStack() as c3:
            sb3 = lambda n, s, dt=F32: c3.enter_context(nc.sbuf_tensor(n, s, dt))
            ps3 = lambda n, s: c3.enter_context(nc.psum_tensor(n, s, F32))
            xg_tm = [sb3("xg_tmA", [128, D]), sb3("xg_tmB", [128, D])]
            xg_fm = sb3("xg_fm", [128, 8 * 512])
            h_sb = sb3("h_sb", [128, 32 * 512])
            w1_sb = [sb3("w1_sbA", [128, 8 * 128]), sb3("w1_sbB", [128, 8 * 128])]
            w2_sb = [sb3("w2_sbA", [128, 512]), sb3("w2_sbB", [128, 512])]
            y_sb = [sb3("y_sbA", [128, 512]), sb3("y_sbB", [128, 512])]
            macc = sb3("macc", [128, capT])
            m1 = sb3("m1", [128, capT])
            m2 = sb3("m2", [128, capT])
            mt = sb3("mt", [128, capT])
            ptr3 = ps3("ptr3", [128, 128])
            ptr3b = ps3("ptr3b", [128, 128])
            ph_ps = [ps3("ph_psA", [128, 512]), ps3("ph_psB", [128, 512])]
            py = [ps3(f"py{i}", [128, 512]) for i in range(4)]

            P = ph3
            P.do("sync", lambda e: e.dma_start(
                out=counts_c[:, 0:1],
                in_=ag1c_out.ap().rearrange("(x b) e -> x (b e)", x=1).transpose([1, 0])),
                dma=True)
            P.do("tensor", lambda e: e.matmul(py[0][0:1, 0:(W + 1) * EPC], counts_c[:], g18_sb[:],
                                              start=True, stop=True))
            P.do("vector", lambda e: e.tensor_copy(g18row[:], py[0][0:1, 0:(W + 1) * EPC]))
            P.do("tensor", lambda e: e.matmul(py[1][:, 0:(W + 1) * EPC], ones_r[:, 0:128], g18row[:],
                                              start=True, stop=True))
            P.do("vector", lambda e: e.tensor_copy(g18bc[:], py[1][:, 0:(W + 1) * EPC]))
            P.do("tensor", lambda e: e.matmul(py[0][0:1, 0:W * EPC], counts_c[:], dm16_sb[:],
                                              start=True, stop=False))
            P.do("tensor", lambda e: e.matmul(py[0][0:1, 0:W * EPC], ones_c[0:1, 0:1], rowb_sb[:],
                                              start=False, stop=True))
            P.do("vector", lambda e: e.tensor_copy(d16row[:], py[0][0:1, 0:W * EPC]))
            P.do("tensor", lambda e: e.matmul(py[1][:, 0:W * EPC], ones_r[:, 0:128], d16row[:],
                                              start=True, stop=True))
            P.do("vector", lambda e: e.tensor_copy(d16bc[:], py[1][:, 0:W * EPC]))
            P.do("tensor", lambda e: e.matmul(py[0][0:1, 0:NE], counts_c[:], mgo_sb[:],
                                              start=True, stop=True))
            P.do("vector", lambda e: e.tensor_copy(d16row[:, 0:NE], py[0][0:1, 0:NE]))
            P.do("tensor", lambda e: e.matmul(py[1][:, 0:NE], ones_r[:, 0:128],
                                              d16row[:, 0:NE], start=True, stop=True))
            P.do("vector", lambda e: e.tensor_copy(gown_bc[:], py[1][:, 0:NE]))
            for j in range(EPC):
                P.do("vector", lambda e: e.memset(macc[:], 0.0))
                for b in range(W):
                    P.do("vector", lambda e, b=b, j=j: e.tensor_scalar(
                        m1[:], s2d_sb[:], g18bc[:, EPC * b + j:EPC * b + j + 1],
                        None, ALU.is_ge))
                    P.do("vector", lambda e, b=b, j=j: e.tensor_scalar(
                        m2[:], s2d_sb[:], g18bc[:, EPC * (b + 1) + j:EPC * (b + 1) + j + 1],
                        None, ALU.is_lt))
                    P.do("vector", lambda e: e.tensor_mul(m1[:], m1[:], m2[:]))
                    P.do("vector", lambda e, b=b, j=j: e.tensor_scalar(
                        mt[:], s2d_sb[:], d16bc[:, EPC * b + j:EPC * b + j + 1],
                        None, ALU.add))
                    P.do("vector", lambda e: e.tensor_mul(mt[:], mt[:], m1[:]))
                    P.do("vector", lambda e: e.tensor_add(macc[:], macc[:], mt[:]))
                P.do("vector", lambda e, j=j: e.tensor_copy(
                    src_i[:, capT * j:capT * (j + 1)], macc[:]))
            NG = NST * 4
            NH1 = NST * 32
            NW2 = NST * 64
            NY = NST * 8
            tr_ev = {}
            w1grp_ev = {}

            w2grp_ev = {}
            bias_ev = {}
            evac_ev = {}
            relu_ev = {}
            xg_done = {}
            yev = {}
            py_last = [0, 0, 0, 0]

            pe_ops = []
            sc_ops = []
            ve_ops = []
            sy_ops = []
            gp_ops = []
            mm_c = [0]; rl_c = [0]; ev_c = [0]

            for st in range(NST):
                j = st // STPE
                stj = st % STPE
                for k in range(4):
                    g = st * 4 + k
                    p = g % 2
                    col = capT * j + 4 * stj + k
                    waits = []
                    if g >= 2:
                        waits.append((gsem[p], 16 * (g // 2)))
                        waits.append((mm3, tr_ev[(g - 2) * 8 + 7]))
                    gp_ops.append((waits, (lambda e, col=col, p=p: e.indirect_dma_start(
                        xg_tm[p][:], None, ag1_out.ap()[:, :],
                        IndirectOffsetOnAxis(src_i[:, col:col + 1], 0))),
                        (gsem[p], 16)))
                    for dc in range(8):
                        n = g * 8 + dc
                        q = n % 2
                        PT = ptr3 if q == 0 else ptr3b
                        waits = []
                        if dc == 0:
                            waits.append((gsem[p], 16 * (g // 2 + 1)))
                        if n >= 2:
                            waits.append((rl3, evac_ev[n - 2]))
                        mm_c[0] += 1
                        tr_ev[n] = mm_c[0]
                        pe_ops.append((waits, (lambda e, dc=dc, p=p, PT=PT: e.transpose(
                            PT[:], xg_tm[p][:, 128 * dc:128 * (dc + 1)], id_sb[:])),
                            (mm3, 1)))
                        rl_c[0] += 1
                        evac_ev[n] = rl_c[0]
                        sc_ops.append(([(mm3, tr_ev[n])],
                                       (lambda e, dc=dc, k=k, PT=PT: e.copy(
                                           xg_fm[:, 512 * dc + 128 * k:512 * dc + 128 * (k + 1)],
                                           PT[:])), (rl3, 1)))
                xg_done[st] = rl_c[0]
                for hc in range(32):
                    l = st * 32 + hc
                    p = l % 2
                    waits = [(w1s[p], 16 * (l // 2 + 1))]
                    if hc == 0:
                        waits.append((rl3, xg_done[st]))
                    if l >= 2:
                        waits.append((rl3, relu_ev[l - 2]))
                    for dc in range(8):
                        mmw = waits if dc == 0 else []
                        inc = None
                        if dc == 7:
                            mm_c[0] += 1
                            w1grp_ev[l] = mm_c[0]
                            inc = (mm3, 1)
                        pe_ops.append((mmw, (lambda e, dc=dc, p=p: e.matmul(
                            ph_ps[p][:], w1_sb[p][:, 128 * dc:128 * (dc + 1)],
                            xg_fm[:, 512 * dc:512 * (dc + 1)],
                            start=(dc == 0), stop=(dc == 7))), inc))
                    lw = [(w1s[p], 16 * (l // 2))] if l >= 2 else []
                    if l >= 2:
                        lw.append((mm3, w1grp_ev[l - 2]))
                    sy_ops.append((lw, (lambda e, j=j, hc=hc, p=p: e.dma_start(
                        out=w1_sb[p][:].rearrange("p (c h) -> p c h", c=8),
                        in_=ap_in["W1p"][j, :, 128 * hc:128 * (hc + 1)]
                        .rearrange("(c p) h -> p c h", p=128))), (w1s[p], 16)))
                    rl_c[0] += 1
                    relu_ev[l] = rl_c[0]
                    sc_ops.append(([(mm3, w1grp_ev[l])],
                                   (lambda e, j=j, hc=hc, p=p: e.activation(
                                       h_sb[:, 512 * hc:512 * (hc + 1)], ph_ps[p][:],
                                       AF.Relu,
                                       bias=b1T_sb[:, 32 * j + hc:32 * j + hc + 1])),
                                   (rl3, 1)))
                for dh in range(2):
                    for hc in range(32):
                        mw2 = (st * 2 + dh) * 32 + hc
                        q = mw2 % 2
                        waits = [(w2s[q], 16 * (mw2 // 2 + 1))]
                        if dh == 0:
                            waits.append((rl3, relu_ev[st * 32 + hc]))
                        if hc == 0:
                            pl_ = max(py_last)
                            if pl_ > 0:
                                waits.append((ev3, pl_))
                        for tt in range(4):
                            mmw = waits if tt == 0 else []
                            inc = None
                            if tt == 3:
                                mm_c[0] += 1
                                w2grp_ev[mw2] = mm_c[0]
                                inc = (mm3, 1)
                            pe_ops.append((mmw, (lambda e, hc=hc, tt=tt, q=q: e.matmul(
                                py[tt][:],
                                h_sb[:, 512 * hc + 128 * tt:512 * hc + 128 * (tt + 1)],
                                w2_sb[q][:], start=(hc == 0), stop=False)), inc))
                        lw = [(w2s[q], 16 * (mw2 // 2))] if mw2 >= 2 else []
                        if mw2 >= 2:
                            lw.append((mm3, w2grp_ev[mw2 - 2]))
                        sy_ops.append((lw, (lambda e, j=j, hc=hc, dh=dh, q=q: e.dma_start(
                            out=w2_sb[q][:],
                            in_=ap_in["W2p"][j, 128 * hc:128 * (hc + 1),
                                             512 * dh:512 * (dh + 1)])), (w2s[q], 16)))
                    for tt in range(4):
                        m = (st * 2 + dh) * 4 + tt
                        q = m % 2
                        mm_c[0] += 1
                        bias_ev[m] = mm_c[0]
                        pe_ops.append(([], (lambda e, tt=tt, j=j, dh=dh: e.matmul(
                            py[tt][:], ones_r[:, 0:128],
                            b2_r[:, D * j + 512 * dh:D * j + 512 * (dh + 1)],
                            start=False, stop=True)), (mm3, 1)))
                        ew = [(mm3, bias_ev[m])]
                        if m >= 2:
                            ew.append((osem[q], 16 * (m // 2)))
                        ev_c[0] += 1
                        yev[m] = ev_c[0]
                        py_last[tt] = ev_c[0]
                        ve_ops.append((ew, (lambda e, tt=tt, q=q: e.tensor_copy(
                            y_sb[q][:], py[tt][:])), (ev3, 1)))
                        row0 = cap * j + 512 * stj + 128 * tt
                        sw = [(ev3, yev[m])]
                        if m >= 2:
                            sw.append((osem[q], 16 * (m // 2)))
                        gp_ops.append((sw, (lambda e, row0=row0, dh=dh, q=q: e.dma_start(
                            out=ag2_in.ap()[row0:row0 + 128, 512 * dh:512 * (dh + 1)],
                            in_=y_sb[q][:])), (osem[q], 16)))
            no2 = [NY // 2, NY - NY // 2]
            gp_ops.append(([(osem[0], 16 * no2[0]), (osem[1], 16 * no2[1])],
                           (lambda e: e.collective_compute(
                               "AllGather", ALU.bypass,
                               replica_groups=[list(range(W))],
                               ins=[ag2_in.ap().opt()], outs=[ag2_out.ap().opt()])),
                           (agd, 1)))

            with nc.Block("seg3") as block:
                ph3.emit_block(nc, block)
                start_w = (s_p3, ph3.cnt)

                def chain_body(ops, start=start_w):
                    def body(eng):
                        eng.wait_ge(start[0], start[1])
                        for waits, fn, inc in ops:
                            for sm, v in waits:
                                eng.wait_ge(sm, v)
                            inst = fn(eng)
                            if inc is not None:
                                inst.then_inc(inc[0], inc[1])
                    return body

                block.tensor(chain_body(pe_ops))
                block.scalar(chain_body(sc_ops))
                block.vector(chain_body(ve_ops))
                block.sync(chain_body(sy_ops))
                block.gpsimd(chain_body(gp_ops))


        with contextlib.ExitStack() as c5:
            sb5 = lambda n, s, dt=F32: c5.enter_context(nc.sbuf_tensor(n, s, dt))
            xt5 = [sb5(f"xt5{i}", [128, D]) for i in range(2)]
            at5 = [sb5(f"at5{i}", [128, D]) for i in range(2)]
            yg5 = [sb5(f"yg5{i}", [128, D]) for i in range(2)]
            oneh5 = [sb5(f"oneh5{i}", [128, NE]) for i in range(2)]
            posf5 = [sb5(f"posf5{i}", [128, NE]) for i in range(2)]
            col5 = [sb5(f"col5{i}", [128, 8]) for i in range(2)]
            yri5 = [sb5(f"yri5{i}", [128, 1], I32) for i in range(2)]
            out5 = [sb5(f"out5{i}", [128, D]) for i in range(2)]

            v5_ops = []
            g5_ops = []
            s5_ld = []
            s5_st = []
            v5c = [0]
            yri_ev = {}
            out_ev = {}

            for ti in range(NT):
                t0 = 128 * ti
                a = ti % 2
                OH, PF, CL = oneh5[a], posf5[a], col5[a]
                lw = [(l5sem[a], 32 * (ti // 2))] if ti >= 2 else []
                if ti >= 2:
                    lw.append((v5, out_ev[ti - 2]))
                s5_ld.append((lw, (lambda e, t0=t0, a=a: e.dma_start(
                    out=xt5[a][:], in_=ap_in["x"][t0:t0 + 128, :])), (l5sem[a], 16)))
                s5_ld.append(([], (lambda e, t0=t0, a=a: e.dma_start(
                    out=at5[a][:], in_=attn_d[t0:t0 + 128, :])), (l5sem[a], 16)))

                def v(fn, waits=(), inc=True):
                    w = list(waits)
                    if v5c[0] > 0:
                        w.append((v5, v5c[0]))
                    v5c[0] += 1
                    v5_ops.append((w, fn, (v5, 1)))

                v(lambda e, ti=ti, OH=OH: e.tensor_scalar(
                    OH[:], iota_sb[:], idx_st[:, ti:ti + 1], None, ALU.is_equal))
                v(lambda e, OH=OH, PF=PF: e.tensor_mul(PF[:], OH[:], gown_bc[:]))
                v(lambda e, PF=PF: e.tensor_add(PF[:, 0:8], PF[:, 0:8], PF[:, 8:16]))
                v(lambda e, PF=PF: e.tensor_add(PF[:, 0:4], PF[:, 0:4], PF[:, 4:8]))
                v(lambda e, PF=PF: e.tensor_add(PF[:, 0:2], PF[:, 0:2], PF[:, 2:4]))
                v(lambda e, PF=PF, CL=CL: e.tensor_add(CL[:, 0:1], PF[:, 0:1], PF[:, 1:2]))
                v(lambda e, ti=ti, CL=CL: e.tensor_add(
                    CL[:, 0:1], CL[:, 0:1], rank_st[:, ti:ti + 1]))
                v(lambda e, CL=CL: e.tensor_single_scalar(
                    CL[:, 1:2], CL[:, 0:1], float(cap), ALU.is_lt))
                v(lambda e, ti=ti, CL=CL: e.tensor_mul(
                    CL[:, 2:3], CL[:, 1:2], gate_st[:, ti:ti + 1]))
                v(lambda e, ti=ti, CL=CL: e.tensor_scalar(
                    CL[:, 3:4], idx_st[:, ti:ti + 1], float(cap), None, ALU.mult))
                v(lambda e, CL=CL: e.tensor_add(CL[:, 3:4], CL[:, 3:4], CL[:, 0:1]))
                v(lambda e, CL=CL: e.tensor_single_scalar(
                    CL[:, 3:4], CL[:, 3:4], float(NE * cap - 1), ALU.min))
                yw = [(g5sem[a], 16 * (ti // 2))] if ti >= 2 else []
                v(lambda e, a=a, CL=CL: e.tensor_copy(yri5[a][:], CL[:, 3:4]), waits=yw)
                yri_ev[ti] = v5c[0]
                gw = [(v5, yri_ev[ti])]
                if ti >= 2:
                    gw.append((g5sem[a], 16 * (ti // 2)))
                g5_ops.append((gw, (lambda e, a=a: e.indirect_dma_start(
                    yg5[a][:], None, ag2_out.ap()[:, :],
                    IndirectOffsetOnAxis(yri5[a][:], 0))), (g5sem[a], 16)))
                cw = [(l5sem[a], 32 * (ti // 2 + 1)), (g5sem[a], 16 * (ti // 2 + 1))]
                if ti >= 2:
                    cw.append((o5sem[a], 16 * (ti // 2)))
                v(lambda e, a=a, CL=CL: e.scalar_tensor_tensor(
                    out5[a][:], yg5[a][:], CL[:, 2:3], at5[a][:], ALU.mult, ALU.add),
                  waits=cw)
                v(lambda e, a=a: e.tensor_add(out5[a][:], out5[a][:], xt5[a][:]))
                out_ev[ti] = v5c[0]
                sw = [(v5, out_ev[ti])]
                if ti >= 2:
                    sw.append((o5sem[a], 16 * (ti // 2)))
                s5_st.append((sw, (lambda e, t0=t0, a=a: e.dma_start(
                    out=out_d[t0:t0 + 128, :], in_=out5[a][:])), (o5sem[a], 16)))

            with nc.Block("seg4") as block:
                pw = [(ph4d.sem, ph4d.cnt), (s_p3, ph3.cnt)]
                if ph3.gd_cnt:
                    pw.append((s_p3g, ph3.gd_cnt))

                def cbody(ops, extra_first=pw):
                    def body(eng):
                        for sm, vv in extra_first:
                            eng.wait_ge(sm, vv)
                        for waits, fn, inc in ops:
                            for sm, vv in waits:
                                eng.wait_ge(sm, vv)
                            fn(eng).then_inc(inc[0], inc[1])
                    return body

                block.vector(cbody(v5_ops))
                block.gpsimd(cbody(g5_ops))
                sy5 = []
                for ti in range(NT):
                    sy5.extend(s5_ld[2 * ti:2 * ti + 2])
                    if ti >= 1:
                        sy5.append(s5_st[ti - 1])
                sy5.append(s5_st[NT - 1])
                block.sync(cbody(sy5))

                def tail(eng):
                    eng.wait_ge(o5sem[0], 16 * (NT - NT // 2))
                    eng.wait_ge(o5sem[1], 16 * (NT // 2))
                block.scalar(tail)

    return nc


def _in_maps(S, inputs):
    x = np.asarray(inputs["L0_attn_x"], dtype=np.float32)
    Wk = np.asarray(inputs["L0_attn_Wk"], dtype=np.float32)
    Wv = np.asarray(inputs["L0_attn_Wv"], dtype=np.float32)
    Wq = np.asarray(inputs["L0_attn_Wq"], dtype=np.float32)
    Wo = np.asarray(inputs["L0_attn_Wo"], dtype=np.float32)
    W1 = np.asarray(inputs["L0_ffn_W1"], dtype=np.float32)
    W2 = np.asarray(inputs["L0_ffn_W2"], dtype=np.float32)
    b1 = np.asarray(inputs["L0_ffn_b1"], dtype=np.float32)
    base = {
        "WkT": np.ascontiguousarray(Wk.T), "WvT": np.ascontiguousarray(Wv.T),
        "WqT": np.ascontiguousarray(Wq.T), "WoT": np.ascontiguousarray(Wo.T),
        "bk": np.asarray(inputs["L0_attn_bk"], np.float32),
        "bv": np.asarray(inputs["L0_attn_bv"], np.float32),
        "bq": np.asarray(inputs["L0_attn_bq"], np.float32),
        "bo": np.asarray(inputs["L0_attn_bo"], np.float32),
        "Wg": np.asarray(inputs["L0_ffn_Wg"], np.float32),
        "bg": np.asarray(inputs["L0_ffn_bg"], np.float32),
    }
    maps = []
    for c in range(W):
        m = dict(base)
        m["x"] = np.ascontiguousarray(x[c])
        e0 = EPC * c
        m["W1p"] = np.ascontiguousarray(W1[e0:e0 + EPC])
        m["W2p"] = np.ascontiguousarray(W2[e0:e0 + EPC])
        m["b1pT"] = np.ascontiguousarray(
            b1[e0:e0 + EPC].reshape(EPC, H // 128, 128).transpose(0, 2, 1))
        m["b2p"] = np.ascontiguousarray(
            np.asarray(inputs["L0_ffn_b2"], np.float32)[e0:e0 + EPC])
        m.update(_host_consts(S, c))
        maps.append(m)
    return maps


def _run(S, inputs):
    nc = _build(S)
    maps = _in_maps(S, inputs)
    res = run_bass_kernel_spmd(nc, maps, core_ids=list(range(W)))
    return np.stack([res.results[c]["out"] for c in range(W)], axis=0)


def kernel(**inputs):
    S = np.asarray(inputs["L0_attn_x"]).shape[1]
    return _run(S, inputs)


# version 06
# speedup vs baseline: 1.7318x; 1.0599x over previous
import contextlib
import numpy as np
import concourse.bass as bass
import concourse.mybir as mybir
from concourse.bass_utils import run_bass_kernel_spmd
from concourse.bass_core_rust import IndirectOffsetOnAxis

F32 = mybir.dt.float32
I32 = mybir.dt.int32
U32 = mybir.dt.uint32
AF = mybir.ActivationFunctionType
ALU = mybir.AluOpType
AX = mybir.AxisListType

DECAY, LR = 0.99, 0.01
D = 1024
M = 128
NE = 16
H = 4096
C = 128
W = 8
EPC = 2


class Phase:

    def __init__(self, sem, gd_sem, prev=None):
        self.sem, self.gd_sem, self.prev = sem, gd_sem, prev
        self.cnt = 0
        self.gd_cnt = 0
        self.ops = {k: [] for k in ("sync", "tensor", "vector", "scalar", "gpsimd")}
        self.first = True

    def do(self, engine, fn, dma=False):
        gd = dma and engine == "gpsimd"
        inc = 16 if dma else 1
        self.ops[engine].append((self.cnt, self.gd_cnt, inc, gd, fn))
        if gd:
            self.gd_cnt += inc
        else:
            self.cnt += inc

    def emit_block(self, nc, block, tail_done=None):
        for eng_name in ("sync", "tensor", "vector", "scalar", "gpsimd"):
            ops = self.ops[eng_name]
            if not ops and not (tail_done and eng_name == "gpsimd"):
                continue

            def body(eng, ops=ops, eng_name=eng_name):
                for wait, gdwait, inc, gd, fn in ops:
                    if wait == 0 and gdwait == 0 and self.prev is not None:
                        eng.wait_ge(self.prev.sem, self.prev.cnt)
                        if getattr(self.prev, "gd_cnt", 0) > 0:
                            eng.wait_ge(self.prev.gd_sem, self.prev.gd_cnt)
                    else:
                        if wait > 0:
                            eng.wait_ge(self.sem, wait)
                        if gdwait > 0:
                            eng.wait_ge(self.gd_sem, gdwait)
                    fn(eng).then_inc(self.gd_sem if gd else self.sem, inc)
                if tail_done and eng_name == "gpsimd":
                    eng.wait_ge(self.sem, self.cnt)
                    if self.gd_cnt > 0:
                        eng.wait_ge(self.gd_sem, self.gd_cnt)

            getattr(block, eng_name)(body)


def _merge(phs):
    return phs


def _host_consts(S, core):
    cap = S * W // NE
    capT = cap // 128
    SP1 = S + 1
    e0 = EPC * core
    u = np.arange(C)[:, None]
    t = np.arange(C)[None, :]
    maskT = np.where(u < t, LR * (DECAY ** (t - u - 1.0)), 0.0).astype(np.float32)
    qdec = np.broadcast_to(DECAY ** np.arange(C, dtype=np.float64),
                           (128, C)).astype(np.float32)
    kw_col = (LR * DECAY ** (C - 1.0 - np.arange(C)))[:, None].astype(np.float32)
    ident = np.eye(128, dtype=np.float32)
    ltriuT = np.triu(np.ones((128, 128), np.float32), 1)
    iota16 = np.broadcast_to(np.arange(NE, dtype=np.float32), (128, NE)).copy()
    s2d = (np.arange(capT)[None, :] * 128 + np.arange(128)[:, None]).astype(np.float32)

    bp = np.arange(W)[:, None, None, None]
    ep = np.arange(NE)[None, :, None, None]
    bb = np.arange(W + 1)[None, None, :, None]
    jj = np.arange(EPC)[None, None, None, :]
    g18 = ((ep == e0 + jj) & (bp < bb)).astype(np.float32).reshape(128, (W + 1) * EPC)
    dm = (((bp == bb) & (ep < e0 + jj)).astype(np.float32)
          - ((bp < bb) & (ep == e0 + jj)).astype(np.float32))
    dm16 = dm.reshape(128, (W + 1) * EPC)[:, :W * EPC]
    rowb16 = (S * np.arange(W)[:, None] + 0 * np.arange(EPC)[None, :]
              ).astype(np.float32).reshape(1, W * EPC)
    eo = np.arange(NE)[None, None, :]
    mgown = ((np.arange(NE)[None, :, None] == eo)
             & (np.arange(W)[:, None, None] < core)).astype(np.float32).reshape(128, NE)
    return dict(maskT=maskT, qdec=qdec, kw_col=kw_col, ident=ident, ltriuT=ltriuT,
                iota16=iota16, s2d=s2d, g18mat=g18, dm16mat=dm16, rowb16=rowb16,
                mgownmat=mgown)


def _build(S):
    cap = S * W // NE
    capT = cap // 128
    SP1 = S + 1
    NT = S // 128
    NSC = S // 512
    NCH = S // C
    NST = (EPC * cap) // 512
    STPE = cap // 512

    nc = bass.Bass("TRN2", target_bir_lowering=False)

    ap_in = {}
    for name, shape in [
        ("x", [S, D]), ("WkT", [D, M]), ("WvT", [D, M]), ("WqT", [D, M]),
        ("bk", [M]), ("bv", [M]), ("bq", [M]), ("WoT", [M, D]), ("bo", [D]),
        ("Wg", [D, NE]), ("bg", [NE]),
        ("W1p", [EPC, D, H]), ("b1pT", [EPC, 128, H // 128]),
        ("W2p", [EPC, H, D]), ("b2p", [EPC, D]),
        ("maskT", [C, C]), ("qdec", [128, C]), ("kw_col", [C, 1]),
        ("ident", [128, 128]), ("ltriuT", [128, 128]), ("iota16", [128, NE]),
        ("s2d", [128, capT]), ("g18mat", [128, (W + 1) * EPC]),
        ("dm16mat", [128, W * EPC]), ("rowb16", [1, W * EPC]),
        ("mgownmat", [128, NE]),
    ]:
        ap_in[name] = nc.dram_tensor(name, shape, F32, kind="ExternalInput").ap()

    out_d = nc.dram_tensor("out", [S, D], F32, kind="ExternalOutput").ap()
    xn_d = nc.dram_tensor("xn_d", [S, D], F32).ap()
    attn_d = nc.dram_tensor("attn_d", [S, D], F32).ap()
    ag1_in = nc.dram_tensor("ag1_in", [S, D], F32)
    ag1_out = nc.dram_tensor("ag1_out", [W * S, D], F32)
    ag1c_in = nc.dram_tensor("ag1c_in", [1, NE], F32)
    ag1c_out = nc.dram_tensor("ag1c_out", [W, NE], F32)
    ag2_in = nc.dram_tensor("ag2_in", [EPC * cap, D], F32)
    ag2_out = nc.dram_tensor("ag2_out", [W * EPC * cap, D], F32)

    ctx = contextlib.ExitStack()
    with ctx:
        sb = lambda n, s, dt=F32: ctx.enter_context(nc.sbuf_tensor(n, s, dt))
        sem = lambda n: ctx.enter_context(nc.semaphore(n))

        WkT_sb = sb("WkT_sb", [128, 8 * M])
        WvT_sb = sb("WvT_sb", [128, 8 * M])
        WqT_sb = sb("WqT_sb", [128, 8 * M])
        WoT_sb = sb("WoT_sb", [128, D])
        Wg_sb = sb("Wg_sb", [128, 8 * NE])
        bk_r = sb("bk_r", [1, M]); bv_r = sb("bv_r", [1, M]); bq_r = sb("bq_r", [1, M])
        bo_r = sb("bo_r", [1, D]); bg_r = sb("bg_r", [1, NE])
        b1T_sb = sb("b1T_sb", [128, EPC * (H // 128)])
        b2_r = sb("b2_r", [1, EPC * D])
        maskT_sb = sb("maskT_sb", [C, C])
        qdec_sb = sb("qdec_sb", [128, C])
        kw_sb = sb("kw_sb", [C, 1])
        id_sb = sb("id_sb", [128, 128])
        ltr_sb = sb("ltr_sb", [128, 128])
        iota_sb = sb("iota_sb", [128, NE])
        s2d_sb = sb("s2d_sb", [128, capT])
        g18_sb = sb("g18_sb", [128, (W + 1) * EPC])
        dm16_sb = sb("dm16_sb", [128, W * EPC])
        rowb_sb = sb("rowb_sb", [1, W * EPC])
        mgo_sb = sb("mgo_sb", [128, NE])
        ones_r = sb("ones_r", [1, 512])
        ones_c = sb("ones_c", [128, 1])
        idx_st = sb("idx_st", [128, NT])
        gate_st = sb("gate_st", [128, NT])
        rank_st = sb("rank_st", [128, NT])
        runb = sb("runb", [1, NE])
        loff_own = sb("loff_own", [1, NE])
        zer16 = sb("zer16", [1, NE])
        counts_c = sb("counts_c", [128, 1])
        g18row = sb("g18row", [1, (W + 1) * EPC])
        d16row = sb("d16row", [1, W * EPC])
        gown_bc = sb("gown_bc", [128, NE])
        g18bc = sb("g18bc", [128, (W + 1) * EPC])
        d16bc = sb("d16bc", [128, W * EPC])
        src_i = sb("src_i", [128, EPC * capT], I32)

        s_p1 = sem("s_p1"); s_p2 = sem("s_p2")
        s_p3 = sem("s_p3"); s_p5 = sem("s_p5")
        s_p1g = sem("s_p1g"); s_p2g = sem("s_p2g")
        s_p3g = sem("s_p3g"); s_p5g = sem("s_p5g")
        gsem = [sem("gs0"), sem("gs1")]
        w1s = [sem("w1s0"), sem("w1s1")]
        w2s = [sem("w2s0"), sem("w2s1")]
        osem = [sem("os0"), sem("os1")]
        mm3 = sem("mm3"); rl3 = sem("rl3"); ev3 = sem("ev3")
        agd = sem("agd")
        v5 = sem("v5")
        pe1 = sem("pe1"); a1 = sem("a1"); v1 = sem("v1")
        xls = [sem("xls0"), sem("xls1")]
        xss = [sem("xss0"), sem("xss1")]
        l5sem = [sem("l5a"), sem("l5b")]
        g5sem = [sem("g5a"), sem("g5b")]
        o5sem = [sem("o5a"), sem("o5b")]

        ph1 = Phase(s_p1, s_p1g)
        ph2 = Phase(s_p2, s_p2g, ph1)
        ph3 = Phase(s_p3, s_p3g, ph2)
        import types
        ph4d = types.SimpleNamespace(sem=agd, cnt=1, gd_cnt=0, gd_sem=None)
        ph5 = Phase(s_p5, s_p5g, ph4d)

        with contextlib.ExitStack() as c1:
            sb1 = lambda n, s, dt=F32: c1.enter_context(nc.sbuf_tensor(n, s, dt))
            K_sb = sb1("K_sb", [128, S])
            Q_sb = sb1("Q_sb", [128, S])
            Kt_sb = sb1("Kt_sb", [128, S])
            Vt_sb = sb1("Vt_sb", [128, S])

            with contextlib.ExitStack() as c1a:
                sb1a = lambda n, s, dt=F32: c1a.enter_context(nc.sbuf_tensor(n, s, dt))
                ps1 = lambda n, s: c1a.enter_context(nc.psum_tensor(n, s, F32))
                xt = [sb1a("xtA", [128, D]), sb1a("xtB", [128, D])]
                sq = sb1a("sq", [128, D])
                xn_tm = [sb1a("xn_tmA", [128, D]), sb1a("xn_tmB", [128, D])]
                xn_fm = [sb1a("xn_fmA", [128, 8 * 512])] * 2
                lcol = [sb1a("lcolA", [128, 16]), sb1a("lcolB", [128, 16])]
                t8s = sb1a("t8s", [128, 16])
                logit_sb = sb1a("logit_sb", [128, NE])
                oneh = sb1a("oneh", [128, NE])
                posf = sb1a("posf", [128, NE])
                ptr = [ps1("ptrA", [128, 128]), ps1("ptrB", [128, 128])]
                prt = ps1("prt", [128, 512])
                pq = ps1("pq", [128, 512])
                pk = ps1("pk", [128, 512])
                pkt = ps1("pkt", [128, 128])
                pvt = ps1("pvt", [128, 128])

                P = ph1
                for wT, dst in [(ap_in["WkT"], WkT_sb), (ap_in["WvT"], WvT_sb),
                                (ap_in["WqT"], WqT_sb)]:
                    P.do("sync", lambda e, wT=wT, dst=dst: e.dma_start(
                        out=dst[:].rearrange("p (c m) -> p c m", c=8),
                        in_=wT.rearrange("(c p) m -> p c m", p=128)), dma=True)
                P.do("sync", lambda e: e.dma_start(out=WoT_sb[:], in_=ap_in["WoT"]), dma=True)
                P.do("sync", lambda e: e.dma_start(
                    out=Wg_sb[:].rearrange("p (c m) -> p c m", c=8),
                    in_=ap_in["Wg"].rearrange("(c p) m -> p c m", p=128)), dma=True)
                for src, dst in [("bk", bk_r), ("bv", bv_r), ("bq", bq_r),
                                 ("bo", bo_r), ("bg", bg_r)]:
                    P.do("sync", lambda e, src=src, dst=dst: e.dma_start(
                        out=dst[:], in_=ap_in[src][None, :]), dma=True)
                for a in range(EPC):
                    P.do("sync", lambda e, a=a: e.dma_start(
                        out=b1T_sb[:, (H // 128) * a:(H // 128) * (a + 1)],
                        in_=ap_in["b1pT"][a]), dma=True)
                    P.do("sync", lambda e, a=a: e.dma_start(
                        out=b2_r[:, D * a:D * (a + 1)],
                        in_=ap_in["b2p"][a][None, :]), dma=True)
                for src, dst in [("maskT", maskT_sb), ("qdec", qdec_sb), ("kw_col", kw_sb),
                                 ("ident", id_sb), ("ltriuT", ltr_sb), ("iota16", iota_sb),
                                 ("s2d", s2d_sb), ("g18mat", g18_sb), ("dm16mat", dm16_sb),
                                 ("rowb16", rowb_sb), ("mgownmat", mgo_sb)]:
                    P.do("sync", lambda e, src=src, dst=dst: e.dma_start(
                        out=dst[:], in_=ap_in[src][:, :]), dma=True)
                P.do("vector", lambda e: e.memset(ones_r[:], 1.0))
                P.do("vector", lambda e: e.memset(ones_c[:], 1.0))
                P.do("vector", lambda e: e.memset(zer16[:], 0.0))
                P.do("vector", lambda e: e.memset(runb[:], 0.0))

                pe_ops = []; ac_ops = []; dv_ops = []; sy_ops = []
                pe_c = [0]; a_c = [0]; v_c = [0]
                tr_ev = {}; evac_ev = {}; fm_ev = {}
                pl_ev = {}; pp_ev = {}; pc_ev = {}
                pq_ev = {}; pk_ev = {}; pkt_ev = {}; pvt_ev = {}
                xn_ev = {}; logit_ev = {}; oneh_ev = {}; posf_ev = {}
                runb_ev = {}; isge_ev = {}; negmean_ev = {}; recip_ev = {}
                negmax_ev = {}; exp_ev = {}; actxt_ev = {}; dvext_ev = {}
                qevac_ev = {}; proj_done = {}

                def pe(fn, waits=(), ev=False):
                    inc = None
                    if ev:
                        pe_c[0] += 1
                        inc = (pe1, 1)
                    pe_ops.append((list(waits), fn, inc))
                    return pe_c[0]

                def ac(fn, waits=()):
                    w = list(waits)
                    if a_c[0] > 0:
                        w.append((a1, a_c[0]))
                    a_c[0] += 1
                    ac_ops.append((w, fn, (a1, 1)))
                    return a_c[0]

                def dv(fn, waits=()):
                    w = list(waits)
                    if v_c[0] > 0:
                        w.append((v1, v_c[0]))
                    v_c[0] += 1
                    dv_ops.append((w, fn, (v1, 1)))
                    return v_c[0]

                for sc in range(NSC):
                    a2 = sc % 2
                    XNF = xn_fm[a2]
                    for it in range(4):
                        ti = 4 * sc + it
                        a = ti % 2
                        XT, XN, LC = xt[a], xn_tm[a], lcol[a]
                        lw = []
                        if ti >= 2:
                            lw.append((xls[a], 16 * (ti // 2)))
                            lw.append((a1, actxt_ev[ti - 2]))
                            lw.append((v1, dvext_ev[ti - 2]))
                        sy_ops.append((lw, (lambda e, ti=ti, XT=XT: e.dma_start(
                            out=XT[:], in_=ap_in["x"][128 * ti:128 * (ti + 1), :])),
                            (xls[a], 16)))
                        aw = [(xls[a], 16 * (ti // 2 + 1))]
                        if ti >= 2:
                            aw.append((v1, isge_ev[ti - 2]))
                        actxt_ev[ti] = ac(lambda e, XT=XT, LC=LC: e.activation(
                            sq[:], XT[:], AF.Copy, accum_out=LC[:, 0:1]), aw)
                        ac(lambda e, XT=XT, LC=LC: e.activation(
                            sq[:], XT[:], AF.Square, accum_out=LC[:, 2:3]))
                        negmean_ev[ti] = dv(lambda e, LC=LC: e.tensor_scalar_mul(
                            LC[:, 1:2], LC[:, 0:1], -1.0 / D), [(a1, actxt_ev[ti])])
                        dv(lambda e, LC=LC: e.tensor_scalar_mul(
                            LC[:, 3:4], LC[:, 2:3], 1.0 / D), [(a1, actxt_ev[ti] + 1)])
                        msq = ac(lambda e, LC=LC: e.activation(
                            LC[:, 4:5], LC[:, 1:2], AF.Square), [(v1, negmean_ev[ti])])
                        dv(lambda e, LC=LC: e.tensor_sub(
                            LC[:, 5:6], LC[:, 3:4], LC[:, 4:5]), [(a1, msq)])
                        dv(lambda e, LC=LC: e.tensor_scalar_add(
                            LC[:, 5:6], LC[:, 5:6], 1e-5))
                        recip_ev[ti] = dv(lambda e, LC=LC: e.reciprocal(
                            LC[:, 6:7], LC[:, 5:6]))
                        srt = ac(lambda e, LC=LC: e.activation(
                            LC[:, 7:8], LC[:, 6:7], AF.Sqrt), [(v1, recip_ev[ti])])
                        xw = [(a1, srt)]
                        if ti >= 2:
                            xw.append((pe1, tr_ev[(ti - 2) * 8 + 7]))
                            xw.append((xss[a], 16 * (ti // 2)))
                        xn_ev[ti] = dv(lambda e, XT=XT, XN=XN, LC=LC: e.tensor_scalar(
                            XN[:], XT[:], LC[:, 1:2], LC[:, 7:8], ALU.add, ALU.mult), xw)
                        dvext_ev[ti] = xn_ev[ti]
                        sw = [(v1, xn_ev[ti])]
                        if ti >= 2:
                            sw.append((xss[a], 16 * (ti // 2)))
                        sy_ops.append((sw, (lambda e, ti=ti, XN=XN: e.dma_start(
                            out=xn_d[128 * ti:128 * (ti + 1), :], in_=XN[:])),
                            (xss[a], 16)))
                        for dc in range(8):
                            n = ti * 8 + dc
                            q = n % 2
                            tw = []
                            if dc == 0:
                                tw.append((v1, xn_ev[ti]))
                            if n >= 2:
                                tw.append((a1, evac_ev[n - 2]))
                            tr_ev[n] = pe(lambda e, dc=dc, XN=XN, q=q: e.transpose(
                                ptr[q][:], XN[:, 128 * dc:128 * (dc + 1)], id_sb[:]),
                                tw, ev=True)
                            ew = [(pe1, tr_ev[n])]
                            if sc >= 1 and dc == 0 and it == 0:
                                ew.append((pe1, proj_done[sc - 1]))
                            evac_ev[n] = ac(lambda e, dc=dc, it=it, q=q, XNF=XNF: e.copy(
                                XNF[:, 512 * dc + 128 * it:512 * dc + 128 * (it + 1)],
                                ptr[q][:]), ew)
                        fm_ev[ti] = a_c[0]
                        rw = [(a1, fm_ev[ti])]
                        if ti >= 1:
                            rw.append((v1, runb_ev[ti - 1]))
                        for dc in range(8):
                            pe(lambda e, dc=dc, it=it, XNF=XNF: e.matmul(
                                prt[:, 0:NE],
                                XNF[:, 512 * dc + 128 * it:512 * dc + 128 * (it + 1)],
                                Wg_sb[:, NE * dc:NE * (dc + 1)],
                                start=(dc == 0), stop=False), rw if dc == 0 else [])
                        pl_ev[ti] = pe(lambda e: e.matmul(
                            prt[:, 0:NE], ones_r[:, 0:128], bg_r[:],
                            start=False, stop=True), [], ev=True)
                        logit_ev[ti] = dv(lambda e: e.tensor_copy(
                            logit_sb[:], prt[:, 0:NE]), [(pe1, pl_ev[ti])])
                        dv(lambda e: e.tensor_max(
                            t8s[:, 0:8], logit_sb[:, 0:8], logit_sb[:, 8:16]))
                        dv(lambda e: e.tensor_max(
                            t8s[:, 8:12], t8s[:, 0:4], t8s[:, 4:8]))
                        dv(lambda e: e.tensor_max(
                            t8s[:, 12:14], t8s[:, 8:10], t8s[:, 10:12]))
                        dv(lambda e, LC=LC: e.tensor_max(
                            LC[:, 12:13], t8s[:, 12:13], t8s[:, 13:14]))
                        negmax_ev[ti] = dv(lambda e, LC=LC: e.tensor_scalar_mul(
                            LC[:, 8:9], LC[:, 12:13], -1.0))
                        exp_ev[ti] = ac(lambda e, LC=LC: e.activation(
                            sq[:, 0:NE], logit_sb[:], AF.Exp,
                            bias=LC[:, 8:9], accum_out=LC[:, 9:10]),
                            [(v1, negmax_ev[ti])])
                        dv(lambda e, ti=ti, LC=LC: e.reciprocal(
                            gate_st[:, ti:ti + 1], LC[:, 9:10]), [(a1, exp_ev[ti])])
                        isge_ev[ti] = dv(lambda e, LC=LC: e.tensor_scalar(
                            oneh[:], logit_sb[:], LC[:, 12:13], None, ALU.is_ge))
                        dv(lambda e: e.tensor_mul(oneh[:], oneh[:], iota_sb[:]))
                        dv(lambda e: e.tensor_add(
                            oneh[:, 0:8], oneh[:, 0:8], oneh[:, 8:16]))
                        dv(lambda e: e.tensor_add(
                            oneh[:, 0:4], oneh[:, 0:4], oneh[:, 4:8]))
                        dv(lambda e: e.tensor_add(
                            oneh[:, 0:2], oneh[:, 0:2], oneh[:, 2:4]))
                        dv(lambda e, ti=ti: e.tensor_add(
                            idx_st[:, ti:ti + 1], oneh[:, 0:1], oneh[:, 1:2]))
                        oneh_ev[ti] = dv(lambda e, ti=ti: e.tensor_scalar(
                            oneh[:], iota_sb[:], idx_st[:, ti:ti + 1], None, ALU.is_equal))
                        pw = [(v1, oneh_ev[ti]), (v1, logit_ev[ti])]
                        pe(lambda e: e.matmul(prt[:, 128:128 + NE], ltr_sb[:], oneh[:],
                                              start=True, stop=False), pw)
                        pp_ev[ti] = pe(lambda e: e.matmul(
                            prt[:, 128:128 + NE], ones_r[:, 0:128], runb[:],
                            start=False, stop=True), [], ev=True)
                        posf_ev[ti] = dv(lambda e: e.tensor_mul(
                            posf[:], oneh[:], prt[:, 128:128 + NE]), [(pe1, pp_ev[ti])])
                        dv(lambda e: e.tensor_add(
                            posf[:, 0:8], posf[:, 0:8], posf[:, 8:16]))
                        dv(lambda e: e.tensor_add(
                            posf[:, 0:4], posf[:, 0:4], posf[:, 4:8]))
                        dv(lambda e: e.tensor_add(
                            posf[:, 0:2], posf[:, 0:2], posf[:, 2:4]))
                        dv(lambda e, ti=ti: e.tensor_add(
                            rank_st[:, ti:ti + 1], posf[:, 0:1], posf[:, 1:2]))
                        pc_ev[ti] = pe(lambda e: e.matmul(
                            prt[0:1, 256:256 + NE], ones_c[:], oneh[:],
                            start=True, stop=True), [(v1, posf_ev[ti])], ev=True)
                        runb_ev[ti] = dv(lambda e: e.tensor_add(
                            runb[:], runb[:], prt[0:1, 256:256 + NE]),
                            [(pe1, pc_ev[ti])])
                    t0 = 512 * sc
                    qw = [(a1, fm_ev[4 * sc + 3])]
                    if sc >= 1:
                        qw.append((v1, qevac_ev[sc - 1]))
                    for dst_ps, wT in [(pq, WqT_sb), (pk, WkT_sb)]:
                        for dc in range(8):
                            pe(lambda e, dc=dc, dst_ps=dst_ps, wT=wT, XNF=XNF: e.matmul(
                                dst_ps[:], wT[:, M * dc:M * (dc + 1)],
                                XNF[:, 512 * dc:512 * (dc + 1)],
                                start=(dc == 0), stop=False),
                                qw if (dc == 0 and dst_ps is pq) else [])
                        b_r = bq_r if dst_ps is pq else bk_r
                        ev = pe(lambda e, dst_ps=dst_ps, b_r=b_r: e.matmul(
                            dst_ps[:], b_r[:], ones_r[:, 0:512], start=False, stop=True),
                            [], ev=True)
                        if dst_ps is pq:
                            pq_ev[sc] = ev
                        else:
                            pk_ev[sc] = ev
                    dv(lambda e, t0=t0: e.tensor_copy(Q_sb[:, t0:t0 + 512], pq[:]),
                       [(pe1, pq_ev[sc])])
                    qevac_ev[sc] = dv(lambda e, t0=t0: e.tensor_copy(
                        K_sb[:, t0:t0 + 512], pk[:]), [(pe1, pk_ev[sc])])
                    for it in range(4):
                        tt = 512 * sc + 128 * it
                        for dst_ps, wT in [(pkt, WkT_sb), (pvt, WvT_sb)]:
                            kw2 = []
                            if sc >= 1 or it >= 1:
                                pidx = 4 * sc + it - 1
                                kw2 = [(v1, pkt_ev.get(("e", pidx), 0))] if ("e", pidx) in pkt_ev else []
                            for dc in range(8):
                                pe(lambda e, dc=dc, it=it, dst_ps=dst_ps, wT=wT, XNF=XNF: e.matmul(
                                    dst_ps[:],
                                    XNF[:, 512 * dc + 128 * it:512 * dc + 128 * (it + 1)],
                                    wT[:, M * dc:M * (dc + 1)],
                                    start=(dc == 0), stop=False),
                                    kw2 if dc == 0 else [])
                            b_r = bk_r if dst_ps is pkt else bv_r
                            ev = pe(lambda e, dst_ps=dst_ps, b_r=b_r: e.matmul(
                                dst_ps[:], ones_r[:, 0:128], b_r[:],
                                start=False, stop=True), [], ev=True)
                            if dst_ps is pkt:
                                pkt_ev[4 * sc + it] = ev
                            else:
                                pvt_ev[4 * sc + it] = ev
                        dv(lambda e, tt=tt: e.tensor_scalar(
                            Kt_sb[:, tt:tt + 128], pkt[:], kw_sb[:], None, ALU.mult),
                           [(pe1, pkt_ev[4 * sc + it])])
                        pkt_ev[("e", 4 * sc + it)] = v_c[0] + 1
                        dv(lambda e, tt=tt: e.tensor_copy(Vt_sb[:, tt:tt + 128], pvt[:]),
                           [(pe1, pvt_ev[4 * sc + it])])
                    proj_done[sc] = pe_c[0]

                with nc.Block("seg1") as block:
                    ph1.emit_block(nc, block)
                    start1 = (s_p1, ph1.cnt)

                    def cb1(ops):
                        def body(eng):
                            eng.wait_ge(start1[0], start1[1])
                            for waits, fn, inc in ops:
                                for sm, vv in waits:
                                    eng.wait_ge(sm, vv)
                                inst = fn(eng)
                                if inc is not None:
                                    inst.then_inc(inc[0], inc[1])
                        return body

                    block.tensor(cb1(pe_ops))
                    block.scalar(cb1(ac_ops))
                    block.vector(cb1(dv_ops))
                    sy2 = []
                    li = 0
                    for k2 in range(0, len(sy_ops), 2):
                        sy2.append(sy_ops[k2])
                        sy2.append(sy_ops[k2 + 1])
                    block.sync(cb1(sy2))

                    def join(eng):
                        eng.wait_ge(pe1, pe_c[0])
                        eng.wait_ge(a1, a_c[0])
                        eng.wait_ge(v1, v_c[0])
                        eng.wait_ge(xss[0], 16 * (NT - NT // 2))
                        eng.wait_ge(xss[1], 16 * (NT // 2))
                        eng.nop().then_inc(s_p1, 1)
                    block.gpsimd(join)
                ph1.cnt += 1


            with contextlib.ExitStack() as c2:
                sb2 = lambda n, s, dt=F32: c2.enter_context(nc.sbuf_tensor(n, s, dt))
                ps2 = lambda n, s: c2.enter_context(nc.psum_tensor(n, s, F32))
                T_sb = sb2("T_sb", [128, 128])
                Qd = sb2("Qd", [128, C])
                Am = sb2("Am", [C, C])
                oneh2 = sb2("oneh2", [128, NE])
                posf2 = sb2("posf2", [128, NE])
                dest_f = sb2("dest_f", [128, 1])
                dest_i = sb2("dest_i", [128, 1], I32)
                lobc = sb2("lobc", [128, NE])
                xrow = sb2("xrow", [128, D])
                arow = sb2("arow", [128, D])
                pA = ps2("pA", [C, C])
                pO = ps2("pO", [128, C])
                pT = ps2("pT", [128, 128])
                pao = ps2("pao", [128, 512])
                plo = ps2("plo", [128, NE])

                P = ph2
                P.do("vector", lambda e: e.tensor_tensor_scan(
                    loff_own[:], runb[:], zer16[:], 0.0, ALU.add, ALU.add))
                P.do("vector", lambda e: e.tensor_sub(loff_own[:], loff_own[:], runb[:]))
                P.do("tensor", lambda e: e.matmul(plo[:], ones_r[:, 0:128], loff_own[:],
                                                  start=True, stop=True))
                P.do("vector", lambda e: e.tensor_copy(lobc[:], plo[:]))
                P.do("sync", lambda e: e.dma_start(
                    out=ag1c_in.ap()[:, :], in_=runb[:]), dma=True)
                for ti in range(NT):
                    P.do("sync", lambda e, ti=ti: e.dma_start(
                        out=xrow[:], in_=xn_d[128 * ti:128 * (ti + 1), :]), dma=True)
                    P.do("vector", lambda e, ti=ti: e.tensor_scalar(
                        oneh2[:], iota_sb[:], idx_st[:, ti:ti + 1], None, ALU.is_equal))
                    P.do("vector", lambda e: e.tensor_mul(posf2[:], oneh2[:], lobc[:]))
                    P.do("vector", lambda e: e.reduce_sum(dest_f[:], posf2[:], axis=AX.X))
                    P.do("vector", lambda e, ti=ti: e.tensor_add(
                        dest_f[:], dest_f[:], rank_st[:, ti:ti + 1]))
                    P.do("vector", lambda e: e.tensor_copy(dest_i[:], dest_f[:]))
                    P.do("gpsimd", lambda e: e.indirect_dma_start(
                        ag1_in.ap()[:, :], IndirectOffsetOnAxis(dest_i[:], 0),
                        xrow[:], None), dma=True)
                P.do("vector", lambda e: e.memset(T_sb[:], 0.0))
                for ch in range(NCH):
                    t0 = C * ch
                    P.do("vector", lambda e, t0=t0: e.tensor_mul(
                        Qd[:], Q_sb[:, t0:t0 + C], qdec_sb[:]))
                    P.do("tensor", lambda e, t0=t0: e.matmul(
                        pA[:], K_sb[:, t0:t0 + C], Q_sb[:, t0:t0 + C], start=True, stop=True))
                    P.do("vector", lambda e: e.tensor_mul(Am[:], pA[:], maskT_sb[:]))
                    P.do("tensor", lambda e: e.matmul(pO[:], T_sb[:], Qd[:],
                                                      start=True, stop=False))
                    P.do("tensor", lambda e, t0=t0: e.matmul(
                        pO[:], Vt_sb[:, t0:t0 + C], Am[:], start=False, stop=True))
                    P.do("tensor", lambda e, t0=t0: e.matmul(
                        pT[:], Kt_sb[:, t0:t0 + C], Vt_sb[:, t0:t0 + C], start=True, stop=True))
                    P.do("vector", lambda e: e.scalar_tensor_tensor(
                        T_sb[:], T_sb[:], DECAY ** C, pT[:], ALU.mult, ALU.add))
                    P.do("vector", lambda e, t0=t0: e.tensor_copy(K_sb[:, t0:t0 + C], pO[:]))
                for ti in range(NT):
                    t0 = 128 * ti
                    for dh in range(2):
                        P.do("tensor", lambda e, t0=t0, dh=dh: e.matmul(
                            pao[:], K_sb[:, t0:t0 + 128], WoT_sb[:, 512 * dh:512 * (dh + 1)],
                            start=True, stop=False))
                        P.do("tensor", lambda e, dh=dh: e.matmul(
                            pao[:], ones_r[:, 0:128], bo_r[:, 512 * dh:512 * (dh + 1)],
                            start=False, stop=True))
                        P.do("vector", lambda e, dh=dh: e.tensor_copy(
                            arow[:, 512 * dh:512 * (dh + 1)], pao[:]))
                    P.do("sync", lambda e, t0=t0: e.dma_start(
                        out=attn_d[t0:t0 + 128, :], in_=arow[:]), dma=True)

                P.do("gpsimd", lambda e: e.collective_compute(
                    "AllGather", ALU.bypass, replica_groups=[list(range(W))],
                    ins=[ag1c_in.ap().opt()], outs=[ag1c_out.ap().opt()]))
                P.do("gpsimd", lambda e: e.collective_compute(
                    "AllGather", ALU.bypass, replica_groups=[list(range(W))],
                    ins=[ag1_in.ap().opt()], outs=[ag1_out.ap().opt()]))
                with nc.Block("seg2") as block:
                    ph2.emit_block(nc, block)

        with contextlib.ExitStack() as c3:
            sb3 = lambda n, s, dt=F32: c3.enter_context(nc.sbuf_tensor(n, s, dt))
            ps3 = lambda n, s: c3.enter_context(nc.psum_tensor(n, s, F32))
            xg_tm = [sb3("xg_tmA", [128, D]), sb3("xg_tmB", [128, D])]
            xg_fm = sb3("xg_fm", [128, 8 * 512])
            h_sb = sb3("h_sb", [128, 32 * 512])
            w1_sb = [sb3("w1_sbA", [128, 8 * 128]), sb3("w1_sbB", [128, 8 * 128])]
            w2_sb = [sb3("w2_sbA", [128, 512]), sb3("w2_sbB", [128, 512])]
            y_sb = [sb3("y_sbA", [128, 512]), sb3("y_sbB", [128, 512])]
            macc = sb3("macc", [128, capT])
            m1 = sb3("m1", [128, capT])
            m2 = sb3("m2", [128, capT])
            mt = sb3("mt", [128, capT])
            ptr3 = ps3("ptr3", [128, 128])
            ptr3b = ps3("ptr3b", [128, 128])
            ph_ps = [ps3("ph_psA", [128, 512]), ps3("ph_psB", [128, 512])]
            py = [ps3(f"py{i}", [128, 512]) for i in range(4)]

            P = ph3
            P.do("sync", lambda e: e.dma_start(
                out=counts_c[:, 0:1],
                in_=ag1c_out.ap().rearrange("(x b) e -> x (b e)", x=1).transpose([1, 0])),
                dma=True)
            P.do("tensor", lambda e: e.matmul(py[0][0:1, 0:(W + 1) * EPC], counts_c[:], g18_sb[:],
                                              start=True, stop=True))
            P.do("vector", lambda e: e.tensor_copy(g18row[:], py[0][0:1, 0:(W + 1) * EPC]))
            P.do("tensor", lambda e: e.matmul(py[1][:, 0:(W + 1) * EPC], ones_r[:, 0:128], g18row[:],
                                              start=True, stop=True))
            P.do("vector", lambda e: e.tensor_copy(g18bc[:], py[1][:, 0:(W + 1) * EPC]))
            P.do("tensor", lambda e: e.matmul(py[0][0:1, 0:W * EPC], counts_c[:], dm16_sb[:],
                                              start=True, stop=False))
            P.do("tensor", lambda e: e.matmul(py[0][0:1, 0:W * EPC], ones_c[0:1, 0:1], rowb_sb[:],
                                              start=False, stop=True))
            P.do("vector", lambda e: e.tensor_copy(d16row[:], py[0][0:1, 0:W * EPC]))
            P.do("tensor", lambda e: e.matmul(py[1][:, 0:W * EPC], ones_r[:, 0:128], d16row[:],
                                              start=True, stop=True))
            P.do("vector", lambda e: e.tensor_copy(d16bc[:], py[1][:, 0:W * EPC]))
            P.do("tensor", lambda e: e.matmul(py[0][0:1, 0:NE], counts_c[:], mgo_sb[:],
                                              start=True, stop=True))
            P.do("vector", lambda e: e.tensor_copy(d16row[:, 0:NE], py[0][0:1, 0:NE]))
            P.do("tensor", lambda e: e.matmul(py[1][:, 0:NE], ones_r[:, 0:128],
                                              d16row[:, 0:NE], start=True, stop=True))
            P.do("vector", lambda e: e.tensor_copy(gown_bc[:], py[1][:, 0:NE]))
            for j in range(EPC):
                P.do("vector", lambda e: e.memset(macc[:], 0.0))
                for b in range(W):
                    P.do("vector", lambda e, b=b, j=j: e.tensor_scalar(
                        m1[:], s2d_sb[:], g18bc[:, EPC * b + j:EPC * b + j + 1],
                        None, ALU.is_ge))
                    P.do("vector", lambda e, b=b, j=j: e.tensor_scalar(
                        m2[:], s2d_sb[:], g18bc[:, EPC * (b + 1) + j:EPC * (b + 1) + j + 1],
                        None, ALU.is_lt))
                    P.do("vector", lambda e: e.tensor_mul(m1[:], m1[:], m2[:]))
                    P.do("vector", lambda e, b=b, j=j: e.tensor_scalar(
                        mt[:], s2d_sb[:], d16bc[:, EPC * b + j:EPC * b + j + 1],
                        None, ALU.add))
                    P.do("vector", lambda e: e.tensor_mul(mt[:], mt[:], m1[:]))
                    P.do("vector", lambda e: e.tensor_add(macc[:], macc[:], mt[:]))
                P.do("vector", lambda e, j=j: e.tensor_copy(
                    src_i[:, capT * j:capT * (j + 1)], macc[:]))
            NG = NST * 4
            NH1 = NST * 32
            NW2 = NST * 64
            NY = NST * 8
            tr_ev = {}
            w1grp_ev = {}

            w2grp_ev = {}
            bias_ev = {}
            evac_ev = {}
            relu_ev = {}
            xg_done = {}
            yev = {}
            py_last = [0, 0, 0, 0]

            pe_ops = []
            sc_ops = []
            ve_ops = []
            sy_ops = []
            gp_ops = []
            mm_c = [0]; rl_c = [0]; ev_c = [0]

            for st in range(NST):
                j = st // STPE
                stj = st % STPE
                for k in range(4):
                    g = st * 4 + k
                    p = g % 2
                    col = capT * j + 4 * stj + k
                    waits = []
                    if g >= 2:
                        waits.append((gsem[p], 16 * (g // 2)))
                        waits.append((mm3, tr_ev[(g - 2) * 8 + 7]))
                    gp_ops.append((waits, (lambda e, col=col, p=p: e.indirect_dma_start(
                        xg_tm[p][:], None, ag1_out.ap()[:, :],
                        IndirectOffsetOnAxis(src_i[:, col:col + 1], 0))),
                        (gsem[p], 16)))
                    for dc in range(8):
                        n = g * 8 + dc
                        q = n % 2
                        PT = ptr3 if q == 0 else ptr3b
                        waits = []
                        if dc == 0:
                            waits.append((gsem[p], 16 * (g // 2 + 1)))
                        if n >= 2:
                            waits.append((rl3, evac_ev[n - 2]))
                        mm_c[0] += 1
                        tr_ev[n] = mm_c[0]
                        pe_ops.append((waits, (lambda e, dc=dc, p=p, PT=PT: e.transpose(
                            PT[:], xg_tm[p][:, 128 * dc:128 * (dc + 1)], id_sb[:])),
                            (mm3, 1)))
                        rl_c[0] += 1
                        evac_ev[n] = rl_c[0]
                        sc_ops.append(([(mm3, tr_ev[n])],
                                       (lambda e, dc=dc, k=k, PT=PT: e.copy(
                                           xg_fm[:, 512 * dc + 128 * k:512 * dc + 128 * (k + 1)],
                                           PT[:])), (rl3, 1)))
                xg_done[st] = rl_c[0]
                for hc in range(32):
                    l = st * 32 + hc
                    p = l % 2
                    waits = [(w1s[p], 16 * (l // 2 + 1))]
                    if hc == 0:
                        waits.append((rl3, xg_done[st]))
                    if l >= 2:
                        waits.append((rl3, relu_ev[l - 2]))
                    for dc in range(8):
                        mmw = waits if dc == 0 else []
                        inc = None
                        if dc == 7:
                            mm_c[0] += 1
                            w1grp_ev[l] = mm_c[0]
                            inc = (mm3, 1)
                        pe_ops.append((mmw, (lambda e, dc=dc, p=p: e.matmul(
                            ph_ps[p][:], w1_sb[p][:, 128 * dc:128 * (dc + 1)],
                            xg_fm[:, 512 * dc:512 * (dc + 1)],
                            start=(dc == 0), stop=(dc == 7))), inc))
                    lw = [(w1s[p], 16 * (l // 2))] if l >= 2 else []
                    if l >= 2:
                        lw.append((mm3, w1grp_ev[l - 2]))
                    sy_ops.append((lw, (lambda e, j=j, hc=hc, p=p: e.dma_start(
                        out=w1_sb[p][:].rearrange("p (c h) -> p c h", c=8),
                        in_=ap_in["W1p"][j, :, 128 * hc:128 * (hc + 1)]
                        .rearrange("(c p) h -> p c h", p=128))), (w1s[p], 16)))
                    rl_c[0] += 1
                    relu_ev[l] = rl_c[0]
                    sc_ops.append(([(mm3, w1grp_ev[l])],
                                   (lambda e, j=j, hc=hc, p=p: e.activation(
                                       h_sb[:, 512 * hc:512 * (hc + 1)], ph_ps[p][:],
                                       AF.Relu,
                                       bias=b1T_sb[:, 32 * j + hc:32 * j + hc + 1])),
                                   (rl3, 1)))
                for dh in range(2):
                    for hc in range(32):
                        mw2 = (st * 2 + dh) * 32 + hc
                        q = mw2 % 2
                        waits = [(w2s[q], 16 * (mw2 // 2 + 1))]
                        if dh == 0:
                            waits.append((rl3, relu_ev[st * 32 + hc]))
                        if hc == 0:
                            pl_ = max(py_last)
                            if pl_ > 0:
                                waits.append((ev3, pl_))
                        for tt in range(4):
                            mmw = waits if tt == 0 else []
                            inc = None
                            if tt == 3:
                                mm_c[0] += 1
                                w2grp_ev[mw2] = mm_c[0]
                                inc = (mm3, 1)
                            pe_ops.append((mmw, (lambda e, hc=hc, tt=tt, q=q: e.matmul(
                                py[tt][:],
                                h_sb[:, 512 * hc + 128 * tt:512 * hc + 128 * (tt + 1)],
                                w2_sb[q][:], start=(hc == 0), stop=False)), inc))
                        lw = [(w2s[q], 16 * (mw2 // 2))] if mw2 >= 2 else []
                        if mw2 >= 2:
                            lw.append((mm3, w2grp_ev[mw2 - 2]))
                        sy_ops.append((lw, (lambda e, j=j, hc=hc, dh=dh, q=q: e.dma_start(
                            out=w2_sb[q][:],
                            in_=ap_in["W2p"][j, 128 * hc:128 * (hc + 1),
                                             512 * dh:512 * (dh + 1)])), (w2s[q], 16)))
                    for tt in range(4):
                        m = (st * 2 + dh) * 4 + tt
                        q = m % 2
                        mm_c[0] += 1
                        bias_ev[m] = mm_c[0]
                        pe_ops.append(([], (lambda e, tt=tt, j=j, dh=dh: e.matmul(
                            py[tt][:], ones_r[:, 0:128],
                            b2_r[:, D * j + 512 * dh:D * j + 512 * (dh + 1)],
                            start=False, stop=True)), (mm3, 1)))
                        ew = [(mm3, bias_ev[m])]
                        if m >= 2:
                            ew.append((osem[q], 16 * (m // 2)))
                        ev_c[0] += 1
                        yev[m] = ev_c[0]
                        py_last[tt] = ev_c[0]
                        ve_ops.append((ew, (lambda e, tt=tt, q=q: e.tensor_copy(
                            y_sb[q][:], py[tt][:])), (ev3, 1)))
                        row0 = cap * j + 512 * stj + 128 * tt
                        sw = [(ev3, yev[m])]
                        if m >= 2:
                            sw.append((osem[q], 16 * (m // 2)))
                        gp_ops.append((sw, (lambda e, row0=row0, dh=dh, q=q: e.dma_start(
                            out=ag2_in.ap()[row0:row0 + 128, 512 * dh:512 * (dh + 1)],
                            in_=y_sb[q][:])), (osem[q], 16)))
            no2 = [NY // 2, NY - NY // 2]
            gp_ops.append(([(osem[0], 16 * no2[0]), (osem[1], 16 * no2[1])],
                           (lambda e: e.collective_compute(
                               "AllGather", ALU.bypass,
                               replica_groups=[list(range(W))],
                               ins=[ag2_in.ap().opt()], outs=[ag2_out.ap().opt()])),
                           (agd, 1)))

            with nc.Block("seg3") as block:
                ph3.emit_block(nc, block)
                start_w = (s_p3, ph3.cnt)

                def chain_body(ops, start=start_w):
                    def body(eng):
                        eng.wait_ge(start[0], start[1])
                        for waits, fn, inc in ops:
                            for sm, v in waits:
                                eng.wait_ge(sm, v)
                            inst = fn(eng)
                            if inc is not None:
                                inst.then_inc(inc[0], inc[1])
                    return body

                block.tensor(chain_body(pe_ops))
                block.scalar(chain_body(sc_ops))
                block.vector(chain_body(ve_ops))
                block.sync(chain_body(sy_ops))
                block.gpsimd(chain_body(gp_ops))


        with contextlib.ExitStack() as c5:
            sb5 = lambda n, s, dt=F32: c5.enter_context(nc.sbuf_tensor(n, s, dt))
            xt5 = [sb5(f"xt5{i}", [128, D]) for i in range(2)]
            at5 = [sb5(f"at5{i}", [128, D]) for i in range(2)]
            yg5 = [sb5(f"yg5{i}", [128, D]) for i in range(2)]
            oneh5 = [sb5(f"oneh5{i}", [128, NE]) for i in range(2)]
            posf5 = [sb5(f"posf5{i}", [128, NE]) for i in range(2)]
            col5 = [sb5(f"col5{i}", [128, 8]) for i in range(2)]
            yri5 = [sb5(f"yri5{i}", [128, 1], I32) for i in range(2)]
            out5 = [sb5(f"out5{i}", [128, D]) for i in range(2)]

            v5_ops = []
            g5_ops = []
            s5_ld = []
            s5_st = []
            v5c = [0]
            yri_ev = {}
            out_ev = {}

            for ti in range(NT):
                t0 = 128 * ti
                a = ti % 2
                OH, PF, CL = oneh5[a], posf5[a], col5[a]
                lw = [(l5sem[a], 32 * (ti // 2))] if ti >= 2 else []
                if ti >= 2:
                    lw.append((v5, out_ev[ti - 2]))
                s5_ld.append((lw, (lambda e, t0=t0, a=a: e.dma_start(
                    out=xt5[a][:], in_=ap_in["x"][t0:t0 + 128, :])), (l5sem[a], 16)))
                s5_ld.append(([], (lambda e, t0=t0, a=a: e.dma_start(
                    out=at5[a][:], in_=attn_d[t0:t0 + 128, :])), (l5sem[a], 16)))

                def v(fn, waits=(), inc=True):
                    w = list(waits)
                    if v5c[0] > 0:
                        w.append((v5, v5c[0]))
                    v5c[0] += 1
                    v5_ops.append((w, fn, (v5, 1)))

                v(lambda e, ti=ti, OH=OH: e.tensor_scalar(
                    OH[:], iota_sb[:], idx_st[:, ti:ti + 1], None, ALU.is_equal))
                v(lambda e, OH=OH, PF=PF: e.tensor_mul(PF[:], OH[:], gown_bc[:]))
                v(lambda e, PF=PF: e.tensor_add(PF[:, 0:8], PF[:, 0:8], PF[:, 8:16]))
                v(lambda e, PF=PF: e.tensor_add(PF[:, 0:4], PF[:, 0:4], PF[:, 4:8]))
                v(lambda e, PF=PF: e.tensor_add(PF[:, 0:2], PF[:, 0:2], PF[:, 2:4]))
                v(lambda e, PF=PF, CL=CL: e.tensor_add(CL[:, 0:1], PF[:, 0:1], PF[:, 1:2]))
                v(lambda e, ti=ti, CL=CL: e.tensor_add(
                    CL[:, 0:1], CL[:, 0:1], rank_st[:, ti:ti + 1]))
                v(lambda e, CL=CL: e.tensor_single_scalar(
                    CL[:, 1:2], CL[:, 0:1], float(cap), ALU.is_lt))
                v(lambda e, ti=ti, CL=CL: e.tensor_mul(
                    CL[:, 2:3], CL[:, 1:2], gate_st[:, ti:ti + 1]))
                v(lambda e, ti=ti, CL=CL: e.tensor_scalar(
                    CL[:, 3:4], idx_st[:, ti:ti + 1], float(cap), None, ALU.mult))
                v(lambda e, CL=CL: e.tensor_add(CL[:, 3:4], CL[:, 3:4], CL[:, 0:1]))
                v(lambda e, CL=CL: e.tensor_single_scalar(
                    CL[:, 3:4], CL[:, 3:4], float(NE * cap - 1), ALU.min))
                yw = [(g5sem[a], 16 * (ti // 2))] if ti >= 2 else []
                v(lambda e, a=a, CL=CL: e.tensor_copy(yri5[a][:], CL[:, 3:4]), waits=yw)
                yri_ev[ti] = v5c[0]
                gw = [(v5, yri_ev[ti])]
                if ti >= 2:
                    gw.append((g5sem[a], 16 * (ti // 2)))
                g5_ops.append((gw, (lambda e, a=a: e.indirect_dma_start(
                    yg5[a][:], None, ag2_out.ap()[:, :],
                    IndirectOffsetOnAxis(yri5[a][:], 0))), (g5sem[a], 16)))
                cw = [(l5sem[a], 32 * (ti // 2 + 1)), (g5sem[a], 16 * (ti // 2 + 1))]
                if ti >= 2:
                    cw.append((o5sem[a], 16 * (ti // 2)))
                v(lambda e, a=a, CL=CL: e.scalar_tensor_tensor(
                    out5[a][:], yg5[a][:], CL[:, 2:3], at5[a][:], ALU.mult, ALU.add),
                  waits=cw)
                v(lambda e, a=a: e.tensor_add(out5[a][:], out5[a][:], xt5[a][:]))
                out_ev[ti] = v5c[0]
                sw = [(v5, out_ev[ti])]
                if ti >= 2:
                    sw.append((o5sem[a], 16 * (ti // 2)))
                s5_st.append((sw, (lambda e, t0=t0, a=a: e.dma_start(
                    out=out_d[t0:t0 + 128, :], in_=out5[a][:])), (o5sem[a], 16)))

            with nc.Block("seg4") as block:
                pw = [(ph4d.sem, ph4d.cnt), (s_p3, ph3.cnt)]
                if ph3.gd_cnt:
                    pw.append((s_p3g, ph3.gd_cnt))

                def cbody(ops, extra_first=pw):
                    def body(eng):
                        for sm, vv in extra_first:
                            eng.wait_ge(sm, vv)
                        for waits, fn, inc in ops:
                            for sm, vv in waits:
                                eng.wait_ge(sm, vv)
                            fn(eng).then_inc(inc[0], inc[1])
                    return body

                block.vector(cbody(v5_ops))
                block.gpsimd(cbody(g5_ops))
                sy5 = []
                for ti in range(NT):
                    sy5.extend(s5_ld[2 * ti:2 * ti + 2])
                    if ti >= 1:
                        sy5.append(s5_st[ti - 1])
                sy5.append(s5_st[NT - 1])
                block.sync(cbody(sy5))

                def tail(eng):
                    eng.wait_ge(o5sem[0], 16 * (NT - NT // 2))
                    eng.wait_ge(o5sem[1], 16 * (NT // 2))
                block.scalar(tail)

    return nc


def _in_maps(S, inputs):
    x = np.asarray(inputs["L0_attn_x"], dtype=np.float32)
    Wk = np.asarray(inputs["L0_attn_Wk"], dtype=np.float32)
    Wv = np.asarray(inputs["L0_attn_Wv"], dtype=np.float32)
    Wq = np.asarray(inputs["L0_attn_Wq"], dtype=np.float32)
    Wo = np.asarray(inputs["L0_attn_Wo"], dtype=np.float32)
    W1 = np.asarray(inputs["L0_ffn_W1"], dtype=np.float32)
    W2 = np.asarray(inputs["L0_ffn_W2"], dtype=np.float32)
    b1 = np.asarray(inputs["L0_ffn_b1"], dtype=np.float32)
    base = {
        "WkT": np.ascontiguousarray(Wk.T), "WvT": np.ascontiguousarray(Wv.T),
        "WqT": np.ascontiguousarray(Wq.T), "WoT": np.ascontiguousarray(Wo.T),
        "bk": np.asarray(inputs["L0_attn_bk"], np.float32),
        "bv": np.asarray(inputs["L0_attn_bv"], np.float32),
        "bq": np.asarray(inputs["L0_attn_bq"], np.float32),
        "bo": np.asarray(inputs["L0_attn_bo"], np.float32),
        "Wg": np.asarray(inputs["L0_ffn_Wg"], np.float32),
        "bg": np.asarray(inputs["L0_ffn_bg"], np.float32),
    }
    maps = []
    for c in range(W):
        m = dict(base)
        m["x"] = np.ascontiguousarray(x[c])
        e0 = EPC * c
        m["W1p"] = np.ascontiguousarray(W1[e0:e0 + EPC])
        m["W2p"] = np.ascontiguousarray(W2[e0:e0 + EPC])
        m["b1pT"] = np.ascontiguousarray(
            b1[e0:e0 + EPC].reshape(EPC, H // 128, 128).transpose(0, 2, 1))
        m["b2p"] = np.ascontiguousarray(
            np.asarray(inputs["L0_ffn_b2"], np.float32)[e0:e0 + EPC])
        m.update(_host_consts(S, c))
        maps.append(m)
    return maps


def _run(S, inputs):
    nc = _build(S)
    maps = _in_maps(S, inputs)
    res = run_bass_kernel_spmd(nc, maps, core_ids=list(range(W)))
    return np.stack([res.results[c]["out"] for c in range(W)], axis=0)


def kernel(**inputs):
    S = np.asarray(inputs["L0_attn_x"]).shape[1]
    return _run(S, inputs)


# version 07 (final)
# speedup vs baseline: 1.9878x; 1.1478x over previous
import contextlib
import numpy as np
import concourse.bass as bass
import concourse.mybir as mybir
from concourse.bass_utils import run_bass_kernel_spmd
from concourse.bass_core_rust import IndirectOffsetOnAxis

F32 = mybir.dt.float32
I32 = mybir.dt.int32
U32 = mybir.dt.uint32
AF = mybir.ActivationFunctionType
ALU = mybir.AluOpType
AX = mybir.AxisListType

DECAY, LR = 0.99, 0.01
D = 1024
M = 128
NE = 16
H = 4096
C = 128
W = 8
EPC = 2


class Phase:

    def __init__(self, sem, gd_sem, prev=None):
        self.sem, self.gd_sem, self.prev = sem, gd_sem, prev
        self.cnt = 0
        self.gd_cnt = 0
        self.ops = {k: [] for k in ("sync", "tensor", "vector", "scalar", "gpsimd")}
        self.first = True

    def do(self, engine, fn, dma=False):
        gd = dma and engine == "gpsimd"
        inc = 16 if dma else 1
        self.ops[engine].append((self.cnt, self.gd_cnt, inc, gd, fn))
        if gd:
            self.gd_cnt += inc
        else:
            self.cnt += inc

    def emit_block(self, nc, block, tail_done=None):
        for eng_name in ("sync", "tensor", "vector", "scalar", "gpsimd"):
            ops = self.ops[eng_name]
            if not ops and not (tail_done and eng_name == "gpsimd"):
                continue

            def body(eng, ops=ops, eng_name=eng_name):
                for wait, gdwait, inc, gd, fn in ops:
                    if wait == 0 and gdwait == 0 and self.prev is not None:
                        eng.wait_ge(self.prev.sem, self.prev.cnt)
                        if getattr(self.prev, "gd_cnt", 0) > 0:
                            eng.wait_ge(self.prev.gd_sem, self.prev.gd_cnt)
                    else:
                        if wait > 0:
                            eng.wait_ge(self.sem, wait)
                        if gdwait > 0:
                            eng.wait_ge(self.gd_sem, gdwait)
                    fn(eng).then_inc(self.gd_sem if gd else self.sem, inc)
                if tail_done and eng_name == "gpsimd":
                    eng.wait_ge(self.sem, self.cnt)
                    if self.gd_cnt > 0:
                        eng.wait_ge(self.gd_sem, self.gd_cnt)

            getattr(block, eng_name)(body)


def _merge(phs):
    return phs


def _host_consts(S, core):
    cap = S * W // NE
    capT = cap // 128
    SP1 = S + 1
    e0 = EPC * core
    u = np.arange(C)[:, None]
    t = np.arange(C)[None, :]
    maskT = np.where(u < t, LR * (DECAY ** (t - u - 1.0)), 0.0).astype(np.float32)
    qdec = np.broadcast_to(DECAY ** np.arange(C, dtype=np.float64),
                           (128, C)).astype(np.float32)
    kw_col = (LR * DECAY ** (C - 1.0 - np.arange(C)))[:, None].astype(np.float32)
    ident = np.eye(128, dtype=np.float32)
    ltriuT = np.triu(np.ones((128, 128), np.float32), 1)
    iota16 = np.broadcast_to(np.arange(NE, dtype=np.float32), (128, NE)).copy()
    s2d = (np.arange(capT)[None, :] * 128 + np.arange(128)[:, None]).astype(np.float32)

    bp = np.arange(W)[:, None, None, None]
    ep = np.arange(NE)[None, :, None, None]
    bb = np.arange(W + 1)[None, None, :, None]
    jj = np.arange(EPC)[None, None, None, :]
    g18 = ((ep == e0 + jj) & (bp < bb)).astype(np.float32).reshape(128, (W + 1) * EPC)
    dm = (((bp == bb) & (ep < e0 + jj)).astype(np.float32)
          - ((bp < bb) & (ep == e0 + jj)).astype(np.float32))
    dm16 = dm.reshape(128, (W + 1) * EPC)[:, :W * EPC]
    rowb16 = (S * np.arange(W)[:, None] + 0 * np.arange(EPC)[None, :]
              ).astype(np.float32).reshape(1, W * EPC)
    eo = np.arange(NE)[None, None, :]
    mgown = ((np.arange(NE)[None, :, None] == eo)
             & (np.arange(W)[:, None, None] < core)).astype(np.float32).reshape(128, NE)
    return dict(maskT=maskT, qdec=qdec, kw_col=kw_col, ident=ident, ltriuT=ltriuT,
                iota16=iota16, s2d=s2d, g18mat=g18, dm16mat=dm16, rowb16=rowb16,
                mgownmat=mgown)


def _build(S):
    cap = S * W // NE
    capT = cap // 128
    SP1 = S + 1
    NT = S // 128
    NSC = S // 512
    NCH = S // C
    NST = (EPC * cap) // 512
    STPE = cap // 512

    nc = bass.Bass("TRN2", target_bir_lowering=False)

    ap_in = {}
    for name, shape in [
        ("x", [S, D]), ("WkT", [D, M]), ("WvT", [D, M]), ("WqT", [D, M]),
        ("bk", [M]), ("bv", [M]), ("bq", [M]), ("WoT", [M, D]), ("bo", [D]),
        ("Wg", [D, NE]), ("bg", [NE]),
        ("W1p", [EPC, D, H]), ("b1pT", [EPC, 128, H // 128]),
        ("W2p", [EPC, H, D]), ("b2p", [EPC, D]),
        ("maskT", [C, C]), ("qdec", [128, C]), ("kw_col", [C, 1]),
        ("ident", [128, 128]), ("ltriuT", [128, 128]), ("iota16", [128, NE]),
        ("s2d", [128, capT]), ("g18mat", [128, (W + 1) * EPC]),
        ("dm16mat", [128, W * EPC]), ("rowb16", [1, W * EPC]),
        ("mgownmat", [128, NE]),
    ]:
        ap_in[name] = nc.dram_tensor(name, shape, F32, kind="ExternalInput").ap()

    out_d = nc.dram_tensor("out", [S, D], F32, kind="ExternalOutput").ap()
    xn_d = nc.dram_tensor("xn_d", [S, D], F32).ap()
    attn_d = nc.dram_tensor("attn_d", [S, D], F32).ap()
    ag1_in = nc.dram_tensor("ag1_in", [S, D], F32)
    ag1_out = nc.dram_tensor("ag1_out", [W * S, D], F32, addr_space="Shared")
    ag1c_in = nc.dram_tensor("ag1c_in", [1, NE], F32)
    ag1c_out = nc.dram_tensor("ag1c_out", [W, NE], F32)
    ag2_in = nc.dram_tensor("ag2_in", [EPC * cap, D], F32)
    ag2_out = nc.dram_tensor("ag2_out", [W * EPC * cap, D], F32, addr_space="Shared")

    ctx = contextlib.ExitStack()
    with ctx:
        sb = lambda n, s, dt=F32: ctx.enter_context(nc.sbuf_tensor(n, s, dt))
        sem = lambda n: ctx.enter_context(nc.semaphore(n))

        WkT_sb = sb("WkT_sb", [128, 8 * M])
        WvT_sb = sb("WvT_sb", [128, 8 * M])
        WqT_sb = sb("WqT_sb", [128, 8 * M])
        WoT_sb = sb("WoT_sb", [128, D])
        Wg_sb = sb("Wg_sb", [128, 8 * NE])
        bk_r = sb("bk_r", [1, M]); bv_r = sb("bv_r", [1, M]); bq_r = sb("bq_r", [1, M])
        bo_r = sb("bo_r", [1, D]); bg_r = sb("bg_r", [1, NE])
        b1T_sb = sb("b1T_sb", [128, EPC * (H // 128)])
        b2_r = sb("b2_r", [1, EPC * D])
        maskT_sb = sb("maskT_sb", [C, C])
        qdec_sb = sb("qdec_sb", [128, C])
        kw_sb = sb("kw_sb", [C, 1])
        id_sb = sb("id_sb", [128, 128])
        ltr_sb = sb("ltr_sb", [128, 128])
        iota_sb = sb("iota_sb", [128, NE])
        s2d_sb = sb("s2d_sb", [128, capT])
        g18_sb = sb("g18_sb", [128, (W + 1) * EPC])
        dm16_sb = sb("dm16_sb", [128, W * EPC])
        rowb_sb = sb("rowb_sb", [1, W * EPC])
        mgo_sb = sb("mgo_sb", [128, NE])
        ones_r = sb("ones_r", [1, 512])
        ones_c = sb("ones_c", [128, 1])
        idx_st = sb("idx_st", [128, NT])
        gate_st = sb("gate_st", [128, NT])
        rank_st = sb("rank_st", [128, NT])
        runb = sb("runb", [1, NE])
        loff_own = sb("loff_own", [1, NE])
        zer16 = sb("zer16", [1, NE])
        counts_c = sb("counts_c", [128, 1])
        g18row = sb("g18row", [1, (W + 1) * EPC])
        d16row = sb("d16row", [1, W * EPC])
        gown_bc = sb("gown_bc", [128, NE])
        g18bc = sb("g18bc", [128, (W + 1) * EPC])
        d16bc = sb("d16bc", [128, W * EPC])
        src_i = sb("src_i", [128, EPC * capT], I32)

        s_p1 = sem("s_p1"); s_p2 = sem("s_p2")
        s_p3 = sem("s_p3"); s_p5 = sem("s_p5")
        s_p1g = sem("s_p1g"); s_p2g = sem("s_p2g")
        s_p3g = sem("s_p3g"); s_p5g = sem("s_p5g")
        gsem = [sem("gs0"), sem("gs1")]
        w1s = [sem("w1s0"), sem("w1s1")]
        w2s = [sem("w2s0"), sem("w2s1")]
        osem = [sem("os0"), sem("os1")]
        mm3 = sem("mm3"); rl3 = sem("rl3"); ev3 = sem("ev3")
        agd = sem("agd")
        v5 = sem("v5")
        pe1 = sem("pe1"); a1 = sem("a1"); v1 = sem("v1")
        xls = [sem("xls0"), sem("xls1")]
        xss = [sem("xss0"), sem("xss1")]
        l5sem = [sem("l5a"), sem("l5b")]
        g5sem = [sem("g5a"), sem("g5b")]
        o5sem = [sem("o5a"), sem("o5b")]

        ph1 = Phase(s_p1, s_p1g)
        ph2 = Phase(s_p2, s_p2g, ph1)
        ph3 = Phase(s_p3, s_p3g, ph2)
        import types
        ph4d = types.SimpleNamespace(sem=agd, cnt=1, gd_cnt=0, gd_sem=None)
        ph5 = Phase(s_p5, s_p5g, ph4d)

        with contextlib.ExitStack() as c1:
            sb1 = lambda n, s, dt=F32: c1.enter_context(nc.sbuf_tensor(n, s, dt))
            K_sb = sb1("K_sb", [128, S])
            Q_sb = sb1("Q_sb", [128, S])
            Kt_sb = sb1("Kt_sb", [128, S])
            Vt_sb = sb1("Vt_sb", [128, S])

            with contextlib.ExitStack() as c1a:
                sb1a = lambda n, s, dt=F32: c1a.enter_context(nc.sbuf_tensor(n, s, dt))
                ps1 = lambda n, s: c1a.enter_context(nc.psum_tensor(n, s, F32))
                xt = [sb1a("xtA", [128, D]), sb1a("xtB", [128, D])]
                sq = sb1a("sq", [128, D])
                xn_tm = [sb1a("xn_tmA", [128, D]), sb1a("xn_tmB", [128, D])]
                xn_fm = [sb1a("xn_fmA", [128, 8 * 512])] * 2
                lcol = [sb1a("lcolA", [128, 16]), sb1a("lcolB", [128, 16])]
                t8s = sb1a("t8s", [128, 16])
                logit_sb = sb1a("logit_sb", [128, NE])
                oneh = sb1a("oneh", [128, NE])
                posf = sb1a("posf", [128, NE])
                ptr = [ps1("ptrA", [128, 128]), ps1("ptrB", [128, 128])]
                prt = ps1("prt", [128, 512])
                pq = ps1("pq", [128, 512])
                pk = ps1("pk", [128, 512])
                pkt = ps1("pkt", [128, 128])
                pvt = ps1("pvt", [128, 128])

                P = ph1
                for wT, dst in [(ap_in["WkT"], WkT_sb), (ap_in["WvT"], WvT_sb),
                                (ap_in["WqT"], WqT_sb)]:
                    P.do("sync", lambda e, wT=wT, dst=dst: e.dma_start(
                        out=dst[:].rearrange("p (c m) -> p c m", c=8),
                        in_=wT.rearrange("(c p) m -> p c m", p=128)), dma=True)
                P.do("sync", lambda e: e.dma_start(out=WoT_sb[:], in_=ap_in["WoT"]), dma=True)
                P.do("sync", lambda e: e.dma_start(
                    out=Wg_sb[:].rearrange("p (c m) -> p c m", c=8),
                    in_=ap_in["Wg"].rearrange("(c p) m -> p c m", p=128)), dma=True)
                for src, dst in [("bk", bk_r), ("bv", bv_r), ("bq", bq_r),
                                 ("bo", bo_r), ("bg", bg_r)]:
                    P.do("sync", lambda e, src=src, dst=dst: e.dma_start(
                        out=dst[:], in_=ap_in[src][None, :]), dma=True)
                for a in range(EPC):
                    P.do("sync", lambda e, a=a: e.dma_start(
                        out=b1T_sb[:, (H // 128) * a:(H // 128) * (a + 1)],
                        in_=ap_in["b1pT"][a]), dma=True)
                    P.do("sync", lambda e, a=a: e.dma_start(
                        out=b2_r[:, D * a:D * (a + 1)],
                        in_=ap_in["b2p"][a][None, :]), dma=True)
                for src, dst in [("maskT", maskT_sb), ("qdec", qdec_sb), ("kw_col", kw_sb),
                                 ("ident", id_sb), ("ltriuT", ltr_sb), ("iota16", iota_sb),
                                 ("s2d", s2d_sb), ("g18mat", g18_sb), ("dm16mat", dm16_sb),
                                 ("rowb16", rowb_sb), ("mgownmat", mgo_sb)]:
                    P.do("sync", lambda e, src=src, dst=dst: e.dma_start(
                        out=dst[:], in_=ap_in[src][:, :]), dma=True)
                P.do("vector", lambda e: e.memset(ones_r[:], 1.0))
                P.do("vector", lambda e: e.memset(ones_c[:], 1.0))
                P.do("vector", lambda e: e.memset(zer16[:], 0.0))
                P.do("vector", lambda e: e.memset(runb[:], 0.0))

                pe_ops = []; ac_ops = []; dv_ops = []; sy_ops = []
                pe_c = [0]; a_c = [0]; v_c = [0]
                tr_ev = {}; evac_ev = {}; fm_ev = {}
                pl_ev = {}; pp_ev = {}; pc_ev = {}
                pq_ev = {}; pk_ev = {}; pkt_ev = {}; pvt_ev = {}
                xn_ev = {}; logit_ev = {}; oneh_ev = {}; posf_ev = {}
                runb_ev = {}; isge_ev = {}; negmean_ev = {}; recip_ev = {}
                negmax_ev = {}; exp_ev = {}; actxt_ev = {}; dvext_ev = {}
                qevac_ev = {}; proj_done = {}

                def pe(fn, waits=(), ev=False):
                    inc = None
                    if ev:
                        pe_c[0] += 1
                        inc = (pe1, 1)
                    pe_ops.append((list(waits), fn, inc))
                    return pe_c[0]

                def ac(fn, waits=()):
                    w = list(waits)
                    if a_c[0] > 0:
                        w.append((a1, a_c[0]))
                    a_c[0] += 1
                    ac_ops.append((w, fn, (a1, 1)))
                    return a_c[0]

                def dv(fn, waits=()):
                    w = list(waits)
                    if v_c[0] > 0:
                        w.append((v1, v_c[0]))
                    v_c[0] += 1
                    dv_ops.append((w, fn, (v1, 1)))
                    return v_c[0]

                for sc in range(NSC):
                    a2 = sc % 2
                    XNF = xn_fm[a2]
                    for it in range(4):
                        ti = 4 * sc + it
                        a = ti % 2
                        XT, XN, LC = xt[a], xn_tm[a], lcol[a]
                        lw = []
                        if ti >= 2:
                            lw.append((xls[a], 16 * (ti // 2)))
                            lw.append((a1, actxt_ev[ti - 2]))
                            lw.append((v1, dvext_ev[ti - 2]))
                        sy_ops.append((lw, (lambda e, ti=ti, XT=XT: e.dma_start(
                            out=XT[:], in_=ap_in["x"][128 * ti:128 * (ti + 1), :])),
                            (xls[a], 16)))
                        aw = [(xls[a], 16 * (ti // 2 + 1))]
                        if ti >= 2:
                            aw.append((v1, isge_ev[ti - 2]))
                        actxt_ev[ti] = ac(lambda e, XT=XT, LC=LC: e.activation(
                            sq[:], XT[:], AF.Copy, accum_out=LC[:, 0:1]), aw)
                        ac(lambda e, XT=XT, LC=LC: e.activation(
                            sq[:], XT[:], AF.Square, accum_out=LC[:, 2:3]))
                        negmean_ev[ti] = dv(lambda e, LC=LC: e.tensor_scalar_mul(
                            LC[:, 1:2], LC[:, 0:1], -1.0 / D), [(a1, actxt_ev[ti])])
                        dv(lambda e, LC=LC: e.tensor_scalar_mul(
                            LC[:, 3:4], LC[:, 2:3], 1.0 / D), [(a1, actxt_ev[ti] + 1)])
                        msq = ac(lambda e, LC=LC: e.activation(
                            LC[:, 4:5], LC[:, 1:2], AF.Square), [(v1, negmean_ev[ti])])
                        dv(lambda e, LC=LC: e.tensor_sub(
                            LC[:, 5:6], LC[:, 3:4], LC[:, 4:5]), [(a1, msq)])
                        dv(lambda e, LC=LC: e.tensor_scalar_add(
                            LC[:, 5:6], LC[:, 5:6], 1e-5))
                        recip_ev[ti] = dv(lambda e, LC=LC: e.reciprocal(
                            LC[:, 6:7], LC[:, 5:6]))
                        srt = ac(lambda e, LC=LC: e.activation(
                            LC[:, 7:8], LC[:, 6:7], AF.Sqrt), [(v1, recip_ev[ti])])
                        xw = [(a1, srt)]
                        if ti >= 2:
                            xw.append((pe1, tr_ev[(ti - 2) * 8 + 7]))
                            xw.append((xss[a], 16 * (ti // 2)))
                        xn_ev[ti] = dv(lambda e, XT=XT, XN=XN, LC=LC: e.tensor_scalar(
                            XN[:], XT[:], LC[:, 1:2], LC[:, 7:8], ALU.add, ALU.mult), xw)
                        dvext_ev[ti] = xn_ev[ti]
                        sw = [(v1, xn_ev[ti])]
                        if ti >= 2:
                            sw.append((xss[a], 16 * (ti // 2)))
                        sy_ops.append((sw, (lambda e, ti=ti, XN=XN: e.dma_start(
                            out=xn_d[128 * ti:128 * (ti + 1), :], in_=XN[:])),
                            (xss[a], 16)))
                        for dc in range(8):
                            n = ti * 8 + dc
                            q = n % 2
                            tw = []
                            if dc == 0:
                                tw.append((v1, xn_ev[ti]))
                            if n >= 2:
                                tw.append((a1, evac_ev[n - 2]))
                            tr_ev[n] = pe(lambda e, dc=dc, XN=XN, q=q: e.transpose(
                                ptr[q][:], XN[:, 128 * dc:128 * (dc + 1)], id_sb[:]),
                                tw, ev=True)
                            ew = [(pe1, tr_ev[n])]
                            if sc >= 1 and dc == 0 and it == 0:
                                ew.append((pe1, proj_done[sc - 1]))
                            evac_ev[n] = ac(lambda e, dc=dc, it=it, q=q, XNF=XNF: e.copy(
                                XNF[:, 512 * dc + 128 * it:512 * dc + 128 * (it + 1)],
                                ptr[q][:]), ew)
                        fm_ev[ti] = a_c[0]
                        rw = [(a1, fm_ev[ti])]
                        if ti >= 1:
                            rw.append((v1, runb_ev[ti - 1]))
                        for dc in range(8):
                            pe(lambda e, dc=dc, it=it, XNF=XNF: e.matmul(
                                prt[:, 0:NE],
                                XNF[:, 512 * dc + 128 * it:512 * dc + 128 * (it + 1)],
                                Wg_sb[:, NE * dc:NE * (dc + 1)],
                                start=(dc == 0), stop=False), rw if dc == 0 else [])
                        pl_ev[ti] = pe(lambda e: e.matmul(
                            prt[:, 0:NE], ones_r[:, 0:128], bg_r[:],
                            start=False, stop=True), [], ev=True)
                        logit_ev[ti] = dv(lambda e: e.tensor_copy(
                            logit_sb[:], prt[:, 0:NE]), [(pe1, pl_ev[ti])])
                        dv(lambda e: e.tensor_max(
                            t8s[:, 0:8], logit_sb[:, 0:8], logit_sb[:, 8:16]))
                        dv(lambda e: e.tensor_max(
                            t8s[:, 8:12], t8s[:, 0:4], t8s[:, 4:8]))
                        dv(lambda e: e.tensor_max(
                            t8s[:, 12:14], t8s[:, 8:10], t8s[:, 10:12]))
                        dv(lambda e, LC=LC: e.tensor_max(
                            LC[:, 12:13], t8s[:, 12:13], t8s[:, 13:14]))
                        negmax_ev[ti] = dv(lambda e, LC=LC: e.tensor_scalar_mul(
                            LC[:, 8:9], LC[:, 12:13], -1.0))
                        exp_ev[ti] = ac(lambda e, LC=LC: e.activation(
                            sq[:, 0:NE], logit_sb[:], AF.Exp,
                            bias=LC[:, 8:9], accum_out=LC[:, 9:10]),
                            [(v1, negmax_ev[ti])])
                        dv(lambda e, ti=ti, LC=LC: e.reciprocal(
                            gate_st[:, ti:ti + 1], LC[:, 9:10]), [(a1, exp_ev[ti])])
                        isge_ev[ti] = dv(lambda e, LC=LC: e.tensor_scalar(
                            oneh[:], logit_sb[:], LC[:, 12:13], None, ALU.is_ge))
                        dv(lambda e: e.tensor_mul(oneh[:], oneh[:], iota_sb[:]))
                        dv(lambda e: e.tensor_add(
                            oneh[:, 0:8], oneh[:, 0:8], oneh[:, 8:16]))
                        dv(lambda e: e.tensor_add(
                            oneh[:, 0:4], oneh[:, 0:4], oneh[:, 4:8]))
                        dv(lambda e: e.tensor_add(
                            oneh[:, 0:2], oneh[:, 0:2], oneh[:, 2:4]))
                        dv(lambda e, ti=ti: e.tensor_add(
                            idx_st[:, ti:ti + 1], oneh[:, 0:1], oneh[:, 1:2]))
                        oneh_ev[ti] = dv(lambda e, ti=ti: e.tensor_scalar(
                            oneh[:], iota_sb[:], idx_st[:, ti:ti + 1], None, ALU.is_equal))
                        pw = [(v1, oneh_ev[ti]), (v1, logit_ev[ti])]
                        pe(lambda e: e.matmul(prt[:, 128:128 + NE], ltr_sb[:], oneh[:],
                                              start=True, stop=False), pw)
                        pp_ev[ti] = pe(lambda e: e.matmul(
                            prt[:, 128:128 + NE], ones_r[:, 0:128], runb[:],
                            start=False, stop=True), [], ev=True)
                        posf_ev[ti] = dv(lambda e: e.tensor_mul(
                            posf[:], oneh[:], prt[:, 128:128 + NE]), [(pe1, pp_ev[ti])])
                        dv(lambda e: e.tensor_add(
                            posf[:, 0:8], posf[:, 0:8], posf[:, 8:16]))
                        dv(lambda e: e.tensor_add(
                            posf[:, 0:4], posf[:, 0:4], posf[:, 4:8]))
                        dv(lambda e: e.tensor_add(
                            posf[:, 0:2], posf[:, 0:2], posf[:, 2:4]))
                        dv(lambda e, ti=ti: e.tensor_add(
                            rank_st[:, ti:ti + 1], posf[:, 0:1], posf[:, 1:2]))
                        pc_ev[ti] = pe(lambda e: e.matmul(
                            prt[0:1, 256:256 + NE], ones_c[:], oneh[:],
                            start=True, stop=True), [(v1, posf_ev[ti])], ev=True)
                        runb_ev[ti] = dv(lambda e: e.tensor_add(
                            runb[:], runb[:], prt[0:1, 256:256 + NE]),
                            [(pe1, pc_ev[ti])])
                    t0 = 512 * sc
                    qw = [(a1, fm_ev[4 * sc + 3])]
                    if sc >= 1:
                        qw.append((v1, qevac_ev[sc - 1]))
                    for dst_ps, wT in [(pq, WqT_sb), (pk, WkT_sb)]:
                        for dc in range(8):
                            pe(lambda e, dc=dc, dst_ps=dst_ps, wT=wT, XNF=XNF: e.matmul(
                                dst_ps[:], wT[:, M * dc:M * (dc + 1)],
                                XNF[:, 512 * dc:512 * (dc + 1)],
                                start=(dc == 0), stop=False),
                                qw if (dc == 0 and dst_ps is pq) else [])
                        b_r = bq_r if dst_ps is pq else bk_r
                        ev = pe(lambda e, dst_ps=dst_ps, b_r=b_r: e.matmul(
                            dst_ps[:], b_r[:], ones_r[:, 0:512], start=False, stop=True),
                            [], ev=True)
                        if dst_ps is pq:
                            pq_ev[sc] = ev
                        else:
                            pk_ev[sc] = ev
                    dv(lambda e, t0=t0: e.tensor_copy(Q_sb[:, t0:t0 + 512], pq[:]),
                       [(pe1, pq_ev[sc])])
                    qevac_ev[sc] = dv(lambda e, t0=t0: e.tensor_copy(
                        K_sb[:, t0:t0 + 512], pk[:]), [(pe1, pk_ev[sc])])
                    for it in range(4):
                        tt = 512 * sc + 128 * it
                        for dst_ps, wT in [(pkt, WkT_sb), (pvt, WvT_sb)]:
                            kw2 = []
                            if sc >= 1 or it >= 1:
                                pidx = 4 * sc + it - 1
                                kw2 = [(v1, pkt_ev.get(("e", pidx), 0))] if ("e", pidx) in pkt_ev else []
                            for dc in range(8):
                                pe(lambda e, dc=dc, it=it, dst_ps=dst_ps, wT=wT, XNF=XNF: e.matmul(
                                    dst_ps[:],
                                    XNF[:, 512 * dc + 128 * it:512 * dc + 128 * (it + 1)],
                                    wT[:, M * dc:M * (dc + 1)],
                                    start=(dc == 0), stop=False),
                                    kw2 if dc == 0 else [])
                            b_r = bk_r if dst_ps is pkt else bv_r
                            ev = pe(lambda e, dst_ps=dst_ps, b_r=b_r: e.matmul(
                                dst_ps[:], ones_r[:, 0:128], b_r[:],
                                start=False, stop=True), [], ev=True)
                            if dst_ps is pkt:
                                pkt_ev[4 * sc + it] = ev
                            else:
                                pvt_ev[4 * sc + it] = ev
                        dv(lambda e, tt=tt: e.tensor_scalar(
                            Kt_sb[:, tt:tt + 128], pkt[:], kw_sb[:], None, ALU.mult),
                           [(pe1, pkt_ev[4 * sc + it])])
                        pkt_ev[("e", 4 * sc + it)] = v_c[0] + 1
                        dv(lambda e, tt=tt: e.tensor_copy(Vt_sb[:, tt:tt + 128], pvt[:]),
                           [(pe1, pvt_ev[4 * sc + it])])
                    proj_done[sc] = pe_c[0]

                with nc.Block("seg1") as block:
                    ph1.emit_block(nc, block)
                    start1 = (s_p1, ph1.cnt)

                    def cb1(ops):
                        def body(eng):
                            eng.wait_ge(start1[0], start1[1])
                            for waits, fn, inc in ops:
                                for sm, vv in waits:
                                    eng.wait_ge(sm, vv)
                                inst = fn(eng)
                                if inc is not None:
                                    inst.then_inc(inc[0], inc[1])
                        return body

                    block.tensor(cb1(pe_ops))
                    block.scalar(cb1(ac_ops))
                    block.vector(cb1(dv_ops))
                    sy2 = []
                    li = 0
                    for k2 in range(0, len(sy_ops), 2):
                        sy2.append(sy_ops[k2])
                        sy2.append(sy_ops[k2 + 1])
                    block.sync(cb1(sy2))

                    def join(eng):
                        eng.wait_ge(pe1, pe_c[0])
                        eng.wait_ge(a1, a_c[0])
                        eng.wait_ge(v1, v_c[0])
                        eng.wait_ge(xss[0], 16 * (NT - NT // 2))
                        eng.wait_ge(xss[1], 16 * (NT // 2))
                        eng.nop().then_inc(s_p1, 1)
                    block.gpsimd(join)
                ph1.cnt += 1


            with contextlib.ExitStack() as c2:
                sb2 = lambda n, s, dt=F32: c2.enter_context(nc.sbuf_tensor(n, s, dt))
                ps2 = lambda n, s: c2.enter_context(nc.psum_tensor(n, s, F32))
                T_sb = sb2("T_sb", [128, 128])
                Qd = sb2("Qd", [128, C])
                Am = sb2("Am", [C, C])
                oneh2 = sb2("oneh2", [128, NE])
                posf2 = sb2("posf2", [128, NE])
                dest_f = sb2("dest_f", [128, 1])
                dest_i = sb2("dest_i", [128, 1], I32)
                lobc = sb2("lobc", [128, NE])
                xrow = sb2("xrow", [128, D])
                arow = sb2("arow", [128, D])
                pA = ps2("pA", [C, C])
                pO = ps2("pO", [128, C])
                pT = ps2("pT", [128, 128])
                pao = ps2("pao", [128, 512])
                plo = ps2("plo", [128, NE])

                P = ph2
                P.do("vector", lambda e: e.tensor_tensor_scan(
                    loff_own[:], runb[:], zer16[:], 0.0, ALU.add, ALU.add))
                P.do("vector", lambda e: e.tensor_sub(loff_own[:], loff_own[:], runb[:]))
                P.do("tensor", lambda e: e.matmul(plo[:], ones_r[:, 0:128], loff_own[:],
                                                  start=True, stop=True))
                P.do("vector", lambda e: e.tensor_copy(lobc[:], plo[:]))
                P.do("sync", lambda e: e.dma_start(
                    out=ag1c_in.ap()[:, :], in_=runb[:]), dma=True)
                for ti in range(NT):
                    P.do("sync", lambda e, ti=ti: e.dma_start(
                        out=xrow[:], in_=xn_d[128 * ti:128 * (ti + 1), :]), dma=True)
                    P.do("vector", lambda e, ti=ti: e.tensor_scalar(
                        oneh2[:], iota_sb[:], idx_st[:, ti:ti + 1], None, ALU.is_equal))
                    P.do("vector", lambda e: e.tensor_mul(posf2[:], oneh2[:], lobc[:]))
                    P.do("vector", lambda e: e.reduce_sum(dest_f[:], posf2[:], axis=AX.X))
                    P.do("vector", lambda e, ti=ti: e.tensor_add(
                        dest_f[:], dest_f[:], rank_st[:, ti:ti + 1]))
                    P.do("vector", lambda e: e.tensor_copy(dest_i[:], dest_f[:]))
                    P.do("gpsimd", lambda e: e.indirect_dma_start(
                        ag1_in.ap()[:, :], IndirectOffsetOnAxis(dest_i[:], 0),
                        xrow[:], None), dma=True)
                P.do("vector", lambda e: e.memset(T_sb[:], 0.0))
                for ch in range(NCH):
                    t0 = C * ch
                    P.do("vector", lambda e, t0=t0: e.tensor_mul(
                        Qd[:], Q_sb[:, t0:t0 + C], qdec_sb[:]))
                    P.do("tensor", lambda e, t0=t0: e.matmul(
                        pA[:], K_sb[:, t0:t0 + C], Q_sb[:, t0:t0 + C], start=True, stop=True))
                    P.do("vector", lambda e: e.tensor_mul(Am[:], pA[:], maskT_sb[:]))
                    P.do("tensor", lambda e: e.matmul(pO[:], T_sb[:], Qd[:],
                                                      start=True, stop=False))
                    P.do("tensor", lambda e, t0=t0: e.matmul(
                        pO[:], Vt_sb[:, t0:t0 + C], Am[:], start=False, stop=True))
                    P.do("tensor", lambda e, t0=t0: e.matmul(
                        pT[:], Kt_sb[:, t0:t0 + C], Vt_sb[:, t0:t0 + C], start=True, stop=True))
                    P.do("vector", lambda e: e.scalar_tensor_tensor(
                        T_sb[:], T_sb[:], DECAY ** C, pT[:], ALU.mult, ALU.add))
                    P.do("vector", lambda e, t0=t0: e.tensor_copy(K_sb[:, t0:t0 + C], pO[:]))
                for ti in range(NT):
                    t0 = 128 * ti
                    for dh in range(2):
                        P.do("tensor", lambda e, t0=t0, dh=dh: e.matmul(
                            pao[:], K_sb[:, t0:t0 + 128], WoT_sb[:, 512 * dh:512 * (dh + 1)],
                            start=True, stop=False))
                        P.do("tensor", lambda e, dh=dh: e.matmul(
                            pao[:], ones_r[:, 0:128], bo_r[:, 512 * dh:512 * (dh + 1)],
                            start=False, stop=True))
                        P.do("vector", lambda e, dh=dh: e.tensor_copy(
                            arow[:, 512 * dh:512 * (dh + 1)], pao[:]))
                    P.do("sync", lambda e, t0=t0: e.dma_start(
                        out=attn_d[t0:t0 + 128, :], in_=arow[:]), dma=True)

                P.do("gpsimd", lambda e: e.collective_compute(
                    "AllGather", ALU.bypass, replica_groups=[list(range(W))],
                    ins=[ag1c_in.ap().opt()], outs=[ag1c_out.ap().opt()]))
                P.do("gpsimd", lambda e: e.collective_compute(
                    "AllGather", ALU.bypass, replica_groups=[list(range(W))],
                    ins=[ag1_in.ap().opt()], outs=[ag1_out.ap().opt()]))
                with nc.Block("seg2") as block:
                    ph2.emit_block(nc, block)

        with contextlib.ExitStack() as c3:
            sb3 = lambda n, s, dt=F32: c3.enter_context(nc.sbuf_tensor(n, s, dt))
            ps3 = lambda n, s: c3.enter_context(nc.psum_tensor(n, s, F32))
            xg_tm = [sb3("xg_tmA", [128, D]), sb3("xg_tmB", [128, D])]
            xg_fm = sb3("xg_fm", [128, 8 * 512])
            h_sb = sb3("h_sb", [128, 32 * 512])
            w1_sb = [sb3("w1_sbA", [128, 8 * 128]), sb3("w1_sbB", [128, 8 * 128])]
            w2_sb = [sb3("w2_sbA", [128, 512]), sb3("w2_sbB", [128, 512])]
            y_sb = [sb3("y_sbA", [128, 512]), sb3("y_sbB", [128, 512])]
            macc = sb3("macc", [128, capT])
            m1 = sb3("m1", [128, capT])
            m2 = sb3("m2", [128, capT])
            mt = sb3("mt", [128, capT])
            ptr3 = ps3("ptr3", [128, 128])
            ptr3b = ps3("ptr3b", [128, 128])
            ph_ps = [ps3("ph_psA", [128, 512]), ps3("ph_psB", [128, 512])]
            py = [ps3(f"py{i}", [128, 512]) for i in range(4)]

            P = ph3
            P.do("sync", lambda e: e.dma_start(
                out=counts_c[:, 0:1],
                in_=ag1c_out.ap().rearrange("(x b) e -> x (b e)", x=1).transpose([1, 0])),
                dma=True)
            P.do("tensor", lambda e: e.matmul(py[0][0:1, 0:(W + 1) * EPC], counts_c[:], g18_sb[:],
                                              start=True, stop=True))
            P.do("vector", lambda e: e.tensor_copy(g18row[:], py[0][0:1, 0:(W + 1) * EPC]))
            P.do("tensor", lambda e: e.matmul(py[1][:, 0:(W + 1) * EPC], ones_r[:, 0:128], g18row[:],
                                              start=True, stop=True))
            P.do("vector", lambda e: e.tensor_copy(g18bc[:], py[1][:, 0:(W + 1) * EPC]))
            P.do("tensor", lambda e: e.matmul(py[0][0:1, 0:W * EPC], counts_c[:], dm16_sb[:],
                                              start=True, stop=False))
            P.do("tensor", lambda e: e.matmul(py[0][0:1, 0:W * EPC], ones_c[0:1, 0:1], rowb_sb[:],
                                              start=False, stop=True))
            P.do("vector", lambda e: e.tensor_copy(d16row[:], py[0][0:1, 0:W * EPC]))
            P.do("tensor", lambda e: e.matmul(py[1][:, 0:W * EPC], ones_r[:, 0:128], d16row[:],
                                              start=True, stop=True))
            P.do("vector", lambda e: e.tensor_copy(d16bc[:], py[1][:, 0:W * EPC]))
            P.do("tensor", lambda e: e.matmul(py[0][0:1, 0:NE], counts_c[:], mgo_sb[:],
                                              start=True, stop=True))
            P.do("vector", lambda e: e.tensor_copy(d16row[:, 0:NE], py[0][0:1, 0:NE]))
            P.do("tensor", lambda e: e.matmul(py[1][:, 0:NE], ones_r[:, 0:128],
                                              d16row[:, 0:NE], start=True, stop=True))
            P.do("vector", lambda e: e.tensor_copy(gown_bc[:], py[1][:, 0:NE]))
            for j in range(EPC):
                P.do("vector", lambda e: e.memset(macc[:], 0.0))
                for b in range(W):
                    P.do("vector", lambda e, b=b, j=j: e.tensor_scalar(
                        m1[:], s2d_sb[:], g18bc[:, EPC * b + j:EPC * b + j + 1],
                        None, ALU.is_ge))
                    P.do("vector", lambda e, b=b, j=j: e.tensor_scalar(
                        m2[:], s2d_sb[:], g18bc[:, EPC * (b + 1) + j:EPC * (b + 1) + j + 1],
                        None, ALU.is_lt))
                    P.do("vector", lambda e: e.tensor_mul(m1[:], m1[:], m2[:]))
                    P.do("vector", lambda e, b=b, j=j: e.tensor_scalar(
                        mt[:], s2d_sb[:], d16bc[:, EPC * b + j:EPC * b + j + 1],
                        None, ALU.add))
                    P.do("vector", lambda e: e.tensor_mul(mt[:], mt[:], m1[:]))
                    P.do("vector", lambda e: e.tensor_add(macc[:], macc[:], mt[:]))
                P.do("vector", lambda e, j=j: e.tensor_copy(
                    src_i[:, capT * j:capT * (j + 1)], macc[:]))
            NG = NST * 4
            NH1 = NST * 32
            NW2 = NST * 64
            NY = NST * 8
            tr_ev = {}
            w1grp_ev = {}

            w2grp_ev = {}
            bias_ev = {}
            evac_ev = {}
            relu_ev = {}
            xg_done = {}
            yev = {}
            py_last = [0, 0, 0, 0]

            pe_ops = []
            sc_ops = []
            ve_ops = []
            sy_ops = []
            gp_ops = []
            mm_c = [0]; rl_c = [0]; ev_c = [0]

            for st in range(NST):
                j = st // STPE
                stj = st % STPE
                for k in range(4):
                    g = st * 4 + k
                    p = g % 2
                    col = capT * j + 4 * stj + k
                    waits = []
                    if g >= 2:
                        waits.append((gsem[p], 16 * (g // 2)))
                        waits.append((mm3, tr_ev[(g - 2) * 8 + 7]))
                    gp_ops.append((waits, (lambda e, col=col, p=p: e.indirect_dma_start(
                        xg_tm[p][:], None, ag1_out.ap()[:, :],
                        IndirectOffsetOnAxis(src_i[:, col:col + 1], 0))),
                        (gsem[p], 16)))
                    for dc in range(8):
                        n = g * 8 + dc
                        q = n % 2
                        PT = ptr3 if q == 0 else ptr3b
                        waits = []
                        if dc == 0:
                            waits.append((gsem[p], 16 * (g // 2 + 1)))
                        if n >= 2:
                            waits.append((rl3, evac_ev[n - 2]))
                        mm_c[0] += 1
                        tr_ev[n] = mm_c[0]
                        pe_ops.append((waits, (lambda e, dc=dc, p=p, PT=PT: e.transpose(
                            PT[:], xg_tm[p][:, 128 * dc:128 * (dc + 1)], id_sb[:])),
                            (mm3, 1)))
                        rl_c[0] += 1
                        evac_ev[n] = rl_c[0]
                        sc_ops.append(([(mm3, tr_ev[n])],
                                       (lambda e, dc=dc, k=k, PT=PT: e.copy(
                                           xg_fm[:, 512 * dc + 128 * k:512 * dc + 128 * (k + 1)],
                                           PT[:])), (rl3, 1)))
                xg_done[st] = rl_c[0]
                for hc in range(32):
                    l = st * 32 + hc
                    p = l % 2
                    waits = [(w1s[p], 16 * (l // 2 + 1))]
                    if hc == 0:
                        waits.append((rl3, xg_done[st]))
                    if l >= 2:
                        waits.append((rl3, relu_ev[l - 2]))
                    for dc in range(8):
                        mmw = waits if dc == 0 else []
                        inc = None
                        if dc == 7:
                            mm_c[0] += 1
                            w1grp_ev[l] = mm_c[0]
                            inc = (mm3, 1)
                        pe_ops.append((mmw, (lambda e, dc=dc, p=p: e.matmul(
                            ph_ps[p][:], w1_sb[p][:, 128 * dc:128 * (dc + 1)],
                            xg_fm[:, 512 * dc:512 * (dc + 1)],
                            start=(dc == 0), stop=(dc == 7))), inc))
                    lw = [(w1s[p], 16 * (l // 2))] if l >= 2 else []
                    if l >= 2:
                        lw.append((mm3, w1grp_ev[l - 2]))
                    sy_ops.append((lw, (lambda e, j=j, hc=hc, p=p: e.dma_start(
                        out=w1_sb[p][:].rearrange("p (c h) -> p c h", c=8),
                        in_=ap_in["W1p"][j, :, 128 * hc:128 * (hc + 1)]
                        .rearrange("(c p) h -> p c h", p=128))), (w1s[p], 16)))
                    rl_c[0] += 1
                    relu_ev[l] = rl_c[0]
                    sc_ops.append(([(mm3, w1grp_ev[l])],
                                   (lambda e, j=j, hc=hc, p=p: e.activation(
                                       h_sb[:, 512 * hc:512 * (hc + 1)], ph_ps[p][:],
                                       AF.Relu,
                                       bias=b1T_sb[:, 32 * j + hc:32 * j + hc + 1])),
                                   (rl3, 1)))
                for dh in range(2):
                    for hc in range(32):
                        mw2 = (st * 2 + dh) * 32 + hc
                        q = mw2 % 2
                        waits = [(w2s[q], 16 * (mw2 // 2 + 1))]
                        if dh == 0:
                            waits.append((rl3, relu_ev[st * 32 + hc]))
                        if hc == 0:
                            pl_ = max(py_last)
                            if pl_ > 0:
                                waits.append((ev3, pl_))
                        for tt in range(4):
                            mmw = waits if tt == 0 else []
                            inc = None
                            if tt == 3:
                                mm_c[0] += 1
                                w2grp_ev[mw2] = mm_c[0]
                                inc = (mm3, 1)
                            pe_ops.append((mmw, (lambda e, hc=hc, tt=tt, q=q: e.matmul(
                                py[tt][:],
                                h_sb[:, 512 * hc + 128 * tt:512 * hc + 128 * (tt + 1)],
                                w2_sb[q][:], start=(hc == 0), stop=False)), inc))
                        lw = [(w2s[q], 16 * (mw2 // 2))] if mw2 >= 2 else []
                        if mw2 >= 2:
                            lw.append((mm3, w2grp_ev[mw2 - 2]))
                        sy_ops.append((lw, (lambda e, j=j, hc=hc, dh=dh, q=q: e.dma_start(
                            out=w2_sb[q][:],
                            in_=ap_in["W2p"][j, 128 * hc:128 * (hc + 1),
                                             512 * dh:512 * (dh + 1)])), (w2s[q], 16)))
                    for tt in range(4):
                        m = (st * 2 + dh) * 4 + tt
                        q = m % 2
                        mm_c[0] += 1
                        bias_ev[m] = mm_c[0]
                        pe_ops.append(([], (lambda e, tt=tt, j=j, dh=dh: e.matmul(
                            py[tt][:], ones_r[:, 0:128],
                            b2_r[:, D * j + 512 * dh:D * j + 512 * (dh + 1)],
                            start=False, stop=True)), (mm3, 1)))
                        ew = [(mm3, bias_ev[m])]
                        if m >= 2:
                            ew.append((osem[q], 16 * (m // 2)))
                        ev_c[0] += 1
                        yev[m] = ev_c[0]
                        py_last[tt] = ev_c[0]
                        ve_ops.append((ew, (lambda e, tt=tt, q=q: e.tensor_copy(
                            y_sb[q][:], py[tt][:])), (ev3, 1)))
                        row0 = cap * j + 512 * stj + 128 * tt
                        sw = [(ev3, yev[m])]
                        if m >= 2:
                            sw.append((osem[q], 16 * (m // 2)))
                        gp_ops.append((sw, (lambda e, row0=row0, dh=dh, q=q: e.dma_start(
                            out=ag2_in.ap()[row0:row0 + 128, 512 * dh:512 * (dh + 1)],
                            in_=y_sb[q][:])), (osem[q], 16)))
            no2 = [NY // 2, NY - NY // 2]
            gp_ops.append(([(osem[0], 16 * no2[0]), (osem[1], 16 * no2[1])],
                           (lambda e: e.collective_compute(
                               "AllGather", ALU.bypass,
                               replica_groups=[list(range(W))],
                               ins=[ag2_in.ap().opt()], outs=[ag2_out.ap().opt()])),
                           (agd, 1)))

            with nc.Block("seg3") as block:
                ph3.emit_block(nc, block)
                start_w = (s_p3, ph3.cnt)

                def chain_body(ops, start=start_w):
                    def body(eng):
                        eng.wait_ge(start[0], start[1])
                        for waits, fn, inc in ops:
                            for sm, v in waits:
                                eng.wait_ge(sm, v)
                            inst = fn(eng)
                            if inc is not None:
                                inst.then_inc(inc[0], inc[1])
                    return body

                block.tensor(chain_body(pe_ops))
                block.scalar(chain_body(sc_ops))
                block.vector(chain_body(ve_ops))
                block.sync(chain_body(sy_ops))
                block.gpsimd(chain_body(gp_ops))


        with contextlib.ExitStack() as c5:
            sb5 = lambda n, s, dt=F32: c5.enter_context(nc.sbuf_tensor(n, s, dt))
            xt5 = [sb5(f"xt5{i}", [128, D]) for i in range(2)]
            at5 = [sb5(f"at5{i}", [128, D]) for i in range(2)]
            yg5 = [sb5(f"yg5{i}", [128, D]) for i in range(2)]
            oneh5 = [sb5(f"oneh5{i}", [128, NE]) for i in range(2)]
            posf5 = [sb5(f"posf5{i}", [128, NE]) for i in range(2)]
            col5 = [sb5(f"col5{i}", [128, 8]) for i in range(2)]
            yri5 = [sb5(f"yri5{i}", [128, 1], I32) for i in range(2)]
            out5 = [sb5(f"out5{i}", [128, D]) for i in range(2)]

            v5_ops = []
            g5_ops = []
            s5_ld = []
            s5_st = []
            v5c = [0]
            yri_ev = {}
            out_ev = {}

            for ti in range(NT):
                t0 = 128 * ti
                a = ti % 2
                OH, PF, CL = oneh5[a], posf5[a], col5[a]
                lw = [(l5sem[a], 32 * (ti // 2))] if ti >= 2 else []
                if ti >= 2:
                    lw.append((v5, out_ev[ti - 2]))
                s5_ld.append((lw, (lambda e, t0=t0, a=a: e.dma_start(
                    out=xt5[a][:], in_=ap_in["x"][t0:t0 + 128, :])), (l5sem[a], 16)))
                s5_ld.append(([], (lambda e, t0=t0, a=a: e.dma_start(
                    out=at5[a][:], in_=attn_d[t0:t0 + 128, :])), (l5sem[a], 16)))

                def v(fn, waits=(), inc=True):
                    w = list(waits)
                    if v5c[0] > 0:
                        w.append((v5, v5c[0]))
                    v5c[0] += 1
                    v5_ops.append((w, fn, (v5, 1)))

                v(lambda e, ti=ti, OH=OH: e.tensor_scalar(
                    OH[:], iota_sb[:], idx_st[:, ti:ti + 1], None, ALU.is_equal))
                v(lambda e, OH=OH, PF=PF: e.tensor_mul(PF[:], OH[:], gown_bc[:]))
                v(lambda e, PF=PF: e.tensor_add(PF[:, 0:8], PF[:, 0:8], PF[:, 8:16]))
                v(lambda e, PF=PF: e.tensor_add(PF[:, 0:4], PF[:, 0:4], PF[:, 4:8]))
                v(lambda e, PF=PF: e.tensor_add(PF[:, 0:2], PF[:, 0:2], PF[:, 2:4]))
                v(lambda e, PF=PF, CL=CL: e.tensor_add(CL[:, 0:1], PF[:, 0:1], PF[:, 1:2]))
                v(lambda e, ti=ti, CL=CL: e.tensor_add(
                    CL[:, 0:1], CL[:, 0:1], rank_st[:, ti:ti + 1]))
                v(lambda e, CL=CL: e.tensor_single_scalar(
                    CL[:, 1:2], CL[:, 0:1], float(cap), ALU.is_lt))
                v(lambda e, ti=ti, CL=CL: e.tensor_mul(
                    CL[:, 2:3], CL[:, 1:2], gate_st[:, ti:ti + 1]))
                v(lambda e, ti=ti, CL=CL: e.tensor_scalar(
                    CL[:, 3:4], idx_st[:, ti:ti + 1], float(cap), None, ALU.mult))
                v(lambda e, CL=CL: e.tensor_add(CL[:, 3:4], CL[:, 3:4], CL[:, 0:1]))
                v(lambda e, CL=CL: e.tensor_single_scalar(
                    CL[:, 3:4], CL[:, 3:4], float(NE * cap - 1), ALU.min))
                yw = [(g5sem[a], 16 * (ti // 2))] if ti >= 2 else []
                v(lambda e, a=a, CL=CL: e.tensor_copy(yri5[a][:], CL[:, 3:4]), waits=yw)
                yri_ev[ti] = v5c[0]
                gw = [(v5, yri_ev[ti])]
                if ti >= 2:
                    gw.append((g5sem[a], 16 * (ti // 2)))
                g5_ops.append((gw, (lambda e, a=a: e.indirect_dma_start(
                    yg5[a][:], None, ag2_out.ap()[:, :],
                    IndirectOffsetOnAxis(yri5[a][:], 0))), (g5sem[a], 16)))
                cw = [(l5sem[a], 32 * (ti // 2 + 1)), (g5sem[a], 16 * (ti // 2 + 1))]
                if ti >= 2:
                    cw.append((o5sem[a], 16 * (ti // 2)))
                v(lambda e, a=a, CL=CL: e.scalar_tensor_tensor(
                    out5[a][:], yg5[a][:], CL[:, 2:3], at5[a][:], ALU.mult, ALU.add),
                  waits=cw)
                v(lambda e, a=a: e.tensor_add(out5[a][:], out5[a][:], xt5[a][:]))
                out_ev[ti] = v5c[0]
                sw = [(v5, out_ev[ti])]
                if ti >= 2:
                    sw.append((o5sem[a], 16 * (ti // 2)))
                s5_st.append((sw, (lambda e, t0=t0, a=a: e.dma_start(
                    out=out_d[t0:t0 + 128, :], in_=out5[a][:])), (o5sem[a], 16)))

            with nc.Block("seg4") as block:
                pw = [(ph4d.sem, ph4d.cnt), (s_p3, ph3.cnt)]
                if ph3.gd_cnt:
                    pw.append((s_p3g, ph3.gd_cnt))

                def cbody(ops, extra_first=pw):
                    def body(eng):
                        for sm, vv in extra_first:
                            eng.wait_ge(sm, vv)
                        for waits, fn, inc in ops:
                            for sm, vv in waits:
                                eng.wait_ge(sm, vv)
                            fn(eng).then_inc(inc[0], inc[1])
                    return body

                block.vector(cbody(v5_ops))
                block.gpsimd(cbody(g5_ops))
                sy5 = []
                for ti in range(NT):
                    sy5.extend(s5_ld[2 * ti:2 * ti + 2])
                    if ti >= 1:
                        sy5.append(s5_st[ti - 1])
                sy5.append(s5_st[NT - 1])
                block.sync(cbody(sy5))

                def tail(eng):
                    eng.wait_ge(o5sem[0], 16 * (NT - NT // 2))
                    eng.wait_ge(o5sem[1], 16 * (NT // 2))
                block.scalar(tail)

    return nc


def _in_maps(S, inputs):
    x = np.asarray(inputs["L0_attn_x"], dtype=np.float32)
    Wk = np.asarray(inputs["L0_attn_Wk"], dtype=np.float32)
    Wv = np.asarray(inputs["L0_attn_Wv"], dtype=np.float32)
    Wq = np.asarray(inputs["L0_attn_Wq"], dtype=np.float32)
    Wo = np.asarray(inputs["L0_attn_Wo"], dtype=np.float32)
    W1 = np.asarray(inputs["L0_ffn_W1"], dtype=np.float32)
    W2 = np.asarray(inputs["L0_ffn_W2"], dtype=np.float32)
    b1 = np.asarray(inputs["L0_ffn_b1"], dtype=np.float32)
    base = {
        "WkT": np.ascontiguousarray(Wk.T), "WvT": np.ascontiguousarray(Wv.T),
        "WqT": np.ascontiguousarray(Wq.T), "WoT": np.ascontiguousarray(Wo.T),
        "bk": np.asarray(inputs["L0_attn_bk"], np.float32),
        "bv": np.asarray(inputs["L0_attn_bv"], np.float32),
        "bq": np.asarray(inputs["L0_attn_bq"], np.float32),
        "bo": np.asarray(inputs["L0_attn_bo"], np.float32),
        "Wg": np.asarray(inputs["L0_ffn_Wg"], np.float32),
        "bg": np.asarray(inputs["L0_ffn_bg"], np.float32),
    }
    maps = []
    for c in range(W):
        m = dict(base)
        m["x"] = np.ascontiguousarray(x[c])
        e0 = EPC * c
        m["W1p"] = np.ascontiguousarray(W1[e0:e0 + EPC])
        m["W2p"] = np.ascontiguousarray(W2[e0:e0 + EPC])
        m["b1pT"] = np.ascontiguousarray(
            b1[e0:e0 + EPC].reshape(EPC, H // 128, 128).transpose(0, 2, 1))
        m["b2p"] = np.ascontiguousarray(
            np.asarray(inputs["L0_ffn_b2"], np.float32)[e0:e0 + EPC])
        m.update(_host_consts(S, c))
        maps.append(m)
    return maps


def _run(S, inputs):
    nc = _build(S)
    maps = _in_maps(S, inputs)
    res = run_bass_kernel_spmd(nc, maps, core_ids=list(range(W)))
    return np.stack([res.results[c]["out"] for c in range(W)], axis=0)


def kernel(**inputs):
    S = np.asarray(inputs["L0_attn_x"]).shape[1]
    return _run(S, inputs)
